# version 19
# baseline (speedup 1.0000x reference)
import numpy as np
from contextlib import ExitStack
import concourse.bass as bass
import concourse.mybir as mybir
from concourse.bass_utils import run_bass_kernel_spmd

F32 = mybir.dt.float32
BF16 = mybir.dt.bfloat16
I32 = mybir.dt.int32
AF = mybir.ActivationFunctionType
ALU = mybir.AluOpType
AX = mybir.AxisListType.X

B, L, H, U3 = 4, 8192, 1024, 3072
T = 4096
HC = 8
UC = 24
IC = 24
NT = 8
NTT = 32
CQ = 65536.0
DE_SEMS = ["de0", "de1", "de2", "de3"]
QMAX = 32767.0


class _RecInstr:
    __slots__ = ("_rq", "_rid")

    def __init__(self, rq, rid):
        self._rq = rq
        self._rid = rid

    def __getattr__(self, name):
        def call(*args, **kwargs):
            self._rq.append(("chain", self._rid, name, args, kwargs))
            return self
        return call


class _RecEngine:
    __slots__ = ("_name", "_q", "_n")

    def __init__(self, name):
        self._name = name
        self._q = []
        self._n = 0

    def __getattr__(self, name):
        if name == "then_inc":
            raise AttributeError(name)

        def call(*args, **kwargs):
            rid = self._n
            self._n += 1
            self._q.append(("call", rid, name, args, kwargs))
            return _RecInstr(self._q, rid)
        return call


class Builder:
    def __init__(self):
        self.nc = bass.Bass("TRN2", target_bir_lowering=False, debug=False)
        self.ctx = ExitStack()
        self.CNT = {}
        self.SEM = {}
        self._rec = {}

    def sem(self, name):
        self.SEM[name] = self.ctx.enter_context(self.nc.semaphore(name))
        self.CNT[name] = 0

    def inc(self, bi, name, amt=1):
        if not hasattr(bi, "then_inc"):
            bi = bi.drain()
        bi.then_inc(self.SEM[name], amt)
        self.CNT[name] += amt
        return self.CNT[name]

    def dinc(self, bi, name):
        return self.inc(bi, name, 16)

    def wait(self, eng, name, val=None):
        eng.wait_ge(self.SEM[name], self.CNT[name] if val is None else val)

    def sb(self, name, shape, dt):
        return self.ctx.enter_context(self.nc.sbuf_tensor(name, shape, dt))

    def on(self, engine_name):
        rec = self._rec.setdefault(engine_name, _RecEngine(engine_name))

        def deco(fn, *a, **k):
            fn(rec, *a, **k)
        return deco

    def flush_engines(self):
        for name, rec in self._rec.items():
            q = rec._q

            def replay(eng, q=q):
                objs = {}
                for kind, rid, meth, args, kwargs in q:
                    if kind == "call":
                        objs[rid] = getattr(eng, meth)(*args, **kwargs)
                    else:
                        objs[rid] = getattr(objs[rid], meth)(*args, **kwargs)
            getattr(self.block, name)(replay)
        self._rec = {}

    def build(self):
        nc = self.nc
        self.x_d = nc.dram_tensor("x", [T, H], F32, kind="ExternalInput").ap()
        self.wrkv_d = nc.dram_tensor("wrkv", [H, U3], F32, kind="ExternalInput").ap()
        self.brkv_d = nc.dram_tensor("brkv", [U3], F32, kind="ExternalInput").ap()
        self.wo_d = nc.dram_tensor("wo", [H, H], F32, kind="ExternalInput").ap()
        self.bo_d = nc.dram_tensor("bo", [H], F32, kind="ExternalInput").ap()
        self.nulog_d = nc.dram_tensor("nulog", [H], F32, kind="ExternalInput").ap()
        self.glog_d = nc.dram_tensor("glog", [H], F32, kind="ExternalInput").ap()
        self.wg_d = nc.dram_tensor("wg", [U3, H], F32, kind="ExternalInput").ap()
        self.wu_d = nc.dram_tensor("wu", [U3, H], F32, kind="ExternalInput").ap()
        self.wd_d = nc.dram_tensor("wd", [H, U3], F32, kind="ExternalInput").ap()
        self.sel_d = nc.dram_tensor("sel", [128, 16], F32, kind="ExternalInput").ap()
        self.y_d = nc.dram_tensor("y", [T, H], F32, kind="ExternalOutput").ap()
        dr = lambda n, s, d: nc.dram_tensor(n, s, d).ap()
        self.u_sp = dr("u_sp", [16, 128, T], BF16)
        self.r_sp = dr("r_sp", [8, 128, T], BF16)
        self.up_sp = dr("up_sp", [IC, 128, T], BF16)
        self.gate_sp = dr("gate_sp", [IC, 128, T], BF16)
        self.g2_sp = dr("g2_sp", [IC, 128, T], BF16)
        self.o_sp = dr("o_sp", [IC, 128, T], BF16)
        self.wuT_sp = dr("wuT_sp", [IC, HC, 128, 128], BF16)
        self.att_sp = dr("att_sp", [NTT, 128, 1024], BF16)
        self.u_att = dr("u_att", [NTT, 128, 16, 128], BF16)
        self.r_att = dr("r_att", [NTT, 128, 8, 128], BF16)
        self.o_att = dr("o_att", [NTT, 128, IC, 128], BF16)
        self.wgT_sp = dr("wgT_sp", [IC, HC, 128, 128], BF16)
        self.cc_in = dr("cc_in", [1, 16], F32)
        self.cc_out = dr("cc_out", [1, 16], F32)
        self.ag_in = dr("ag_in", [16, 128], F32)
        self.ag_out = dr("ag_out", [128, 128], F32)

        sb = self.sb
        self.arenaA = sb("arenaA", [128, 8 * T], BF16)
        self.wdT = sb("wdT", [128, IC * 1024], BF16)
        self.wo_sb = sb("wo_sb", [128, HC * 1024], BF16)
        self.wkB = sb("wkB", [128, 8192], F32)
        self.ident = sb("ident", [128, 128], BF16)
        self.identf = sb("identf", [128, 128], F32)
        self.ones_b = sb("ones_b", [1, 128], BF16)
        self.ones_f = sb("ones_f", [1, 128], F32)
        self.bo_row = sb("bo_row", [1, 1024], BF16)
        self.consts = sb("consts", [128, 8], F32)
        self.brkv_t = sb("brkv_t", [128, 24], F32)
        self.nulog_t = sb("nulog_t", [128, 8], F32)
        self.glog_t = sb("glog_t", [128, 8], F32)
        self.lam_t = sb("lam_t", [128, 8], F32)
        self.gamma_t = sb("gamma_t", [128, 8], F32)
        self.stats = sb("stats", [128, 2, 12], F32)
        self.mvt = sb("mvt", [128, 2, 2], F32)
        self.tiny = sb("tiny", [128, 12], F32)
        self.maxc = sb("maxc", [128, 36], F32)
        self.upmaxc = sb("upmaxc", [128, 96], F32)
        self.gatemaxc = sb("gatemaxc", [128, 96], F32)
        self.g2maxc = sb("g2maxc", [128, 96], F32)
        self.pmaxc = sb("pmaxc", [128, 96], F32)
        self.mx1 = sb("mx1", [128, 1], F32)
        self.cc_sb = sb("cc_sb", [1, 16], F32)
        self.ccr_sb = sb("ccr_sb", [1, 16], F32)
        self.scales = sb("scales", [128, 20], F32)
        self.S_loc = sb("S_loc", [128, 16], F32)
        self.S_prev = sb("S_prev", [128, 16], F32)
        self.carries = sb("carries", [128, 16], F32)
        self.ag_sb = sb("ag_sb", [128, 128], F32)
        self.agT_sb = None
        self.sel_sb = sb("sel_sb", [128, 16], F32)
        self.xld = sb("xld", [128, 2, 1024], F32)
        self.xnstage = sb("xnstage", [128, 2, 1024], BF16)
        self.wstream = sb("wstream", [128, 2, HC, 128], BF16)
        self.estage = sb("estage", [128, 4, 1024], BF16)
        self.iscr = sb("iscr", [128, 4096], I32)
        self.kstage = self.wkB[:, :].bitcast(BF16).rearrange(
            "p (c t n) -> p c t n", c=8, t=4)
        self.pdqst = self.xnstage
        self.bmax = sb("bmax", [128, 1, 96], F32)
        self.binv = sb("binv", [128, 1, 96], F32)
        self.bs = sb("bs", [128, 1, 96], F32)

        wkb = self.wkB[:, :]
        self.wf = wkb[:, 0:6144].rearrange("p (s n) -> p s n", s=2)
        self.wq = wkb[:, 6144:7680].bitcast(BF16)
        self.ubuf = wkb[:, 0:4096].bitcast(BF16).rearrange("p (s n) -> p s n", s=2)
        self.ydisc = wkb[:, 4096:8192]
        self.gbuf = self.ubuf
        self.sigb = wkb[:, 4096:6144].rearrange("p (s n) -> p s n", s=2)
        self.svb = wkb[:, 6144:8192].rearrange("p (s n) -> p s n", s=2)
        self.att_u = wkb[:, 0:2048].bitcast(BF16).rearrange("p (s n) -> p s n", s=2)
        self.att_r = wkb[:, 2048:3072].bitcast(BF16).rearrange("p (s n) -> p s n", s=2)
        self.att_y = wkb[:, 3072:5120]
        self.att_n = wkb[:, 5120:6144]
        self.att_d = wkb[:, 6144:7168]
        self.att_c = wkb[:, 7168:8192]
        self.ratio = self.att_c[:, :].bitcast(BF16).rearrange("p (s c n) -> p s c n", s=2, c=8)
        self.g2in = wkb[:, 0:4096].bitcast(BF16).rearrange("p (s n) -> p s n", s=2)
        self.upin = wkb[:, 4096:8192].bitcast(BF16).rearrange("p (s n) -> p s n", s=2)
        self.ostage = wkb[:, 0:3072].bitcast(BF16).rearrange("p (s n) -> p s n", s=2)
        self.oq = wkb[:, 3072:4608].bitcast(BF16)
        self.oq2 = wkb[:, 6656:8192].bitcast(BF16)
        self.outst = wkb[:, 4608:6656].rearrange("p (s n) -> p s n", s=2)
        self.upq_t = self.xnstage[:, :, :].rearrange("p s n -> p (s n)")

        self.bigps = self.ctx.enter_context(nc.psum_tensor("bigps", [128, 3584], F32))
        self.pt = self.ctx.enter_context(nc.psum_tensor("pt", [128, 1024], BF16))
        self.pa = self.bigps[:, 2048:3072]
        self.fdq = self.bigps[:, 2048:3072]
        self.px = self.bigps[:, 3072:3584]

        for n in ["dx0", "dx1", "dw0", "dw1", "dwg0", "dwg1", "da0", "da1", "db0", "db1",
                  "dsp", "de0", "de1", "de2", "de3", "dci", "dq0", "dq1", "dy0", "dy1", "dy", "dg", "dcc", "dk", "dk0", "dk1", "mt", "mv_", "ms", "mg", "mc"]:
            self.sem(n)

        self.block = self.ctx.enter_context(nc.Block())

        self.p0()
        self.p1()
        self.r1_start()
        self.dq_init()
        self.p2()
        self.pdq_down()
        self.r1_finish()
        self.p3()
        self.p4()
        self.p5()
        self.r2()
        self.p6()
        self.r3()
        self.p675()
        self.r4()
        self.p7()
        self.fin()
        self.flush_engines()
        return nc

    def bank(self, i):
        return self.bigps[:, i * 512:(i + 1) * 512]

    def ring_init(self):
        self.rbank = [self.bigps[:, i * 512:(i + 1) * 512] for i in range(7)]
        self.rbank.append(self.pt[:, :].bitcast(F32))
        self.rnext = 0
        self.rmark = {i: None for i in range(8)}

    def ralloc(self, n):
        idx = [(self.rnext + j) % 8 for j in range(n)]
        self.rnext = (self.rnext + n) % 8
        return idx

    def xnA(self, hc, t0, n):
        return self.arenaA[:, hc * T + t0: hc * T + t0 + n]

    def attnA(self, tc):
        return self.arenaA[:, tc * 1024:(tc + 1) * 1024]

    def p0(self):
        inc, dinc, wait = self.inc, self.dinc, self.wait

        @self.on("gpsimd")
        def _(g):
            g.iota(out=self.iscr[:, 0:128], pattern=[[1, 128]], base=0, channel_multiplier=-1)
            g.memset(self.consts[:, 0:1], 65536.5)
            g.memset(self.consts[:, 1:2], 0.5)
            g.memset(self.consts[:, 2:3], 8.0)
            g.memset(self.ones_b[:, :], 1.0)
            g.memset(self.cc_sb[:, :], 0.0)
            g.memset(self.ones_f[:, :], 1.0)
            g.drain()
            g.tensor_scalar(out=self.ident[:, :], in0=self.iscr[:, 0:128], scalar1=0.0,
                            scalar2=None, op0=ALU.is_equal)
            g.tensor_scalar(out=self.identf[:, :], in0=self.iscr[:, 0:128], scalar1=0.0,
                            scalar2=None, op0=ALU.is_equal)
            g.drain()
            self.mg_init = inc(g, "mg")
            dinc(g.dma_start(out=self.wo_sb.rearrange("p (c n) -> p c n", n=1024),
                        in_=self.wo_d.rearrange("(c p) n -> p c n", p=128)), "dg")
            self.dg_wo = dinc(g.dma_start(out=self.bo_row[:, :], in_=self.bo_d[None, :]), "dg")

        @self.on("sync")
        def _(s):
            dinc(s.dma_start(out=self.brkv_t[:, :], in_=self.brkv_d.rearrange("(c p) -> p c", p=128), allow_slow_non_contiguous=True), "da0")
            dinc(s.dma_start(out=self.nulog_t[:, :], in_=self.nulog_d.rearrange("(c p) -> p c", p=128), allow_slow_non_contiguous=True), "da0")
            dinc(s.dma_start(out=self.glog_t[:, :], in_=self.glog_d.rearrange("(c p) -> p c", p=128), allow_slow_non_contiguous=True), "da0")
            self.da_p0 = dinc(s.dma_start(out=self.sel_sb[:, :], in_=self.sel_d[:, :]), "da0")

        @self.on("scalar")
        def _(s):
            wait(s, "da0", self.da_p0)
            s.activation(out=self.tiny[:, 0:8], in_=self.nulog_t[:, :], func=AF.Exp)
            s.drain()
            s.activation(out=self.lam_t[:, :], in_=self.tiny[:, 0:8], func=AF.Exp, scale=-1.0)
            s.drain()
            inc(s, "ms")

        @self.on("vector")
        def _(v):
            wait(v, "da0", self.da_p0)
            v.tensor_tensor(out=self.glog_t[:, :], in0=self.nulog_t[:, :],
                            in1=self.glog_t[:, :], op=ALU.add)
            v.drain()
            inc(v, "mv_")

        @self.on("scalar")
        def _(s):
            wait(s, "mv_")
            s.activation(out=self.gamma_t[:, :], in_=self.glog_t[:, :], func=AF.Exp)
            s.drain()
            inc(s, "ms")

        @self.on("vector")
        def _(v):
            wait(v, "ms")
            v.tensor_scalar(out=self.gamma_t[:, :], in0=self.gamma_t[:, :], scalar1=-1.0,
                            scalar2=None, op0=ALU.add)
            v.drain()
            self.mv_gamma = inc(v, "mv_")

    def p1(self):
        inc, dinc, wait = self.inc, self.dinc, self.wait
        st_mark = {}
        ln_mark = {}
        xn_mark = {}
        tp_mark = {}
        ev_mark = {}
        xnA_v = self.arenaA[:, :].rearrange("p (c t) -> p c t", t=T)

        def load(tt):
            slot = tt % 2
            dxs = "dx0" if slot == 0 else "dx1"

            @self.on("sync")
            def _(s, tt=tt, slot=slot, dxs=dxs):
                if tt >= 2:
                    wait(s, "mv_", xn_mark[tt - 2])
                dinc(s.dma_start(out=self.xld[:, slot, :],
                                 in_=self.x_d[tt * 128:(tt + 1) * 128, :]), dxs)

        def stats(tt):
            slot = tt % 2
            dxs = "dx0" if slot == 0 else "dx1"

            @self.on("vector")
            def _(v, tt=tt, slot=slot, dxs=dxs):
                wait(v, dxs)
                if tt >= 2:
                    wait(v, "ms", ln_mark[tt - 2])
                v.bn_stats(out=self.stats[:, slot, 0:6], in_=self.xld[:, slot, 0:512])
                v.bn_stats(out=self.stats[:, slot, 6:12], in_=self.xld[:, slot, 512:1024])
                v.drain()
                v.bn_aggr(out=self.mvt[:, slot, :], in_=self.stats[:, slot, :])
                v.drain()
                v.tensor_scalar(out=self.tiny[:, slot:slot + 1],
                                in0=self.mvt[:, slot, 1:2], scalar1=1e-5,
                                scalar2=None, op0=ALU.add)
                st_mark[tt] = inc(v, "mv_")

        def lnexp(tt):
            slot = tt % 2

            @self.on("scalar")
            def _(s, tt=tt, slot=slot):
                wait(s, "mv_", st_mark[tt])
                if tt >= 2:
                    wait(s, "mv_", xn_mark[tt - 2])
                s.activation(out=self.tiny[:, 2 + slot:3 + slot],
                             in_=self.tiny[:, slot:slot + 1], func=AF.Ln)
                s.drain()
                s.activation(out=self.tiny[:, 4 + slot:5 + slot],
                             in_=self.tiny[:, 2 + slot:3 + slot], func=AF.Exp,
                             scale=-0.5)
                s.drain()
                s.activation(out=self.tiny[:, 6 + slot:7 + slot],
                             in_=self.tiny[:, 4 + slot:5 + slot], func=AF.Identity,
                             scale=-1.0)
                s.drain()
                s.activation(out=self.tiny[:, 8 + slot:9 + slot],
                             in_=self.mvt[:, slot, 0:1], func=AF.Identity,
                             scale=self.tiny[:, 6 + slot:7 + slot])
                s.drain()
                s.activation(out=self.xnstage[:, slot, :], in_=self.xld[:, slot, :],
                             func=AF.Identity, scale=self.tiny[:, 4 + slot:5 + slot],
                             bias=self.tiny[:, 8 + slot:9 + slot])
                ln_mark[tt] = inc(s, "ms")

        def xn(tt):
            slot = tt % 2

            @self.on("vector")
            def _(v, tt=tt, slot=slot):
                wait(v, "ms", ln_mark[tt])
                v.tensor_reduce(out=self.maxc[:, tt:tt + 1], in_=self.xnstage[:, slot, :],
                                axis=AX, op=ALU.max, apply_absolute_value=True)
                xn_mark[tt] = inc(v, "mv_")

        def transpose(tt):
            slot = tt % 2

            @self.on("tensor")
            def _(t, tt=tt, slot=slot):
                if tt == 0:
                    wait(t, "mg", self.mg_init)
                wait(t, "mv_", xn_mark[tt])
                if tt >= 1:
                    wait(t, "ms", ev_mark[tt - 1])
                for hc in range(HC):
                    li = t.transpose(self.pt[:, hc * 128:(hc + 1) * 128],
                                     self.xnstage[:, slot, hc * 128:(hc + 1) * 128],
                                     self.ident[:, :])
                tp_mark[tt] = inc(li, "mt")

        def evac(tt):
            @self.on("scalar")
            def _(s, tt=tt):
                wait(s, "mt", tp_mark[tt])
                s.activation(out=xnA_v[:, :, tt * 128:(tt + 1) * 128],
                             in_=self.pt[:, :].rearrange("p (c q) -> p c q", q=128),
                             func=AF.Identity)
                ev_mark[tt] = inc(s, "ms")

        for tt in range(NTT):
            load(tt)
            stats(tt)
            lnexp(tt)
            if tt >= 1:
                xn(tt - 1)
                transpose(tt - 1)
                evac(tt - 1)
        xn(NTT - 1)
        transpose(NTT - 1)
        evac(NTT - 1)
        self.p1_done = self.CNT["ms"]

    def xpart_max(self, cols_ap, out_col):
        inc, wait = self.inc, self.wait

        @self.on("vector")
        def _(v):
            v.tensor_reduce(out=self.mx1[:, :], in_=cols_ap, axis=AX, op=ALU.max)
            v.drain()
            inc(v, "mv_")

        @self.on("tensor")
        def _(t):
            wait(t, "mv_")
            wait(t, "ms")
            inc(t.transpose(self.px[0:1, 0:128], self.mx1[:, 0:1], self.identf[:, :]),
                "mt")

        @self.on("vector")
        def _(v, out_col=out_col):
            wait(v, "mt")
            v.tensor_reduce(out=self.cc_sb[0:1, out_col:out_col + 1],
                            in_=self.px[0:1, 0:128], axis=AX, op=ALU.max)
            v.drain()
            inc(v, "mv_")

    def all_reduce_max_trigger(self):
        inc, dinc, wait = self.inc, self.dinc, self.wait

        @self.on("scalar")
        def _(s):
            wait(s, "mv_")
            dinc(s.dma_start(out=self.cc_in[:, :], in_=self.cc_sb[:, :]), "dci")

        @self.on("gpsimd")
        def _(g):
            wait(g, "dci")
            inc(g.collective_compute("AllReduce", ALU.max,
                                     replica_groups=[list(range(8))],
                                     ins=[self.cc_in.opt()], outs=[self.cc_out.opt()]),
                "mc")

    def all_reduce_max_fetch(self):
        inc, dinc, wait = self.inc, self.dinc, self.wait

        @self.on("gpsimd")
        def _(g):
            wait(g, "mc")
            dinc(g.dma_start(out=self.ccr_sb[:, :], in_=self.cc_out[:, :]), "dcc")

    def all_reduce_max(self):
        self.all_reduce_max_trigger()
        self.all_reduce_max_fetch()

    def bcast_scale(self, src_col, s_col, inv_col, raw_col=None):
        inc, wait = self.inc, self.wait

        @self.on("tensor")
        def _(t):
            wait(t, "dcc")
            wait(t, "mv_")
            wait(t, "ms")
            inc(t.matmul(self.px[0:128, 0:1], self.ones_f[0:1, 0:128],
                         self.ccr_sb[0:1, src_col:src_col + 1], start=True, stop=True),
                "mt")

        @self.on("vector")
        def _(v):
            wait(v, "mt")
            if raw_col is not None:
                v.tensor_copy(out=self.scales[:, raw_col:raw_col + 1], in_=self.px[0:128, 0:1])
            v.tensor_scalar(out=self.scales[:, s_col:s_col + 1], in0=self.px[0:128, 0:1],
                            scalar1=1.0 / QMAX, scalar2=1e-12, op0=ALU.mult, op1=ALU.max)
            v.drain()
            v.reciprocal(out=self.scales[:, inv_col:inv_col + 1],
                         in_=self.scales[:, s_col:s_col + 1])
            v.drain()
            inc(v, "mv_")

    def r1_start(self):
        self.xpart_max(self.maxc[:, 0:NTT], 0)
        self.all_reduce_max_trigger()

    def r1_finish(self):
        self.all_reduce_max_fetch()
        self.bcast_scale(0, 0, 1)

        @self.on("vector")
        def _(v):
            self.wait(v, "mv_")
            v.tensor_scalar(out=self.scales[:, 17:18], in0=self.scales[:, 0:1],
                            scalar1=-CQ, scalar2=None, op0=ALU.mult)
            v.drain()
            self.inc(v, "mv_")
        self.r1_scales = self.CNT["mv_"]

    def p2(self):
        inc, dinc, wait = self.inc, self.dinc, self.wait
        mt_of = {}
        ev_of = {}
        self.last_e = {}
        last_e = self.last_e
        k_sg = {}
        kv_last = {}
        spillq = []
        self._spq = spillq
        kst = self.wkB[:, 0:2048].bitcast(BF16)

        uc_order = list(range(8)) + [x for cc in range(8) for x in (8 + cc, 16 + cc)]
        dq_jobs = [(self.wu_d, icq, self.wuT_sp) for icq in range(IC)] + \
                  [(self.wg_d, icq, self.wgT_sp) for icq in range(IC)]
        dq_next = [0]
        gsg = -1
        for it, uc in enumerate(uc_order):
            slot = it % 2
            dws = "dwg0" if slot == 0 else "dwg1"

            @self.on("gpsimd")
            def _(g, uc=uc, slot=slot, dws=dws, it=it):
                if it >= 2:
                    wait(g, "mt", mt_of[(it - 2) * 4 + 3])
                dinc(g.dma_start(out=self.wstream[:, slot, :, :],
                                 in_=self.wrkv_d[:, uc * 128:(uc + 1) * 128]
                                 .rearrange("(hc p) q -> p hc q", p=128)), dws)

            for sg in range(4):
                gsg += 1
                t0 = sg * 1024
                b0 = (gsg % 2) * 2

                @self.on("tensor")
                def _(t, uc=uc, slot=slot, dws=dws, it=it, sg=sg, gsg=gsg, t0=t0, b0=b0):
                    if sg == 0:
                        wait(t, dws)
                        if it == 0:
                            wait(t, "ms", self.p1_done)
                    if gsg >= 2:
                        sname, scnt = ev_of[gsg - 2]
                        wait(t, sname, scnt)
                    for hc in range(HC):
                        for tl in range(2):
                            li = t.matmul(self.bank(b0 + tl), self.wstream[:, slot, hc, :],
                                          self.xnA(hc, t0 + tl * 512, 512),
                                          start=(hc == 0), stop=(hc == HC - 1))
                    mt_of[gsg] = inc(li, "mt")

                if uc < 8:
                    if sg == 0:
                        while spillq:
                            spillq.pop(0)()

                    @self.on("scalar")
                    def _(s, uc=uc, sg=sg, gsg=gsg, t0=t0, b0=b0):
                        wait(s, "mt", mt_of[gsg])
                        esl = gsg % 4
                        if esl in last_e:
                            wait(s, last_e[esl][0], last_e[esl][1])
                        for tl in range(2):
                            s.activation(out=self.estage[:, esl, tl * 512:(tl + 1) * 512],
                                         in_=self.bank(b0 + tl), func=AF.Sigmoid,
                                         bias=self.brkv_t[:, uc:uc + 1])
                        ev_of[gsg] = ("ms", inc(s, "ms"))

                    def sp_r(uc=uc, sg=sg, gsg=gsg):
                        @self.on("sync")
                        def _(s, uc=uc, sg=sg, gsg=gsg):
                            esl = gsg % 4
                            des = DE_SEMS[esl]
                            wait(s, ev_of[gsg][0], ev_of[gsg][1])
                            dinc(s.dma_start(
                                out=self.r_att[sg * 8:(sg + 1) * 8, :, uc, :]
                                .rearrange("tc p n -> p tc n"),
                                in_=self.estage[:, esl, 0:1024]
                                .rearrange("p (tc n) -> p tc n", n=128)), des)
                            last_e[esl] = (des, self.CNT[des])
                    spillq.append(sp_r)
                elif uc < 16:
                    if sg == 0:
                        while spillq:
                            spillq.pop(0)()

                    @self.on("scalar")
                    def _(s, uc=uc, sg=sg, gsg=gsg, t0=t0, b0=b0):
                        wait(s, "mt", mt_of[gsg])
                        cc = uc - 8
                        if sg == 0 and cc >= 1:
                            wait(s, "mv_", kv_last[cc - 1])
                            dkp = "dk0" if (cc - 1) % 2 == 0 else "dk1"
                            wait(s, dkp, self.CNT[dkp])
                        for tl in range(2):
                            s.activation(out=kst[:, t0 + tl * 512:t0 + (tl + 1) * 512],
                                         in_=self.bank(b0 + tl), func=AF.Exp,
                                         bias=self.brkv_t[:, uc:uc + 1])
                        k_sg[(cc, sg)] = ev_of[gsg] = ("ms", inc(s, "ms"))

                    def sp_k(uc=uc, sg=sg, gsg=gsg, t0=t0):
                        @self.on("sync")
                        def _(s, uc=uc, sg=sg, gsg=gsg, t0=t0):
                            cc = uc - 8
                            wait(s, ev_of[gsg][0], ev_of[gsg][1])
                            dkp = "dk0" if cc % 2 == 0 else "dk1"
                            dinc(s.dma_start(out=self.u_sp[8 + cc, :, t0:t0 + 1024],
                                             in_=kst[:, t0:t0 + 1024]), dkp)
                            dinc(s.dma_start(
                                out=self.u_att[sg * 8:(sg + 1) * 8, :, 8 + cc, :]
                                .rearrange("tc p n -> p tc n"),
                                in_=kst[:, t0:t0 + 1024]
                                .rearrange("p (tc n) -> p tc n", n=128)), dkp)
                    spillq.append(sp_k)
                else:
                    @self.on("vector")
                    def _(v, uc=uc, sg=sg, gsg=gsg, t0=t0, b0=b0):
                        cc = uc - 16
                        wait(v, "mt", mt_of[gsg])
                        sname, scnt = k_sg[(cc, sg)]
                        wait(v, sname, scnt)
                        esl = gsg % 4
                        if esl in last_e:
                            wait(v, last_e[esl][0], last_e[esl][1])
                        for tl in range(2):
                            v.scalar_tensor_tensor(
                                out=self.estage[:, esl, tl * 512:(tl + 1) * 512],
                                in0=self.bank(b0 + tl), scalar=self.brkv_t[:, uc:uc + 1],
                                in1=kst[:, t0 + tl * 512:t0 + (tl + 1) * 512],
                                op0=ALU.add, op1=ALU.mult)
                        kv_last[cc] = inc(v, "mv_")
                        ev_of[gsg] = ("mv_", kv_last[cc])

                    def sp_v(uc=uc, sg=sg, gsg=gsg, t0=t0):
                        @self.on("sync")
                        def _(s, uc=uc, sg=sg, gsg=gsg, t0=t0):
                            cc = uc - 16
                            wait(s, "mv_", kv_last[cc])
                            esl = gsg % 4
                            des = DE_SEMS[esl]
                            dinc(s.dma_start(
                                out=self.u_sp[cc, :, t0:t0 + 1024],
                                in_=self.estage[:, esl, 0:1024]), des)
                            dinc(s.dma_start(
                                out=self.u_att[sg * 8:(sg + 1) * 8, :, cc, :]
                                .rearrange("tc p n -> p tc n"),
                                in_=self.estage[:, esl, 0:1024]
                                .rearrange("p (tc n) -> p tc n", n=128)), des)
                            last_e[esl] = (des, self.CNT[des])
                    spillq.append(sp_v)
                if len(spillq) >= 3:
                    spillq.pop(0)()

            want = ((it + 1) * len(dq_jobs)) // len(uc_order)
            while dq_next[0] < want:
                n = dq_next[0]
                srcw, icq, dest = dq_jobs[n]
                self.dq_front(srcw, icq, 1024, self.wf_ug,
                              self.wq_ug2[n % 3], n % 2)
                if n >= 2:
                    picq = dq_jobs[n - 2][1]
                    self.dq_back(self._dq["g"] - 2, picq, 1024, "hbm",
                                 dq_jobs[n - 2][2], self.wq_ug2[(n - 2) % 3])
                dq_next[0] += 1

        while dq_next[0] < len(dq_jobs):
            n = dq_next[0]
            srcw, icq, dest = dq_jobs[n]
            self.dq_front(srcw, icq, 1024, self.wf_ug, self.wq_ug2[n % 3], n % 2)
            if n >= 2:
                picq = dq_jobs[n - 2][1]
                self.dq_back(self._dq["g"] - 2, picq, 1024, "hbm",
                             dq_jobs[n - 2][2], self.wq_ug2[(n - 2) % 3])
            dq_next[0] += 1
        for n in (len(dq_jobs) - 2, len(dq_jobs) - 1):
            self.dq_back(self._dq["g"] - (len(dq_jobs) - 1 - n), dq_jobs[n][1], 1024,
                         "hbm", dq_jobs[n][2], self.wq_ug2[n % 3])
        while spillq:
            spillq.pop(0)()
        self._spq = None
        self.p2_mt_last = self.CNT["mt"]
        self.p2_ev_ms = self.CNT["ms"]
        self.p2_done_marks = (self.CNT["mv_"], tuple(self.CNT[d] for d in DE_SEMS),
                              self.CNT["dk0"], self.CNT["dk1"])

    def dq_init(self):
        self._dq = {"vdone": {}, "tdone": {}, "sdone": {}, "meta": {}, "g": -1}
        wkb = self.wkB[:, :]
        self.wf_ug = wkb[:, 2048:4096].rearrange("p (s n) -> p s n", s=2)
        self.wq_ug2 = [wkb[:, 4096:4608].bitcast(BF16),
                       wkb[:, 4608:5120].bitcast(BF16),
                       wkb[:, 5120:5632].bitcast(BF16)]

    def dq_chunk(self, srcw, icq, I, kind, dest, wf, wq, islot):
        self.dq_front(srcw, icq, I, wf, wq, islot, guard=1)
        self.dq_back(self._dq["g"], icq, I, kind, dest, wq)

    def dq_front(self, srcw, icq, I, wf, wq, islot, guard=3):
        inc, dinc, wait = self.inc, self.dinc, self.wait
        D = self._dq
        D["g"] += 1
        g = D["g"]
        D["meta"][g] = (icq, I, wq)
        vdone, tdone, sdone = D["vdone"], D["tdone"], D["sdone"]
        nblk = I // 32
        nb = I // 128
        slot = g % 2
        dbs = "db0" if slot == 0 else "db1"

        @self.on("sync")
        def _(s, srcw=srcw, icq=icq, slot=slot, islot=islot, dbs=dbs, I=I, g=g, wf=wf):
            if g >= 2:
                wait(s, "mv_", vdone[g - 2])
            dinc(s.dma_start(out=wf[:, islot, 0:I],
                             in_=srcw[icq * 128:(icq + 1) * 128, :]), dbs)

        @self.on("vector")
        def _(v, slot=slot, islot=islot, dbs=dbs, I=I, nblk=nblk, g=g, wf=wf):
            wait(v, dbs)
            v.tensor_reduce(out=self.bmax[:, 0, 0:nblk],
                            in_=wf[:, islot, 0:I].rearrange("p (b k) -> p b k", k=32),
                            axis=AX, op=ALU.max, apply_absolute_value=True)
            v.drain()
            v.tensor_scalar(out=self.bs[:, 0, 0:nblk],
                            in0=self.bmax[:, 0, 0:nblk],
                            scalar1=1.0 / 7.0, scalar2=1e-12,
                            op0=ALU.mult, op1=ALU.max)
            v.drain()
            v.reciprocal(out=self.binv[:, 0, 0:nblk],
                         in_=self.bs[:, 0, 0:nblk])
            v.drain()
            v.tensor_tensor(out=wf[:, islot, 0:I].rearrange("p (b k) -> p b k", k=32),
                            in0=wf[:, islot, 0:I].rearrange("p (b k) -> p b k", k=32),
                            in1=self.binv[:, 0, 0:nblk].broadcast_to((128, nblk, 32)),
                            op=ALU.mult)
            inc(v, "mv_")

        ibdone = D.setdefault("ibdone", {})
        for q0 in range(0, I, 1024):
            @self.on("scalar")
            def _(s, slot=slot, islot=islot, q0=q0, g=g, wf=wf):
                wait(s, "mv_")
                if g >= 2:
                    wait(s, "mv_", ibdone[g - 2])
                s.activation(out=self.iscr[:, slot * 1024:slot * 1024 + 1024],
                             in_=wf[:, islot, q0:q0 + 1024],
                             func=AF.Identity, scale=1.0,
                             bias=self.consts[:, 2:3])
                inc(s, "ms")

            @self.on("vector")
            def _(v, slot=slot, islot=islot, q0=q0, nblk=nblk, g=g, wf=wf, wq=wq,
                  guard=guard):
                if g >= guard and q0 == 0:
                    wait(v, "mt", tdone[g - guard])
                wait(v, "ms")
                fq = self.fdq[:, 0:1024]
                i1 = self.iscr[:, slot * 1024:slot * 1024 + 1024]
                v.scalar_tensor_tensor(out=fq, in0=i1,
                                       scalar=7.5, in1=wf[:, islot, q0:q0 + 1024],
                                       op0=ALU.subtract, op1=ALU.is_le)
                v.drain()
                v.scalar_tensor_tensor(out=fq, in0=i1,
                                       scalar=8.0, in1=fq,
                                       op0=ALU.subtract, op1=ALU.add)
                v.drain()
                b0 = q0 // 32
                v.tensor_tensor(out=wq[:, q0:q0 + 1024]
                                .rearrange("p (b k) -> p b k", k=32),
                                in0=fq.rearrange("p (b k) -> p b k", k=32),
                                in1=self.bs[:, 0, b0:b0 + 32]
                                .broadcast_to((128, 32, 32)),
                                op=ALU.mult)
                ibdone[g] = inc(v, "mv_")
        vdone[g] = self.CNT["mv_"]

    def dq_back(self, g, icq, I, kind, dest, wq):
        inc, dinc, wait = self.inc, self.dinc, self.wait
        D = self._dq
        vdone, tdone, sdone = D["vdone"], D["tdone"], D["sdone"]
        nb = I // 128
        for gq in range(nb // 8):
            @self.on("tensor")
            def _(t, nb=nb, g=g, gq=gq, wq=wq):
                if gq == 0:
                    wait(t, "mv_", vdone[g])
                    if g >= 1:
                        wait(t, "ms", sdone[g - 1])
                else:
                    wait(t, "ms")
                for bq in range(gq * 8, gq * 8 + 8):
                    li = t.transpose(self.pt[:, (bq % 8) * 128:(bq % 8) * 128 + 128],
                                     wq[:, bq * 128:(bq + 1) * 128],
                                     self.ident[:, :])
                inc(li, "mt")

            if kind != "sbuf" and getattr(self, "_spq", None) is not None:
                while self._spq:
                    self._spq.pop(0)()

            @self.on("scalar")
            def _(s, icq=icq, nb=nb, g=g, gq=gq, kind=kind, dest=dest):
                wait(s, "mt")
                if kind == "sbuf":
                    s.activation(
                        out=self.wdT.rearrange("p (b h) -> p b h", h=1024)
                        [:, gq * 8:(gq + 1) * 8, icq * 128:(icq + 1) * 128],
                        in_=self.pt[:, :].rearrange("p (b q) -> p b q", q=128),
                        func=AF.Identity)
                    inc(s, "ms")
                else:
                    wait(s, "dq0", self.CNT["dq0"])
                    wait(s, "dq1", self.CNT["dq1"])
                    for half in range(2):
                        s.activation(out=self.pdqst[:, half, 0:512],
                                     in_=self.pt[:, half * 512:(half + 1) * 512],
                                     func=AF.Identity)
                    inc(s, "ms")

            if kind != "sbuf":
                dqm = self.CNT["ms"]

                def sp_dq(icq=icq, g=g, gq=gq, dest=dest, dqm=dqm):
                    @self.on("sync")
                    def _(s, icq=icq, g=g, gq=gq, dest=dest, dqm=dqm):
                        wait(s, "ms", dqm)
                        for half in range(2):
                            dws = "dq0" if half == 0 else "dq1"
                            dinc(s.dma_start(
                                out=dest[icq, gq * 8 + half * 4:gq * 8 + half * 4 + 4, :, :]
                                .rearrange("h p q -> p h q"),
                                in_=self.pdqst[:, half, 0:512]
                                .rearrange("p (h q) -> p h q", q=128)), dws)
                if getattr(self, "_spq", None) is not None:
                    self._spq.append(sp_dq)
                else:
                    sp_dq()
        tdone[g] = self.CNT["mt"]
        sdone[g] = self.CNT["ms"]

    def pdq_down(self):
        p2_ev_last, p2_de, p2_dk0, p2_dk1 = self.p2_done_marks
        wait = self.wait

        @self.on("sync")
        def _(s):
            wait(s, "mv_", p2_ev_last)
            wait(s, "ms", self.p2_ev_ms)
            wait(s, "dk0", p2_dk0)
            wait(s, "dk1", p2_dk1)
            for dd, vv in zip(DE_SEMS, p2_de):
                wait(s, dd, vv)
            wait(s, "mt", self._dq["tdone"][self._dq["g"]])
        wf_dn = self.wkB[:, 0:6144].rearrange("p (s n) -> p s n", s=2)
        wq_dn = self.wkB[:, 6144:7680].bitcast(BF16)
        for icq in range(HC):
            self.dq_chunk(self.wd_d, icq, 3072, "sbuf", None, wf_dn, wq_dn, icq % 2)
        self.pdq_done = self.CNT["mv_"]
        self.dq_spill = (self.CNT["dq0"], self.CNT["dq1"])

    def p3(self):
        inc, dinc, wait = self.inc, self.dinc, self.wait
        scan_done = {}

        for c in range(16):
            slot = c % 2
            das = "da0" if slot == 0 else "da1"

            @self.on("sync")
            def _(s, c=c, slot=slot, das=das):
                if c == 0:
                    wait(s, "mv_", self.pdq_done)
                    for dd, vv in zip(DE_SEMS, self.p2_done_marks[1]):
                        wait(s, dd, vv)
                    wait(s, "dk0", self.p2_done_marks[2])
                    wait(s, "dk1", self.p2_done_marks[3])
                if c >= 2:
                    wait(s, "mv_", scan_done[c - 2])
                dinc(s.dma_start(out=self.ubuf[:, slot, :], in_=self.u_sp[c, :, :]), das)

            @self.on("vector")
            def _(v, c=c, slot=slot, das=das):
                wait(v, das)
                v.tensor_tensor_scan(out=self.ydisc[:, :],
                                     data0=self.lam_t[:, c % 8:c % 8 + 1]
                                     .broadcast_to((128, T)),
                                     data1=self.ubuf[:, slot, :],
                                     initial=0.0, op0=ALU.mult, op1=ALU.add)
                v.drain()
                v.tensor_copy(out=self.S_loc[:, c:c + 1], in_=self.ydisc[:, T - 1:T])
                v.drain()
                scan_done[c] = inc(v, "mv_")

        @self.on("tensor")
        def _(t):
            wait(t, "mv_")
            inc(t.transpose(self.px[0:16, 0:128], self.S_loc[:, :], self.identf[:, :]),
                "mt")

        @self.on("vector")
        def _(v):
            wait(v, "mt")
            v.tensor_copy(out=self.ag_sb[0:16, :], in_=self.px[0:16, 0:128])
            v.drain()
            inc(v, "mv_")

        @self.on("scalar")
        def _(s):
            wait(s, "mv_")
            dinc(s.dma_start(out=self.ag_in[:, :], in_=self.ag_sb[0:16, :]), "dci")

        @self.on("gpsimd")
        def _(g):
            wait(g, "dci")
            inc(g.collective_compute("AllGather", ALU.bypass,
                                     replica_groups=[list(range(8))],
                                     ins=[self.ag_in.opt()], outs=[self.ag_out.opt()]),
                "mc")
            wait(g, "mc")
            dinc(g.dma_start(out=self.ag_sb[:, :], in_=self.ag_out[:, :]), "dcc")

        @self.on("tensor")
        def _(t):
            wait(t, "dcc")
            inc(t.matmul(self.px[0:128, 0:16], self.ag_sb[:, :], self.sel_sb[:, :],
                         start=True, stop=True), "mt")

        @self.on("vector")
        def _(v):
            wait(v, "mt")
            v.tensor_copy(out=self.S_prev[:, :], in_=self.px[0:128, 0:16])
            v.drain()
            self.sprev_ready = inc(v, "mv_")

    def p4(self):
        inc, wait = self.inc, self.wait
        prev = {}
        for st in range(32):
            @self.on("scalar")
            def _(s, st=st):
                if st == 0:
                    wait(s, "mv_", self.r1_scales)
                    wait(s, "mt", self.p2_mt_last)
                s.activation(out=self.iscr[:, (st % 2) * 1024:(st % 2) * 1024 + 1024],
                             in_=self.arenaA[:, st * 1024:(st + 1) * 1024],
                             func=AF.Identity, scale=self.scales[:, 1:2],
                             bias=self.consts[:, 0:1])
                s.drain()
                s.activation(out=self.arenaA[:, st * 1024:(st + 1) * 1024],
                             in_=self.iscr[:, (st % 2) * 1024:(st % 2) * 1024 + 1024],
                             func=AF.Identity, scale=self.scales[:, 0:1],
                             bias=self.scales[:, 17:18])
                s.drain()
                prev[st] = inc(s, "ms")
        self.p4_done = ("ms", self.CNT["ms"])

    def p5(self):
        inc, dinc, wait = self.inc, self.dinc, self.wait
        mt_of = {}
        mvred = {}
        ev_mm = {}
        spillq = []
        self.p65_init()
        self.ring_init()
        att_next = [0]
        git = -1
        gsg = -1

        for wi, (wT, spill, mcols) in enumerate(
                [(self.wuT_sp, self.up_sp, self.upmaxc),
                 (self.wgT_sp, self.gate_sp, self.gatemaxc)]):
            for ic in range(IC):
                git += 1
                slot = git % 2
                dws = "dw0" if slot == 0 else "dw1"

                @self.on("sync")
                def _(s, ic=ic, slot=slot, dws=dws, git=git, wT=wT):
                    if git < 2:
                        wait(s, "dq0", self.dq_spill[0])
                        wait(s, "dq1", self.dq_spill[1])
                    if git >= 2:
                        wait(s, "mt", mt_of[(git - 2, 1)])
                    dinc(s.dma_start(out=self.wstream[:, slot, :, :],
                                     in_=wT[ic, :, :, :].rearrange("h p q -> p h q")),
                         dws)

                for g2 in range(2):
                    banks = self.ralloc(4)
                    bmarks = [self.rmark[b] for b in banks]
                    t0 = g2 * 2048

                    @self.on("tensor")
                    def _(t, ic=ic, slot=slot, dws=dws, git=git, g2=g2,
                          banks=banks, bmarks=bmarks, t0=t0):
                        if g2 == 0:
                            wait(t, dws)
                            if git == 0:
                                wait(t, "ms")
                                wait(t, "mv_")
                                wait(t, "mt", self.p2_mt_last)
                        for m in bmarks:
                            if m is not None:
                                wait(t, m[0], m[1])
                        for hc in range(HC):
                            for j in range(4):
                                li = t.matmul(self.rbank[banks[j]],
                                              self.wstream[:, slot, hc, :],
                                              self.xnA(hc, t0 + j * 512, 512),
                                              start=(hc == 0), stop=(hc == HC - 1))
                        mt_of[(git, g2)] = inc(li, "mt")

                    for pj in range(2):
                        gsg += 1
                        esl = gsg % 4
                        des = DE_SEMS[esl]
                        b0_, b1_ = banks[2 * pj], banks[2 * pj + 1]
                        tt0 = t0 + pj * 1024

                        @self.on("scalar")
                        def _(s, ic=ic, git=git, g2=g2, gsg=gsg, esl=esl, des=des,
                              b0_=b0_, b1_=b1_, tt0=tt0, spill=spill):
                            wait(s, "mt", mt_of[(git, g2)])
                            le = self.last_e.get(esl)
                            if le is not None:
                                wait(s, le[0], le[1])
                            if gsg >= 4:
                                wait(s, "mv_", mvred[gsg - 4])
                            s.activation(out=self.estage[:, esl, 0:512],
                                         in_=self.rbank[b0_], func=AF.Identity)
                            s.activation(out=self.estage[:, esl, 512:1024],
                                         in_=self.rbank[b1_], func=AF.Identity)
                            m = ("ms", inc(s, "ms"))
                            self.rmark[b0_] = m
                            self.rmark[b1_] = m
                            ev_mm[gsg] = m[1]
                        spillq.append((ic, gsg, esl, des, tt0, spill))
                        if len(spillq) >= 3:
                            sic, sgsg, sesl, sdes, stt0, sspill = spillq.pop(0)

                            @self.on("sync")
                            def _(s, sic=sic, sgsg=sgsg, sesl=sesl, sdes=sdes,
                                  stt0=stt0, sspill=sspill):
                                wait(s, "ms", ev_mm[sgsg])
                                self.last_e[sesl] = (sdes, dinc(s.dma_start(
                                    out=sspill[sic, :, stt0:stt0 + 1024],
                                    in_=self.estage[:, sesl, 0:1024]), sdes))

                        @self.on("vector")
                        def _(v, gsg=gsg, esl=esl, mcols=mcols, git=git, g2=g2,
                              pj=pj, wi=wi):
                            m = self.rmark[0]
                            wait(v, "ms", self.CNT["ms"])
                            lit = (git - wi * IC) * 4 + g2 * 2 + pj
                            v.tensor_reduce(out=mcols[:, lit:lit + 1],
                                            in_=self.estage[:, esl, 0:1024],
                                            axis=AX, op=ALU.max,
                                            apply_absolute_value=True)
                            mvred[gsg] = inc(v, "mv_")

                want = ((git + 1) * NTT) // (2 * IC)
                while att_next[0] < want:
                    self.p65_front(att_next[0])
                    if att_next[0] >= 1:
                        self.p65_back(att_next[0] - 1)
                    att_next[0] += 1

        while spillq:
            sic, sgsg, sesl, sdes, stt0, sspill = spillq.pop(0)

            @self.on("sync")
            def _(s, sic=sic, sgsg=sgsg, sesl=sesl, sdes=sdes, stt0=stt0,
                  sspill=sspill):
                wait(s, "ms", ev_mm[sgsg])
                self.last_e[sesl] = (sdes, dinc(s.dma_start(
                    out=sspill[sic, :, stt0:stt0 + 1024],
                    in_=self.estage[:, sesl, 0:1024]), sdes))

        while att_next[0] < NTT:
            self.p65_front(att_next[0])
            if att_next[0] >= 1:
                self.p65_back(att_next[0] - 1)
            att_next[0] += 1
        self.p65_back(NTT - 1)
        self.p65_finish()
        self.p5_done_ms = self.CNT["ms"]
        self.p5_done_mv = self.CNT["mv_"]
        self.p5_done_de = tuple(self.CNT[d] for d in DE_SEMS)

    def r2(self):
        inc, wait = self.inc, self.wait
        self.xpart_max(self.upmaxc[:, 0:96], 1)
        self.xpart_max(self.gatemaxc[:, 0:96], 2)
        self.all_reduce_max()
        self.bcast_scale(1, 2, 3)
        self.bcast_scale(2, 4, 5)

        @self.on("vector")
        def _(v):
            wait(v, "mv_")
            v.tensor_scalar(out=self.scales[:, 10:11], in0=self.scales[:, 4:5],
                            scalar1=-CQ, scalar2=None, op0=ALU.mult)
            v.tensor_scalar(out=self.scales[:, 11:12], in0=self.scales[:, 4:5],
                            scalar1=1.0 / 65536.0, scalar2=None, op0=ALU.mult)
            v.tensor_scalar(out=self.scales[:, 16:17], in0=self.scales[:, 2:3],
                            scalar1=-CQ, scalar2=None, op0=ALU.mult)
            v.drain()
            self.r2_scales = inc(v, "mv_")

    def p6(self):
        inc, dinc, wait = self.inc, self.dinc, self.wait
        gi_mark = {}
        sig_mark = {}
        g2_mark = {}
        done = {}
        NQ = IC * 4

        def load(ic):
            slot = ic % 2
            das = "da0" if slot == 0 else "da1"

            @self.on("sync")
            def _(s, ic=ic, slot=slot, das=das):
                if ic == 0:
                    wait(s, "mv_", self.r2_scales)
                    wait(s, "mv_", self.p65_done_mv)
                    for dd, vv in zip(DE_SEMS, self.p5_done_de):
                        wait(s, dd, vv)
                if ic >= 2:
                    wait(s, "mv_", done[ic - 2])
                dinc(s.dma_start(out=self.gbuf[:, slot, :],
                                 in_=self.gate_sp[ic, :, :]), das)

        def gi_stage(Q):
            ic, q = Q // 4, Q % 4
            slot = ic % 2
            das = "da0" if slot == 0 else "da1"
            Qp = Q % 2
            o0 = q * 1024

            @self.on("vector")
            def _(v, ic=ic, slot=slot, das=das, o0=o0, Q=Q, Qp=Qp):
                if o0 == 0:
                    wait(v, das)
                if Q >= 2:
                    wait(v, "ms", sig_mark[Q - 2])
                v.tensor_scalar(out=self.iscr[:, Qp * 1024:Qp * 1024 + 1024],
                                in0=self.gbuf[:, slot, o0:o0 + 1024],
                                scalar1=self.scales[:, 5:6],
                                scalar2=self.consts[:, 0:1],
                                op0=ALU.mult, op1=ALU.add)
                gi_mark[Q] = inc(v, "mv_")

        def sig_stage(Q):
            Qp = Q % 2

            @self.on("scalar")
            def _(s, Q=Q, Qp=Qp):
                wait(s, "mv_", gi_mark[Q])
                if Q >= 2:
                    wait(s, "mv_", g2_mark[Q - 2])
                s.activation(out=self.sigb[:, Qp, :],
                             in_=self.iscr[:, Qp * 1024:Qp * 1024 + 1024],
                             func=AF.Sigmoid, scale=self.scales[:, 4:5],
                             bias=self.scales[:, 10:11])
                s.drain()
                s.activation(out=self.svb[:, Qp, :].bitcast(I32),
                             in_=self.sigb[:, Qp, :],
                             func=AF.Identity, scale=65536.0,
                             bias=self.consts[:, 1:2])
                sig_mark[Q] = inc(s, "ms")

        def tail_stage(Q):
            ic, q = Q // 4, Q % 4
            Qp = Q % 2
            o0 = q * 1024

            @self.on("vector")
            def _(v, ic=ic, o0=o0, Q=Q, Qp=Qp):
                wait(v, "ms", sig_mark[Q])
                Ep = Q % 4
                le = self.last_e[Ep]
                wait(v, le[0], le[1])
                v.tensor_scalar(out=self.sigb[:, Qp, :],
                                in0=self.svb[:, Qp, :].bitcast(I32),
                                scalar1=65535.0, scalar2=self.scales[:, 11:12],
                                op0=ALU.min, op1=ALU.mult)
                v.drain()
                v.scalar_tensor_tensor(out=self.estage[:, Ep, 0:1024],
                                       in0=self.iscr[:, Qp * 1024:Qp * 1024 + 1024],
                                       scalar=CQ, in1=self.sigb[:, Qp, :],
                                       op0=ALU.subtract, op1=ALU.mult)
                v.drain()
                v.tensor_reduce(out=self.g2maxc[:, Q:Q + 1],
                                in_=self.estage[:, Ep, 0:1024], axis=AX,
                                op=ALU.max, apply_absolute_value=True)
                g2_mark[Q] = done[ic] = inc(v, "mv_")

        def dma_stage(Q):
            ic, q = Q // 4, Q % 4
            Qp = Q % 2
            o0 = q * 1024

            @self.on("sync")
            def _(s, ic=ic, o0=o0, Q=Q, Qp=Qp):
                wait(s, "mv_", g2_mark[Q])
                Ep = Q % 4
                des = DE_SEMS[Ep]
                self.last_e[Ep] = (des, dinc(s.dma_start(
                    out=self.g2_sp[ic, :, o0:o0 + 1024],
                    in_=self.estage[:, Ep, 0:1024]), des))

        for Q in range(NQ):
            if Q % 4 == 0:
                load(Q // 4)
            gi_stage(Q)
            sig_stage(Q)
            if Q >= 1:
                tail_stage(Q - 1)
            if Q >= 2:
                dma_stage(Q - 2)
        tail_stage(NQ - 1)
        dma_stage(NQ - 2)
        dma_stage(NQ - 1)

        self.p6_done = self.CNT["mv_"]
        self.p6_de = tuple(self.CNT[d] for d in DE_SEMS)

    def r3(self):
        self.xpart_max(self.g2maxc[:, 0:96], 3)
        self.all_reduce_max()
        self.bcast_scale(3, 6, 7)
        self.r3_scales = self.CNT["mv_"]

    def p65_init(self):
        self._a65 = {"blk_done": {}, "ev_prev": {}, "mt65": {}, "dma": {}}

    def p65_front(self, tc):
        inc, dinc, wait = self.inc, self.dinc, self.wait
        blk_done = self._a65["blk_done"]
        mt65 = self._a65["mt65"]
        slot = tc % 2
        das = "da0" if slot == 0 else "da1"
        dbs = "db0" if slot == 0 else "db1"
        t0 = tc * 128

        @self.on("sync")
        def _(s, tc=tc, slot=slot, das=das, dbs=dbs, t0=t0):
            if tc == 0:
                wait(s, "mv_", self.sprev_ready)
                for dd, vv in zip(DE_SEMS, self.p2_done_marks[1]):
                    wait(s, dd, vv)
                wait(s, "dk0", self.p2_done_marks[2])
                wait(s, "dk1", self.p2_done_marks[3])
            if tc >= 2:
                wait(s, "mv_", blk_done[tc - 2])
            dinc(s.dma_start(out=self.att_u[:, slot, :],
                             in_=self.u_att[tc, :, :, :].rearrange("p c n -> p (c n)")),
                 das)
            dinc(s.dma_start(out=self.att_r[:, slot, :],
                             in_=self.r_att[tc, :, :, :].rearrange("p c n -> p (c n)")),
                 dbs)

        @self.on("vector")
        def _(v, tc=tc, slot=slot, das=das, dbs=dbs):
            wait(v, das)
            wait(v, dbs)
            for c in range(16):
                init = self.S_prev[:, c:c + 1] if tc == 0 else self.carries[:, c:c + 1]
                v.tensor_tensor_scan(
                    out=self.att_y[:, c * 128:(c + 1) * 128],
                    data0=self.lam_t[:, c % 8:c % 8 + 1].broadcast_to((128, 128)),
                    data1=self.att_u[:, slot, c * 128:(c + 1) * 128],
                    initial=init, op0=ALU.mult, op1=ALU.add)
            v.drain()
            v.tensor_copy(out=self.carries[:, :],
                          in_=self.att_y[:, :].rearrange("p (c n) -> p c n", n=128)
                          [:, :, 127:128].rearrange("p c n -> p (c n)"))
            v.drain()
            for c in range(8):
                v.scalar_tensor_tensor(
                    out=self.att_n[:, c * 128:(c + 1) * 128],
                    in0=self.att_u[:, slot, c * 128:(c + 1) * 128],
                    scalar=self.gamma_t[:, c:c + 1],
                    in1=self.att_y[:, c * 128:(c + 1) * 128],
                    op0=ALU.mult, op1=ALU.add)
            for c in range(8):
                v.scalar_tensor_tensor(
                    out=self.att_d[:, c * 128:(c + 1) * 128],
                    in0=self.att_u[:, slot, (8 + c) * 128:(9 + c) * 128],
                    scalar=self.gamma_t[:, c:c + 1],
                    in1=self.att_y[:, (8 + c) * 128:(9 + c) * 128],
                    op0=ALU.mult, op1=ALU.add)
            v.drain()
            v.tensor_scalar(out=self.att_d[:, :], in0=self.att_d[:, :], scalar1=1e-6,
                            scalar2=None, op0=ALU.add)
            v.drain()
            v.reciprocal(out=self.att_y[:, 0:1024], in_=self.att_d[:, :])
            v.drain()
            v.tensor_tensor(out=self.att_n[:, :], in0=self.att_n[:, :],
                            in1=self.att_y[:, 0:1024], op=ALU.mult)
            v.drain()
            if tc >= 2:
                wait(v, "mt", mt65[tc - 2])
            v.tensor_tensor(out=self.ratio[:, slot, :, :].rearrange("p c n -> p (c n)"),
                            in0=self.att_n[:, :], in1=self.att_r[:, slot, :],
                            op=ALU.mult)
            blk_done[tc] = inc(v, "mv_")

    def p65_back(self, tc):
        inc, dinc, wait = self.inc, self.dinc, self.wait
        blk_done = self._a65["blk_done"]
        ev_prev = self._a65["ev_prev"]
        mt65 = self._a65["mt65"]
        adma = self._a65["dma"]
        slot = tc % 2

        banks = self.ralloc(2)
        bmarks = [self.rmark[b] for b in banks]

        @self.on("tensor")
        def _(t, tc=tc, banks=banks, bmarks=bmarks):
            wait(t, "mv_", blk_done[tc])
            if tc == 0:
                wait(t, "dg", self.dg_wo)
            for m in bmarks:
                if m is not None:
                    wait(t, m[0], m[1])
            for c in range(8):
                t.matmul(self.rbank[banks[0]], self.ratio[:, slot, c, :],
                         self.wo_sb[:, c * 1024:c * 1024 + 512],
                         start=(c == 0), stop=False)
                t.matmul(self.rbank[banks[1]], self.ratio[:, slot, c, :],
                         self.wo_sb[:, c * 1024 + 512:(c + 1) * 1024],
                         start=(c == 0), stop=False)
            t.matmul(self.rbank[banks[0]], self.ones_b[0:1, 0:128],
                     self.bo_row[0:1, 0:512], start=False, stop=True)
            li = t.matmul(self.rbank[banks[1]], self.ones_b[0:1, 0:128],
                          self.bo_row[0:1, 512:1024], start=False, stop=True)
            mt65[tc] = inc(li, "mt")

        @self.on("scalar")
        def _(s, tc=tc, slot=slot, banks=banks):
            wait(s, "mt", mt65[tc])
            if tc >= 2:
                wait(s, adma[tc - 2][0], adma[tc - 2][1])
            s.activation(out=self.xnstage[:, slot, 0:512],
                         in_=self.rbank[banks[0]], func=AF.Identity)
            s.activation(out=self.xnstage[:, slot, 512:1024],
                         in_=self.rbank[banks[1]], func=AF.Identity)
            m = ("ms", inc(s, "ms"))
            self.rmark[banks[0]] = m
            self.rmark[banks[1]] = m
            ev_prev[tc] = m[1]

        @self.on("sync")
        def _(s, tc=tc, slot=slot):
            wait(s, "ms", ev_prev[tc])
            dys = "dy0" if slot == 0 else "dy1"
            adma[tc] = (dys, dinc(s.dma_start(
                out=self.att_sp[tc, :, :], in_=self.xnstage[:, slot, :]), dys))

    def p65_finish(self):
        self.p65_done_ms = self.CNT["ms"]
        self.p65_done_mv = self.CNT["mv_"]
        self.p65_att_dy = (self.CNT["dy0"], self.CNT["dy1"])

    def p675(self):
        inc, dinc, wait = self.inc, self.dinc, self.wait
        done = {}
        ci_mark = {}
        pv_mark = {}

        def spill(Q):
            ic, q = Q // 4, Q % 4

            @self.on("sync")
            def _(s, ic=ic, q=q, Q=Q):
                wait(s, "mv_", pv_mark[Q])
                Ep = Q % 4
                des = DE_SEMS[Ep]
                self.last_e[Ep] = (des, dinc(s.dma_start(
                    out=self.o_att[q * 8:(q + 1) * 8, :, ic, :]
                    .rearrange("tc p n -> p tc n"),
                    in_=self.estage[:, Ep, 0:1024]
                    .rearrange("p (tc n) -> p tc n", n=128)), des))

        for ic in range(IC):
            slot = ic % 2
            das = "da0" if slot == 0 else "da1"
            dbs = "db0" if slot == 0 else "db1"

            @self.on("sync")
            def _(s, ic=ic, slot=slot, das=das, dbs=dbs):
                if ic == 0:
                    wait(s, "mv_", self.r3_scales)
                    for dd, vv in zip(DE_SEMS, self.p6_de):
                        wait(s, dd, vv)
                    wait(s, "dy0", self.p65_att_dy[0])
                    wait(s, "dy1", self.p65_att_dy[1])
                if ic >= 2:
                    wait(s, "mv_", done[ic - 2])
                dinc(s.dma_start(out=self.g2in[:, slot, :], in_=self.g2_sp[ic, :, :]), das)
                dinc(s.dma_start(out=self.upin[:, slot, :], in_=self.up_sp[ic, :, :]), dbs)

            for q in range(4):
                Q = ic * 4 + q
                Qp = Q % 2
                o0 = q * 1024

                @self.on("scalar")
                def _(s, ic=ic, slot=slot, das=das, dbs=dbs, o0=o0, Q=Q, Qp=Qp):
                    if o0 == 0:
                        wait(s, das)
                        wait(s, dbs)
                    if Q >= 2:
                        wait(s, "mv_", pv_mark[Q - 2])
                    s.activation(out=self.iscr[:, Qp * 2048 + 1024:Qp * 2048 + 2048],
                                 in_=self.upin[:, slot, o0:o0 + 1024],
                                 func=AF.Identity, scale=self.scales[:, 3:4],
                                 bias=self.consts[:, 0:1])
                    s.activation(out=self.iscr[:, Qp * 2048:Qp * 2048 + 1024],
                                 in_=self.g2in[:, slot, o0:o0 + 1024],
                                 func=AF.Identity, scale=self.scales[:, 7:8],
                                 bias=self.consts[:, 0:1])
                    if Q % 2 == 0:
                        s.drain()
                        s.activation(out=self.upq_t[:, Qp * 1024:Qp * 1024 + 1024],
                                     in_=self.iscr[:, Qp * 2048 + 1024:Qp * 2048 + 2048],
                                     func=AF.Identity, scale=self.scales[:, 2:3],
                                     bias=self.scales[:, 16:17])
                    ci_mark[Q] = inc(s, "ms")

                @self.on("vector")
                def _(v, ic=ic, o0=o0, Q=Q, Qp=Qp):
                    wait(v, "ms", ci_mark[Q])
                    Ep = Q % 4
                    le = self.last_e[Ep]
                    wait(v, le[0], le[1])
                    if Q % 2 == 1:
                        v.tensor_scalar(out=self.upq_t[:, Qp * 1024:Qp * 1024 + 1024],
                                        in0=self.iscr[:, Qp * 2048 + 1024:Qp * 2048 + 2048],
                                        scalar1=CQ, scalar2=self.scales[:, 2:3],
                                        op0=ALU.subtract, op1=ALU.mult)
                        v.drain()
                    v.scalar_tensor_tensor(out=self.estage[:, Ep, 0:1024],
                                           in0=self.iscr[:, Qp * 2048:Qp * 2048 + 1024],
                                           scalar=CQ,
                                           in1=self.upq_t[:, Qp * 1024:Qp * 1024 + 1024],
                                           op0=ALU.subtract, op1=ALU.mult)
                    v.drain()
                    v.tensor_reduce(out=self.pmaxc[:, Q:Q + 1],
                                    in_=self.estage[:, Ep, 0:1024], axis=AX,
                                    op=ALU.max, apply_absolute_value=True)
                    pv_mark[Q] = done[ic] = inc(v, "mv_")

                if Q >= 2:
                    spill(Q - 2)
        spill(IC * 4 - 2)
        spill(IC * 4 - 1)

        self.p675_done = self.CNT["mv_"]
        self.p675_de = tuple(self.CNT[d] for d in DE_SEMS)

    def r4(self):
        inc, wait = self.inc, self.wait
        self.xpart_max(self.pmaxc[:, 0:96], 4)
        self.all_reduce_max()
        self.bcast_scale(4, 13, 14, raw_col=12)

        @self.on("vector")
        def _(v):
            wait(v, "mv_")
            v.tensor_tensor(out=self.scales[:, 15:16], in0=self.scales[:, 6:7],
                            in1=self.scales[:, 12:13], op=ALU.mult)
            v.drain()
            v.tensor_scalar(out=self.scales[:, 8:9], in0=self.scales[:, 15:16],
                            scalar1=1.0 / QMAX, scalar2=1e-12, op0=ALU.mult, op1=ALU.max)
            v.drain()
            v.reciprocal(out=self.scales[:, 15:16], in_=self.scales[:, 8:9])
            v.drain()
            v.tensor_tensor(out=self.scales[:, 9:10], in0=self.scales[:, 6:7],
                            in1=self.scales[:, 15:16], op=ALU.mult)
            v.drain()
            self.r4_scales = inc(v, "mv_")

    def p7(self):
        inc, dinc, wait = self.inc, self.dinc, self.wait
        oq_mark = {}
        mm_mark = {}
        ev_mark = {}
        dma_mark = {}
        att_mark = {}
        pa2 = self.bigps[:, 0:1024]

        def loads(tc):
            slot = tc % 2
            das = "da0" if slot == 0 else "da1"
            dbs = "db0" if slot == 0 else "db1"
            dxs = "dx0" if slot == 0 else "dx1"
            t0 = tc * 128

            @self.on("sync")
            def _(s, tc=tc, slot=slot, das=das, dbs=dbs, dxs=dxs, t0=t0):
                if tc == 0:
                    wait(s, "mv_", self.p675_done)
                    wait(s, "mv_", self.r4_scales)
                    for dd, vv in zip(DE_SEMS, self.p675_de):
                        wait(s, dd, vv)
                    wait(s, "dy0", self.p65_att_dy[0])
                    wait(s, "dy1", self.p65_att_dy[1])
                if tc >= 2:
                    wait(s, "mv_", ev_mark[tc - 2])
                dinc(s.dma_start(out=self.ostage[:, slot, :],
                                 in_=self.o_att[tc, :, :, :]
                                 .rearrange("p c n -> p (c n)")), das)
                dinc(s.dma_start(out=self.xld[:, slot, :], in_=self.x_d[t0:t0 + 128, :]),
                     dxs)
                dinc(s.dma_start(out=self.attnA(tc), in_=self.att_sp[tc, :, :]), dbs)

        def oquant(tc):
            slot = tc % 2
            das = "da0" if slot == 0 else "da1"
            for hf in range(2):
                h0 = hf * 1536

                @self.on("scalar")
                def _(s, tc=tc, slot=slot, das=das, h0=h0, hf=hf):
                    if hf == 0:
                        wait(s, das)
                        if tc >= 2:
                            wait(s, "mt", mm_mark[tc - 2])
                    if tc >= 1 or hf >= 1:
                        wait(s, "mv_")
                    s.activation(out=self.iscr[:, 0:1536],
                                 in_=self.ostage[:, slot, h0:h0 + 1536],
                                 func=AF.Identity, scale=self.scales[:, 9:10],
                                 bias=self.consts[:, 0:1])
                    s.drain()
                    inc(s, "ms")

                @self.on("vector")
                def _(v, tc=tc, h0=h0):
                    wait(v, "ms")
                    oq = self.oq if tc % 2 == 0 else self.oq2
                    v.tensor_scalar(out=oq[:, h0:h0 + 1536], in0=self.iscr[:, 0:1536],
                                    scalar1=CQ, scalar2=self.scales[:, 8:9],
                                    op0=ALU.subtract, op1=ALU.mult)
                    oq_mark[tc] = inc(v, "mv_")

        def mms(tc):
            @self.on("tensor")
            def _(t, tc=tc):
                wait(t, "mv_", oq_mark[tc])
                if tc == 0:
                    wait(t, "ms", self.p65_done_ms)
                if tc >= 2:
                    wait(t, "mv_", ev_mark[tc - 2])
                oq = self.oq if tc % 2 == 0 else self.oq2
                ps = self.pa if tc % 2 == 0 else pa2
                for ic in range(IC):
                    t.matmul(ps[:, 0:512], oq[:, ic * 128:(ic + 1) * 128],
                             self.wdT[:, ic * 1024:ic * 1024 + 512],
                             start=(ic == 0), stop=(ic == IC - 1))
                    li = t.matmul(ps[:, 512:1024], oq[:, ic * 128:(ic + 1) * 128],
                                  self.wdT[:, ic * 1024 + 512:(ic + 1) * 1024],
                                  start=(ic == 0), stop=(ic == IC - 1))
                mm_mark[tc] = inc(li, "mt")

        def evac(tc):
            slot = tc % 2
            dxs = "dx0" if slot == 0 else "dx1"
            dbs = "db0" if slot == 0 else "db1"

            @self.on("vector")
            def _(v, tc=tc, slot=slot, dxs=dxs, dbs=dbs):
                wait(v, "mt", mm_mark[tc])
                wait(v, dxs)
                wait(v, dbs)
                if tc >= 2:
                    wait(v, dma_mark[tc - 2][0], dma_mark[tc - 2][1])
                ps = self.pa if tc % 2 == 0 else pa2
                v.scalar_tensor_tensor(out=self.outst[:, slot, :], in0=ps[:, :],
                                       scalar=1.0, in1=self.attnA(tc),
                                       op0=ALU.mult, op1=ALU.add)
                v.drain()
                v.tensor_tensor(out=self.outst[:, slot, :], in0=self.outst[:, slot, :],
                                in1=self.xld[:, slot, :], op=ALU.add)
                ev_mark[tc] = inc(v, "mv_")

        def store(tc):
            slot = tc % 2
            t0 = tc * 128

            @self.on("sync")
            def _(s, tc=tc, slot=slot, t0=t0):
                wait(s, "mv_", ev_mark[tc])
                dys = "dy0" if slot == 0 else "dy1"
                dma_mark[tc] = (dys, dinc(s.dma_start(
                    out=self.y_d[t0:t0 + 128, :], in_=self.outst[:, slot, :]), dys))

        for tc in range(NTT):
            loads(tc)
            oquant(tc)
            mms(tc)
            if tc >= 1:
                evac(tc - 1)
                store(tc - 1)
        evac(NTT - 1)
        store(NTT - 1)

    def fin(self):
        @self.on("sync")
        def _(s):
            self.wait(s, "dy0")
            self.wait(s, "dy1")


def build_nc():
    b = Builder()
    nc = b.build()
    b.ctx.close()
    return nc


_NC_CACHE = None


def kernel(L0_attn_inputs, L0_attn_W_rkv, L0_attn_b_rkv, L0_attn_W_o, L0_attn_b_o,
           L0_attn_nu_log, L0_attn_gamma_log, L0_ffn_w_gate, L0_ffn_w_up, L0_ffn_w_down):
    global _NC_CACHE
    x = np.ascontiguousarray(np.asarray(L0_attn_inputs, dtype=np.float32))
    wrkv = np.ascontiguousarray(np.asarray(L0_attn_W_rkv, dtype=np.float32))
    brkv = np.ascontiguousarray(np.asarray(L0_attn_b_rkv, dtype=np.float32))
    wo = np.ascontiguousarray(np.asarray(L0_attn_W_o, dtype=np.float32))
    bo = np.ascontiguousarray(np.asarray(L0_attn_b_o, dtype=np.float32))
    nulog = np.ascontiguousarray(np.asarray(L0_attn_nu_log, dtype=np.float32))
    glog = np.ascontiguousarray(np.asarray(L0_attn_gamma_log, dtype=np.float32))
    wg = np.ascontiguousarray(np.asarray(L0_ffn_w_gate, dtype=np.float32))
    wu = np.ascontiguousarray(np.asarray(L0_ffn_w_up, dtype=np.float32))
    wd = np.ascontiguousarray(np.asarray(L0_ffn_w_down, dtype=np.float32))

    if _NC_CACHE is None:
        _NC_CACHE = build_nc()
    nc = _NC_CACHE

    in_maps = []
    for c in range(8):
        bb, hh = c // 2, c % 2
        sel = np.zeros((128, 16), np.float32)
        if hh == 1:
            for g in range(16):
                sel[16 * (c - 1) + g, g] = 1.0
        in_maps.append({
            "x": np.ascontiguousarray(x[bb, hh * T:(hh + 1) * T, :]),
            "wrkv": wrkv, "brkv": brkv, "wo": wo, "bo": bo,
            "nulog": nulog, "glog": glog, "wg": wg, "wu": wu, "wd": wd,
            "sel": sel,
        })
    res = run_bass_kernel_spmd(nc, in_maps, core_ids=list(range(8)))
    out = np.empty((B, L, H), dtype=np.float32)
    for c in range(8):
        bb, hh = c // 2, c % 2
        out[bb, hh * T:(hh + 1) * T, :] = res.results[c]["y"]
    return out


# version 20
# speedup vs baseline: 1.0195x; 1.0195x over previous
import numpy as np
from contextlib import ExitStack
import concourse.bass as bass
import concourse.mybir as mybir
from concourse.bass_utils import run_bass_kernel_spmd

F32 = mybir.dt.float32
BF16 = mybir.dt.bfloat16
I32 = mybir.dt.int32
AF = mybir.ActivationFunctionType
ALU = mybir.AluOpType
AX = mybir.AxisListType.X

B, L, H, U3 = 4, 8192, 1024, 3072
T = 4096
HC = 8
UC = 24
IC = 24
NT = 8
NTT = 32
CQ = 65536.0
DE_SEMS = ["de0", "de1", "de2", "de3"]
QMAX = 32767.0


class _RecInstr:
    __slots__ = ("_rq", "_rid")

    def __init__(self, rq, rid):
        self._rq = rq
        self._rid = rid

    def __getattr__(self, name):
        def call(*args, **kwargs):
            self._rq.append(("chain", self._rid, name, args, kwargs))
            return self
        return call


class _RecEngine:
    __slots__ = ("_name", "_q", "_n")

    def __init__(self, name):
        self._name = name
        self._q = []
        self._n = 0

    def __getattr__(self, name):
        if name == "then_inc":
            raise AttributeError(name)

        def call(*args, **kwargs):
            rid = self._n
            self._n += 1
            self._q.append(("call", rid, name, args, kwargs))
            return _RecInstr(self._q, rid)
        return call


class Builder:
    def __init__(self):
        self.nc = bass.Bass("TRN2", target_bir_lowering=False, debug=False)
        self.ctx = ExitStack()
        self.CNT = {}
        self.SEM = {}
        self._rec = {}

    def sem(self, name):
        self.SEM[name] = self.ctx.enter_context(self.nc.semaphore(name))
        self.CNT[name] = 0

    def inc(self, bi, name, amt=1):
        if not hasattr(bi, "then_inc"):
            bi = bi.drain()
        bi.then_inc(self.SEM[name], amt)
        self.CNT[name] += amt
        return self.CNT[name]

    def dinc(self, bi, name):
        return self.inc(bi, name, 16)

    def wait(self, eng, name, val=None):
        eng.wait_ge(self.SEM[name], self.CNT[name] if val is None else val)

    def sb(self, name, shape, dt):
        return self.ctx.enter_context(self.nc.sbuf_tensor(name, shape, dt))

    def on(self, engine_name):
        rec = self._rec.setdefault(engine_name, _RecEngine(engine_name))

        def deco(fn, *a, **k):
            fn(rec, *a, **k)
        return deco

    def flush_engines(self):
        for name, rec in self._rec.items():
            q = rec._q

            def replay(eng, q=q):
                objs = {}
                for kind, rid, meth, args, kwargs in q:
                    if kind == "call":
                        objs[rid] = getattr(eng, meth)(*args, **kwargs)
                    else:
                        objs[rid] = getattr(objs[rid], meth)(*args, **kwargs)
            getattr(self.block, name)(replay)
        self._rec = {}

    def build(self):
        nc = self.nc
        self.x_d = nc.dram_tensor("x", [T, H], F32, kind="ExternalInput").ap()
        self.wrkv_d = nc.dram_tensor("wrkv", [H, U3], F32, kind="ExternalInput").ap()
        self.brkv_d = nc.dram_tensor("brkv", [U3], F32, kind="ExternalInput").ap()
        self.wo_d = nc.dram_tensor("wo", [H, H], F32, kind="ExternalInput").ap()
        self.bo_d = nc.dram_tensor("bo", [H], F32, kind="ExternalInput").ap()
        self.nulog_d = nc.dram_tensor("nulog", [H], F32, kind="ExternalInput").ap()
        self.glog_d = nc.dram_tensor("glog", [H], F32, kind="ExternalInput").ap()
        self.wg_d = nc.dram_tensor("wg", [U3, H], F32, kind="ExternalInput").ap()
        self.wu_d = nc.dram_tensor("wu", [U3, H], F32, kind="ExternalInput").ap()
        self.wd_d = nc.dram_tensor("wd", [H, U3], F32, kind="ExternalInput").ap()
        self.sel_d = nc.dram_tensor("sel", [128, 16], F32, kind="ExternalInput").ap()
        self.y_d = nc.dram_tensor("y", [T, H], F32, kind="ExternalOutput").ap()
        dr = lambda n, s, d: nc.dram_tensor(n, s, d).ap()
        self.u_sp = dr("u_sp", [16, 128, T], BF16)
        self.r_sp = dr("r_sp", [8, 128, T], BF16)
        self.up_sp = dr("up_sp", [IC, 128, T], BF16)
        self.gate_sp = dr("gate_sp", [IC, 128, T], BF16)
        self.g2_sp = dr("g2_sp", [IC, 128, T], BF16)
        self.o_sp = dr("o_sp", [IC, 128, T], BF16)
        self.wuT_sp = dr("wuT_sp", [IC, HC, 128, 128], BF16)
        self.att_sp = dr("att_sp", [NTT, 128, 1024], BF16)
        self.u_att = dr("u_att", [NTT, 128, 16, 128], BF16)
        self.r_att = dr("r_att", [NTT, 128, 8, 128], BF16)
        self.o_att = dr("o_att", [NTT, 128, IC, 128], BF16)
        self.wgT_sp = dr("wgT_sp", [IC, HC, 128, 128], BF16)
        self.cc_in = dr("cc_in", [1, 16], F32)
        self.cc_out = dr("cc_out", [1, 16], F32)
        self.ag_in = dr("ag_in", [16, 128], F32)
        self.ag_out = dr("ag_out", [128, 128], F32)

        sb = self.sb
        self.arenaA = sb("arenaA", [128, 8 * T], BF16)
        self.wdT = sb("wdT", [128, IC * 1024], BF16)
        self.wo_sb = sb("wo_sb", [128, HC * 1024], BF16)
        self.wkB = sb("wkB", [128, 8192], F32)
        self.ident = sb("ident", [128, 128], BF16)
        self.identf = sb("identf", [128, 128], F32)
        self.ones_b = sb("ones_b", [1, 128], BF16)
        self.ones_f = sb("ones_f", [1, 128], F32)
        self.bo_row = sb("bo_row", [1, 1024], BF16)
        self.consts = sb("consts", [128, 8], F32)
        self.brkv_t = sb("brkv_t", [128, 24], F32)
        self.nulog_t = sb("nulog_t", [128, 8], F32)
        self.glog_t = sb("glog_t", [128, 8], F32)
        self.lam_t = sb("lam_t", [128, 8], F32)
        self.gamma_t = sb("gamma_t", [128, 8], F32)
        self.stats = sb("stats", [128, 2, 12], F32)
        self.mvt = sb("mvt", [128, 2, 2], F32)
        self.tiny = sb("tiny", [128, 12], F32)
        self.maxc = sb("maxc", [128, 36], F32)
        self.upmaxc = sb("upmaxc", [128, 96], F32)
        self.gatemaxc = sb("gatemaxc", [128, 96], F32)
        self.g2maxc = sb("g2maxc", [128, 96], F32)
        self.pmaxc = sb("pmaxc", [128, 96], F32)
        self.mx1 = sb("mx1", [128, 1], F32)
        self.cc_sb = sb("cc_sb", [1, 16], F32)
        self.ccr_sb = sb("ccr_sb", [1, 16], F32)
        self.scales = sb("scales", [128, 20], F32)
        self.S_loc = sb("S_loc", [128, 16], F32)
        self.S_prev = sb("S_prev", [128, 16], F32)
        self.carries = sb("carries", [128, 16], F32)
        self.ag_sb = sb("ag_sb", [128, 128], F32)
        self.agT_sb = None
        self.sel_sb = sb("sel_sb", [128, 16], F32)
        self.xld = sb("xld", [128, 2, 1024], F32)
        self.xnstage = sb("xnstage", [128, 2, 1024], BF16)
        self.wstream = sb("wstream", [128, 2, HC, 128], BF16)
        self.estage = sb("estage", [128, 4, 1024], BF16)
        self.iscr = sb("iscr", [128, 4096], I32)
        self.kstage = self.wkB[:, :].bitcast(BF16).rearrange(
            "p (c t n) -> p c t n", c=8, t=4)
        self.pdqst = self.xnstage
        self.bmax = sb("bmax", [128, 1, 96], F32)
        self.binv = sb("binv", [128, 1, 96], F32)
        self.bs = sb("bs", [128, 1, 96], F32)

        wkb = self.wkB[:, :]
        self.wf = wkb[:, 0:6144].rearrange("p (s n) -> p s n", s=2)
        self.wq = wkb[:, 6144:7680].bitcast(BF16)
        self.ubuf = wkb[:, 0:4096].bitcast(BF16).rearrange("p (s n) -> p s n", s=2)
        self.ydisc = wkb[:, 4096:8192]
        self.gbuf = self.ubuf
        self.sigb = wkb[:, 4096:6144].rearrange("p (s n) -> p s n", s=2)
        self.svb = wkb[:, 6144:8192].rearrange("p (s n) -> p s n", s=2)
        self.att_u = wkb[:, 0:2048].bitcast(BF16).rearrange("p (s n) -> p s n", s=2)
        self.att_r = wkb[:, 2048:3072].bitcast(BF16).rearrange("p (s n) -> p s n", s=2)
        self.att_y = wkb[:, 3072:5120]
        self.att_n = wkb[:, 5120:6144]
        self.att_d = wkb[:, 6144:7168]
        self.att_c = wkb[:, 7168:8192]
        self.ratio = self.att_c[:, :].bitcast(BF16).rearrange("p (s c n) -> p s c n", s=2, c=8)
        self.g2in = wkb[:, 0:4096].bitcast(BF16).rearrange("p (s n) -> p s n", s=2)
        self.upin = wkb[:, 4096:8192].bitcast(BF16).rearrange("p (s n) -> p s n", s=2)
        self.ostage = wkb[:, 0:3072].bitcast(BF16).rearrange("p (s n) -> p s n", s=2)
        self.oq = wkb[:, 3072:4608].bitcast(BF16)
        self.oq2 = wkb[:, 6656:8192].bitcast(BF16)
        self.outst = wkb[:, 4608:6656].rearrange("p (s n) -> p s n", s=2)
        self.upq_t = self.xnstage[:, :, :].rearrange("p s n -> p (s n)")

        self.bigps = self.ctx.enter_context(nc.psum_tensor("bigps", [128, 3584], F32))
        self.pt = self.ctx.enter_context(nc.psum_tensor("pt", [128, 1024], BF16))
        self.pa = self.bigps[:, 2048:3072]
        self.fdq = self.bigps[:, 2048:3072]
        self.px = self.bigps[:, 3072:3584]

        for n in ["dx0", "dx1", "dw0", "dw1", "dwg0", "dwg1", "da0", "da1", "db0", "db1",
                  "dsp", "de0", "de1", "de2", "de3", "dci", "dq0", "dq1", "dy0", "dy1", "dy", "dg", "dcc", "dk", "dk0", "dk1", "mt", "mv_", "ms", "mg", "mc"]:
            self.sem(n)

        self.block = self.ctx.enter_context(nc.Block())

        self.p0()
        self.p1()
        self.r1_start()
        self.dq_init()
        self.p2()
        self.pdq_down()
        self.r1_finish()
        self.p3()
        self.p4()
        self.p5()
        self.r2()
        self.p6()
        self.r3()
        self.p675()
        self.r4()
        self.p7()
        self.fin()
        self.flush_engines()
        return nc

    def bank(self, i):
        return self.bigps[:, i * 512:(i + 1) * 512]

    def ring_init(self):
        self.rbank = [self.bigps[:, i * 512:(i + 1) * 512] for i in range(7)]
        self.rbank.append(self.pt[:, :].bitcast(F32))
        self.rnext = 0
        self.rmark = {i: None for i in range(8)}

    def ralloc(self, n):
        idx = [(self.rnext + j) % 8 for j in range(n)]
        self.rnext = (self.rnext + n) % 8
        return idx

    def xnA(self, hc, t0, n):
        return self.arenaA[:, hc * T + t0: hc * T + t0 + n]

    def attnA(self, tc):
        return self.arenaA[:, tc * 1024:(tc + 1) * 1024]

    def p0(self):
        inc, dinc, wait = self.inc, self.dinc, self.wait

        @self.on("gpsimd")
        def _(g):
            g.iota(out=self.iscr[:, 0:128], pattern=[[1, 128]], base=0, channel_multiplier=-1)
            g.memset(self.consts[:, 0:1], 65536.5)
            g.memset(self.consts[:, 1:2], 0.5)
            g.memset(self.consts[:, 2:3], 8.0)
            g.memset(self.ones_b[:, :], 1.0)
            g.memset(self.cc_sb[:, :], 0.0)
            g.memset(self.ones_f[:, :], 1.0)
            g.drain()
            g.tensor_scalar(out=self.ident[:, :], in0=self.iscr[:, 0:128], scalar1=0.0,
                            scalar2=None, op0=ALU.is_equal)
            g.tensor_scalar(out=self.identf[:, :], in0=self.iscr[:, 0:128], scalar1=0.0,
                            scalar2=None, op0=ALU.is_equal)
            g.drain()
            self.mg_init = inc(g, "mg")
            dinc(g.dma_start(out=self.wo_sb.rearrange("p (c n) -> p c n", n=1024),
                        in_=self.wo_d.rearrange("(c p) n -> p c n", p=128)), "dg")
            self.dg_wo = dinc(g.dma_start(out=self.bo_row[:, :], in_=self.bo_d[None, :]), "dg")

        @self.on("sync")
        def _(s):
            dinc(s.dma_start(out=self.brkv_t[:, :], in_=self.brkv_d.rearrange("(c p) -> p c", p=128), allow_slow_non_contiguous=True), "da0")
            dinc(s.dma_start(out=self.nulog_t[:, :], in_=self.nulog_d.rearrange("(c p) -> p c", p=128), allow_slow_non_contiguous=True), "da0")
            dinc(s.dma_start(out=self.glog_t[:, :], in_=self.glog_d.rearrange("(c p) -> p c", p=128), allow_slow_non_contiguous=True), "da0")
            self.da_p0 = dinc(s.dma_start(out=self.sel_sb[:, :], in_=self.sel_d[:, :]), "da0")

        @self.on("scalar")
        def _(s):
            wait(s, "da0", self.da_p0)
            s.activation(out=self.tiny[:, 0:8], in_=self.nulog_t[:, :], func=AF.Exp)
            s.drain()
            s.activation(out=self.lam_t[:, :], in_=self.tiny[:, 0:8], func=AF.Exp, scale=-1.0)
            s.drain()
            inc(s, "ms")

        @self.on("vector")
        def _(v):
            wait(v, "da0", self.da_p0)
            v.tensor_tensor(out=self.glog_t[:, :], in0=self.nulog_t[:, :],
                            in1=self.glog_t[:, :], op=ALU.add)
            v.drain()
            inc(v, "mv_")

        @self.on("scalar")
        def _(s):
            wait(s, "mv_")
            s.activation(out=self.gamma_t[:, :], in_=self.glog_t[:, :], func=AF.Exp)
            s.drain()
            inc(s, "ms")

        @self.on("vector")
        def _(v):
            wait(v, "ms")
            v.tensor_scalar(out=self.gamma_t[:, :], in0=self.gamma_t[:, :], scalar1=-1.0,
                            scalar2=None, op0=ALU.add)
            v.drain()
            self.mv_gamma = inc(v, "mv_")

    def p1(self):
        inc, dinc, wait = self.inc, self.dinc, self.wait
        st_mark = {}
        ln_mark = {}
        xn_mark = {}
        tp_mark = {}
        ev_mark = {}
        xnA_v = self.arenaA[:, :].rearrange("p (c t) -> p c t", t=T)

        def load(tt):
            slot = tt % 2
            dxs = "dx0" if slot == 0 else "dx1"

            @self.on("sync")
            def _(s, tt=tt, slot=slot, dxs=dxs):
                if tt >= 2:
                    wait(s, "mv_", xn_mark[tt - 2])
                dinc(s.dma_start(out=self.xld[:, slot, :],
                                 in_=self.x_d[tt * 128:(tt + 1) * 128, :]), dxs)

        def stats(tt):
            slot = tt % 2
            dxs = "dx0" if slot == 0 else "dx1"

            @self.on("vector")
            def _(v, tt=tt, slot=slot, dxs=dxs):
                wait(v, dxs)
                if tt >= 2:
                    wait(v, "ms", ln_mark[tt - 2])
                v.bn_stats(out=self.stats[:, slot, 0:6], in_=self.xld[:, slot, 0:512])
                v.bn_stats(out=self.stats[:, slot, 6:12], in_=self.xld[:, slot, 512:1024])
                v.drain()
                v.bn_aggr(out=self.mvt[:, slot, :], in_=self.stats[:, slot, :])
                v.drain()
                v.tensor_scalar(out=self.tiny[:, slot:slot + 1],
                                in0=self.mvt[:, slot, 1:2], scalar1=1e-5,
                                scalar2=None, op0=ALU.add)
                st_mark[tt] = inc(v, "mv_")

        def lnexp(tt):
            slot = tt % 2

            @self.on("scalar")
            def _(s, tt=tt, slot=slot):
                wait(s, "mv_", st_mark[tt])
                if tt >= 2:
                    wait(s, "mv_", xn_mark[tt - 2])
                s.activation(out=self.tiny[:, 2 + slot:3 + slot],
                             in_=self.tiny[:, slot:slot + 1], func=AF.Ln)
                s.drain()
                s.activation(out=self.tiny[:, 4 + slot:5 + slot],
                             in_=self.tiny[:, 2 + slot:3 + slot], func=AF.Exp,
                             scale=-0.5)
                s.drain()
                s.activation(out=self.tiny[:, 6 + slot:7 + slot],
                             in_=self.tiny[:, 4 + slot:5 + slot], func=AF.Identity,
                             scale=-1.0)
                s.drain()
                s.activation(out=self.tiny[:, 8 + slot:9 + slot],
                             in_=self.mvt[:, slot, 0:1], func=AF.Identity,
                             scale=self.tiny[:, 6 + slot:7 + slot])
                s.drain()
                s.activation(out=self.xnstage[:, slot, :], in_=self.xld[:, slot, :],
                             func=AF.Identity, scale=self.tiny[:, 4 + slot:5 + slot],
                             bias=self.tiny[:, 8 + slot:9 + slot])
                ln_mark[tt] = inc(s, "ms")

        def xn(tt):
            slot = tt % 2

            @self.on("vector")
            def _(v, tt=tt, slot=slot):
                wait(v, "ms", ln_mark[tt])
                v.tensor_reduce(out=self.maxc[:, tt:tt + 1], in_=self.xnstage[:, slot, :],
                                axis=AX, op=ALU.max, apply_absolute_value=True)
                xn_mark[tt] = inc(v, "mv_")

        def transpose(tt):
            slot = tt % 2

            @self.on("tensor")
            def _(t, tt=tt, slot=slot):
                if tt == 0:
                    wait(t, "mg", self.mg_init)
                wait(t, "mv_", xn_mark[tt])
                if tt >= 1:
                    wait(t, "ms", ev_mark[tt - 1])
                for hc in range(HC):
                    li = t.transpose(self.pt[:, hc * 128:(hc + 1) * 128],
                                     self.xnstage[:, slot, hc * 128:(hc + 1) * 128],
                                     self.ident[:, :])
                tp_mark[tt] = inc(li, "mt")

        def evac(tt):
            @self.on("scalar")
            def _(s, tt=tt):
                wait(s, "mt", tp_mark[tt])
                s.activation(out=xnA_v[:, :, tt * 128:(tt + 1) * 128],
                             in_=self.pt[:, :].rearrange("p (c q) -> p c q", q=128),
                             func=AF.Identity)
                ev_mark[tt] = inc(s, "ms")

        for tt in range(NTT):
            load(tt)
            stats(tt)
            lnexp(tt)
            if tt >= 1:
                xn(tt - 1)
                transpose(tt - 1)
                evac(tt - 1)
        xn(NTT - 1)
        transpose(NTT - 1)
        evac(NTT - 1)
        self.p1_done = self.CNT["ms"]

    def xpart_max(self, cols_ap, out_col):
        inc, wait = self.inc, self.wait

        @self.on("vector")
        def _(v):
            v.tensor_reduce(out=self.mx1[:, :], in_=cols_ap, axis=AX, op=ALU.max)
            v.drain()
            inc(v, "mv_")

        @self.on("tensor")
        def _(t):
            wait(t, "mv_")
            wait(t, "ms")
            inc(t.transpose(self.px[0:1, 0:128], self.mx1[:, 0:1], self.identf[:, :]),
                "mt")

        @self.on("vector")
        def _(v, out_col=out_col):
            wait(v, "mt")
            v.tensor_reduce(out=self.cc_sb[0:1, out_col:out_col + 1],
                            in_=self.px[0:1, 0:128], axis=AX, op=ALU.max)
            v.drain()
            inc(v, "mv_")

    def all_reduce_max_trigger(self):
        inc, dinc, wait = self.inc, self.dinc, self.wait

        @self.on("scalar")
        def _(s):
            wait(s, "mv_")
            dinc(s.dma_start(out=self.cc_in[:, :], in_=self.cc_sb[:, :]), "dci")

        @self.on("gpsimd")
        def _(g):
            wait(g, "dci")
            inc(g.collective_compute("AllReduce", ALU.max,
                                     replica_groups=[list(range(8))],
                                     ins=[self.cc_in.opt()], outs=[self.cc_out.opt()]),
                "mc")

    def all_reduce_max_fetch(self):
        inc, dinc, wait = self.inc, self.dinc, self.wait

        @self.on("gpsimd")
        def _(g):
            wait(g, "mc")
            dinc(g.dma_start(out=self.ccr_sb[:, :], in_=self.cc_out[:, :]), "dcc")

    def all_reduce_max(self):
        self.all_reduce_max_trigger()
        self.all_reduce_max_fetch()

    def bcast_scale(self, src_col, s_col, inv_col, raw_col=None):
        inc, wait = self.inc, self.wait

        @self.on("tensor")
        def _(t):
            wait(t, "dcc")
            wait(t, "mv_")
            wait(t, "ms")
            inc(t.matmul(self.px[0:128, 0:1], self.ones_f[0:1, 0:128],
                         self.ccr_sb[0:1, src_col:src_col + 1], start=True, stop=True),
                "mt")

        @self.on("vector")
        def _(v):
            wait(v, "mt")
            if raw_col is not None:
                v.tensor_copy(out=self.scales[:, raw_col:raw_col + 1], in_=self.px[0:128, 0:1])
            v.tensor_scalar(out=self.scales[:, s_col:s_col + 1], in0=self.px[0:128, 0:1],
                            scalar1=1.0 / QMAX, scalar2=1e-12, op0=ALU.mult, op1=ALU.max)
            v.drain()
            v.reciprocal(out=self.scales[:, inv_col:inv_col + 1],
                         in_=self.scales[:, s_col:s_col + 1])
            v.drain()
            inc(v, "mv_")

    def r1_start(self):
        self.xpart_max(self.maxc[:, 0:NTT], 0)
        self.all_reduce_max_trigger()

    def r1_finish(self):
        self.all_reduce_max_fetch()
        self.bcast_scale(0, 0, 1)

        @self.on("vector")
        def _(v):
            self.wait(v, "mv_")
            v.tensor_scalar(out=self.scales[:, 17:18], in0=self.scales[:, 0:1],
                            scalar1=-CQ, scalar2=None, op0=ALU.mult)
            v.drain()
            self.inc(v, "mv_")
        self.r1_scales = self.CNT["mv_"]

    def p2(self):
        inc, dinc, wait = self.inc, self.dinc, self.wait
        mt_of = {}
        ev_of = {}
        self.last_e = {}
        last_e = self.last_e
        k_sg = {}
        kv_last = {}
        spillq = []
        self._spq = spillq
        kst = self.wkB[:, 0:2048].bitcast(BF16)

        uc_order = list(range(8)) + [x for cc in range(8) for x in (8 + cc, 16 + cc)]
        dq_jobs = [(self.wu_d, icq, self.wuT_sp) for icq in range(IC)] + \
                  [(self.wg_d, icq, self.wgT_sp) for icq in range(IC)]
        dq_next = [0]
        gsg = -1
        for it, uc in enumerate(uc_order):
            slot = it % 2
            dws = "dwg0" if slot == 0 else "dwg1"

            @self.on("gpsimd")
            def _(g, uc=uc, slot=slot, dws=dws, it=it):
                if it >= 2:
                    wait(g, "mt", mt_of[(it - 2) * 4 + 3])
                dinc(g.dma_start(out=self.wstream[:, slot, :, :],
                                 in_=self.wrkv_d[:, uc * 128:(uc + 1) * 128]
                                 .rearrange("(hc p) q -> p hc q", p=128)), dws)

            for sg in range(4):
                gsg += 1
                t0 = sg * 1024
                b0 = (gsg % 2) * 2

                @self.on("tensor")
                def _(t, uc=uc, slot=slot, dws=dws, it=it, sg=sg, gsg=gsg, t0=t0, b0=b0):
                    if sg == 0:
                        wait(t, dws)
                        if it == 0:
                            wait(t, "ms", self.p1_done)
                    if gsg >= 2:
                        sname, scnt = ev_of[gsg - 2]
                        wait(t, sname, scnt)
                    for hc in range(HC):
                        for tl in range(2):
                            li = t.matmul(self.bank(b0 + tl), self.wstream[:, slot, hc, :],
                                          self.xnA(hc, t0 + tl * 512, 512),
                                          start=(hc == 0), stop=(hc == HC - 1))
                    mt_of[gsg] = inc(li, "mt")

                if uc < 8:
                    if sg == 0:
                        while spillq:
                            spillq.pop(0)()

                    @self.on("scalar")
                    def _(s, uc=uc, sg=sg, gsg=gsg, t0=t0, b0=b0):
                        wait(s, "mt", mt_of[gsg])
                        esl = gsg % 4
                        if esl in last_e:
                            wait(s, last_e[esl][0], last_e[esl][1])
                        for tl in range(2):
                            s.activation(out=self.estage[:, esl, tl * 512:(tl + 1) * 512],
                                         in_=self.bank(b0 + tl), func=AF.Sigmoid,
                                         bias=self.brkv_t[:, uc:uc + 1])
                        ev_of[gsg] = ("ms", inc(s, "ms"))

                    def sp_r(uc=uc, sg=sg, gsg=gsg):
                        @self.on("sync")
                        def _(s, uc=uc, sg=sg, gsg=gsg):
                            esl = gsg % 4
                            des = DE_SEMS[esl]
                            wait(s, ev_of[gsg][0], ev_of[gsg][1])
                            dinc(s.dma_start(
                                out=self.r_att[sg * 8:(sg + 1) * 8, :, uc, :]
                                .rearrange("tc p n -> p tc n"),
                                in_=self.estage[:, esl, 0:1024]
                                .rearrange("p (tc n) -> p tc n", n=128)), des)
                            last_e[esl] = (des, self.CNT[des])
                    spillq.append(sp_r)
                elif uc < 16:
                    if sg == 0:
                        while spillq:
                            spillq.pop(0)()

                    @self.on("scalar")
                    def _(s, uc=uc, sg=sg, gsg=gsg, t0=t0, b0=b0):
                        wait(s, "mt", mt_of[gsg])
                        cc = uc - 8
                        if sg == 0 and cc >= 1:
                            wait(s, "mv_", kv_last[cc - 1])
                            dkp = "dk0" if (cc - 1) % 2 == 0 else "dk1"
                            wait(s, dkp, self.CNT[dkp])
                        for tl in range(2):
                            s.activation(out=kst[:, t0 + tl * 512:t0 + (tl + 1) * 512],
                                         in_=self.bank(b0 + tl), func=AF.Exp,
                                         bias=self.brkv_t[:, uc:uc + 1])
                        k_sg[(cc, sg)] = ev_of[gsg] = ("ms", inc(s, "ms"))

                    def sp_k(uc=uc, sg=sg, gsg=gsg, t0=t0):
                        @self.on("sync")
                        def _(s, uc=uc, sg=sg, gsg=gsg, t0=t0):
                            cc = uc - 8
                            wait(s, ev_of[gsg][0], ev_of[gsg][1])
                            dkp = "dk0" if cc % 2 == 0 else "dk1"
                            dinc(s.dma_start(out=self.u_sp[8 + cc, :, t0:t0 + 1024],
                                             in_=kst[:, t0:t0 + 1024]), dkp)
                            dinc(s.dma_start(
                                out=self.u_att[sg * 8:(sg + 1) * 8, :, 8 + cc, :]
                                .rearrange("tc p n -> p tc n"),
                                in_=kst[:, t0:t0 + 1024]
                                .rearrange("p (tc n) -> p tc n", n=128)), dkp)
                    spillq.append(sp_k)
                else:
                    @self.on("vector")
                    def _(v, uc=uc, sg=sg, gsg=gsg, t0=t0, b0=b0):
                        cc = uc - 16
                        wait(v, "mt", mt_of[gsg])
                        sname, scnt = k_sg[(cc, sg)]
                        wait(v, sname, scnt)
                        esl = gsg % 4
                        if esl in last_e:
                            wait(v, last_e[esl][0], last_e[esl][1])
                        for tl in range(2):
                            v.scalar_tensor_tensor(
                                out=self.estage[:, esl, tl * 512:(tl + 1) * 512],
                                in0=self.bank(b0 + tl), scalar=self.brkv_t[:, uc:uc + 1],
                                in1=kst[:, t0 + tl * 512:t0 + (tl + 1) * 512],
                                op0=ALU.add, op1=ALU.mult)
                        kv_last[cc] = inc(v, "mv_")
                        ev_of[gsg] = ("mv_", kv_last[cc])

                    def sp_v(uc=uc, sg=sg, gsg=gsg, t0=t0):
                        @self.on("sync")
                        def _(s, uc=uc, sg=sg, gsg=gsg, t0=t0):
                            cc = uc - 16
                            wait(s, "mv_", kv_last[cc])
                            esl = gsg % 4
                            des = DE_SEMS[esl]
                            dinc(s.dma_start(
                                out=self.u_sp[cc, :, t0:t0 + 1024],
                                in_=self.estage[:, esl, 0:1024]), des)
                            dinc(s.dma_start(
                                out=self.u_att[sg * 8:(sg + 1) * 8, :, cc, :]
                                .rearrange("tc p n -> p tc n"),
                                in_=self.estage[:, esl, 0:1024]
                                .rearrange("p (tc n) -> p tc n", n=128)), des)
                            last_e[esl] = (des, self.CNT[des])
                    spillq.append(sp_v)
                if len(spillq) >= 3:
                    spillq.pop(0)()

            want = ((it + 1) * len(dq_jobs)) // len(uc_order)
            while dq_next[0] < want:
                n = dq_next[0]
                srcw, icq, dest = dq_jobs[n]
                self.dq_front(srcw, icq, 1024, self.wf_ug,
                              self.wq_ug2[n % 3], n % 2)
                if n >= 2:
                    picq = dq_jobs[n - 2][1]
                    self.dq_back(self._dq["g"] - 2, picq, 1024, "hbm",
                                 dq_jobs[n - 2][2], self.wq_ug2[(n - 2) % 3])
                dq_next[0] += 1

        while dq_next[0] < len(dq_jobs):
            n = dq_next[0]
            srcw, icq, dest = dq_jobs[n]
            self.dq_front(srcw, icq, 1024, self.wf_ug, self.wq_ug2[n % 3], n % 2)
            if n >= 2:
                picq = dq_jobs[n - 2][1]
                self.dq_back(self._dq["g"] - 2, picq, 1024, "hbm",
                             dq_jobs[n - 2][2], self.wq_ug2[(n - 2) % 3])
            dq_next[0] += 1
        for n in (len(dq_jobs) - 2, len(dq_jobs) - 1):
            self.dq_back(self._dq["g"] - (len(dq_jobs) - 1 - n), dq_jobs[n][1], 1024,
                         "hbm", dq_jobs[n][2], self.wq_ug2[n % 3])
        while spillq:
            spillq.pop(0)()
        self._spq = None
        self.p2_mt_last = self.CNT["mt"]
        self.p2_ev_ms = self.CNT["ms"]
        self.p2_done_marks = (self.CNT["mv_"], tuple(self.CNT[d] for d in DE_SEMS),
                              self.CNT["dk0"], self.CNT["dk1"])

    def dq_init(self):
        self._dq = {"vdone": {}, "tdone": {}, "sdone": {}, "meta": {}, "g": -1}
        wkb = self.wkB[:, :]
        self.wf_ug = wkb[:, 2048:4096].rearrange("p (s n) -> p s n", s=2)
        self.wq_ug2 = [wkb[:, 4096:4608].bitcast(BF16),
                       wkb[:, 4608:5120].bitcast(BF16),
                       wkb[:, 5120:5632].bitcast(BF16)]

    def dq_chunk(self, srcw, icq, I, kind, dest, wf, wq, islot):
        self.dq_front(srcw, icq, I, wf, wq, islot, guard=1)
        self.dq_back(self._dq["g"], icq, I, kind, dest, wq)

    def dq_front(self, srcw, icq, I, wf, wq, islot, guard=3):
        inc, dinc, wait = self.inc, self.dinc, self.wait
        D = self._dq
        D["g"] += 1
        g = D["g"]
        D["meta"][g] = (icq, I, wq)
        vdone, tdone, sdone = D["vdone"], D["tdone"], D["sdone"]
        nblk = I // 32
        nb = I // 128
        slot = g % 2
        dbs = "db0" if slot == 0 else "db1"

        @self.on("sync")
        def _(s, srcw=srcw, icq=icq, slot=slot, islot=islot, dbs=dbs, I=I, g=g, wf=wf):
            if g >= 2:
                wait(s, "mv_", vdone[g - 2])
            dinc(s.dma_start(out=wf[:, islot, 0:I],
                             in_=srcw[icq * 128:(icq + 1) * 128, :]), dbs)

        @self.on("vector")
        def _(v, slot=slot, islot=islot, dbs=dbs, I=I, nblk=nblk, g=g, wf=wf):
            wait(v, dbs)
            v.tensor_reduce(out=self.bmax[:, 0, 0:nblk],
                            in_=wf[:, islot, 0:I].rearrange("p (b k) -> p b k", k=32),
                            axis=AX, op=ALU.max, apply_absolute_value=True)
            v.drain()
            v.tensor_scalar(out=self.bs[:, 0, 0:nblk],
                            in0=self.bmax[:, 0, 0:nblk],
                            scalar1=1.0 / 7.0, scalar2=1e-12,
                            op0=ALU.mult, op1=ALU.max)
            v.drain()
            v.reciprocal(out=self.binv[:, 0, 0:nblk],
                         in_=self.bs[:, 0, 0:nblk])
            v.drain()
            v.tensor_tensor(out=wf[:, islot, 0:I].rearrange("p (b k) -> p b k", k=32),
                            in0=wf[:, islot, 0:I].rearrange("p (b k) -> p b k", k=32),
                            in1=self.binv[:, 0, 0:nblk].broadcast_to((128, nblk, 32)),
                            op=ALU.mult)
            inc(v, "mv_")

        for q0 in range(0, I, 1024):
            @self.on("vector")
            def _(v, slot=slot, islot=islot, q0=q0, nblk=nblk, g=g, wf=wf, wq=wq,
                  guard=guard):
                if g >= guard and q0 == 0:
                    wait(v, "mt", tdone[g - guard])
                fq = self.fdq[:, 0:1024]
                v.tensor_scalar(out=self.iscr[:, 0:1024],
                                in0=wf[:, islot, q0:q0 + 1024],
                                scalar1=8.0, scalar2=None, op0=ALU.add)
                v.drain()
                v.scalar_tensor_tensor(out=fq, in0=self.iscr[:, 0:1024],
                                       scalar=7.5, in1=wf[:, islot, q0:q0 + 1024],
                                       op0=ALU.subtract, op1=ALU.is_le)
                v.drain()
                v.scalar_tensor_tensor(out=fq, in0=self.iscr[:, 0:1024],
                                       scalar=8.0, in1=fq,
                                       op0=ALU.subtract, op1=ALU.add)
                v.drain()
                b0 = q0 // 32
                v.tensor_tensor(out=wq[:, q0:q0 + 1024]
                                .rearrange("p (b k) -> p b k", k=32),
                                in0=fq.rearrange("p (b k) -> p b k", k=32),
                                in1=self.bs[:, 0, b0:b0 + 32]
                                .broadcast_to((128, 32, 32)),
                                op=ALU.mult)
                inc(v, "mv_")
        vdone[g] = self.CNT["mv_"]

    def dq_back(self, g, icq, I, kind, dest, wq):
        inc, dinc, wait = self.inc, self.dinc, self.wait
        D = self._dq
        vdone, tdone, sdone = D["vdone"], D["tdone"], D["sdone"]
        nb = I // 128
        for gq in range(nb // 8):
            @self.on("tensor")
            def _(t, nb=nb, g=g, gq=gq, wq=wq):
                if gq == 0:
                    wait(t, "mv_", vdone[g])
                    if g >= 1:
                        wait(t, "ms", sdone[g - 1])
                else:
                    wait(t, "ms")
                for bq in range(gq * 8, gq * 8 + 8):
                    li = t.transpose(self.pt[:, (bq % 8) * 128:(bq % 8) * 128 + 128],
                                     wq[:, bq * 128:(bq + 1) * 128],
                                     self.ident[:, :])
                inc(li, "mt")

            if kind != "sbuf" and getattr(self, "_spq", None) is not None:
                while self._spq:
                    self._spq.pop(0)()

            @self.on("scalar")
            def _(s, icq=icq, nb=nb, g=g, gq=gq, kind=kind, dest=dest):
                wait(s, "mt")
                if kind == "sbuf":
                    s.activation(
                        out=self.wdT.rearrange("p (b h) -> p b h", h=1024)
                        [:, gq * 8:(gq + 1) * 8, icq * 128:(icq + 1) * 128],
                        in_=self.pt[:, :].rearrange("p (b q) -> p b q", q=128),
                        func=AF.Identity)
                    inc(s, "ms")
                else:
                    wait(s, "dq0", self.CNT["dq0"])
                    wait(s, "dq1", self.CNT["dq1"])
                    for half in range(2):
                        s.activation(out=self.pdqst[:, half, 0:512],
                                     in_=self.pt[:, half * 512:(half + 1) * 512],
                                     func=AF.Identity)
                    inc(s, "ms")

            if kind != "sbuf":
                dqm = self.CNT["ms"]

                def sp_dq(icq=icq, g=g, gq=gq, dest=dest, dqm=dqm):
                    @self.on("sync")
                    def _(s, icq=icq, g=g, gq=gq, dest=dest, dqm=dqm):
                        wait(s, "ms", dqm)
                        for half in range(2):
                            dws = "dq0" if half == 0 else "dq1"
                            dinc(s.dma_start(
                                out=dest[icq, gq * 8 + half * 4:gq * 8 + half * 4 + 4, :, :]
                                .rearrange("h p q -> p h q"),
                                in_=self.pdqst[:, half, 0:512]
                                .rearrange("p (h q) -> p h q", q=128)), dws)
                if getattr(self, "_spq", None) is not None:
                    self._spq.append(sp_dq)
                else:
                    sp_dq()
        tdone[g] = self.CNT["mt"]
        sdone[g] = self.CNT["ms"]

    def pdq_down(self):
        p2_ev_last, p2_de, p2_dk0, p2_dk1 = self.p2_done_marks
        wait = self.wait

        @self.on("sync")
        def _(s):
            wait(s, "mv_", p2_ev_last)
            wait(s, "ms", self.p2_ev_ms)
            wait(s, "dk0", p2_dk0)
            wait(s, "dk1", p2_dk1)
            for dd, vv in zip(DE_SEMS, p2_de):
                wait(s, dd, vv)
            wait(s, "mt", self._dq["tdone"][self._dq["g"]])
        wf_dn = self.wkB[:, 0:6144].rearrange("p (s n) -> p s n", s=2)
        wq_dn = self.wkB[:, 6144:7680].bitcast(BF16)
        for icq in range(HC):
            self.dq_chunk(self.wd_d, icq, 3072, "sbuf", None, wf_dn, wq_dn, icq % 2)
        self.pdq_done = self.CNT["mv_"]
        self.dq_spill = (self.CNT["dq0"], self.CNT["dq1"])

    def p3(self):
        inc, dinc, wait = self.inc, self.dinc, self.wait
        scan_done = {}

        for c in range(16):
            slot = c % 2
            das = "da0" if slot == 0 else "da1"

            @self.on("sync")
            def _(s, c=c, slot=slot, das=das):
                if c == 0:
                    wait(s, "mv_", self.pdq_done)
                    for dd, vv in zip(DE_SEMS, self.p2_done_marks[1]):
                        wait(s, dd, vv)
                    wait(s, "dk0", self.p2_done_marks[2])
                    wait(s, "dk1", self.p2_done_marks[3])
                if c >= 2:
                    wait(s, "mv_", scan_done[c - 2])
                dinc(s.dma_start(out=self.ubuf[:, slot, :], in_=self.u_sp[c, :, :]), das)

            @self.on("vector")
            def _(v, c=c, slot=slot, das=das):
                wait(v, das)
                v.tensor_tensor_scan(out=self.ydisc[:, :],
                                     data0=self.lam_t[:, c % 8:c % 8 + 1]
                                     .broadcast_to((128, T)),
                                     data1=self.ubuf[:, slot, :],
                                     initial=0.0, op0=ALU.mult, op1=ALU.add)
                v.drain()
                v.tensor_copy(out=self.S_loc[:, c:c + 1], in_=self.ydisc[:, T - 1:T])
                v.drain()
                scan_done[c] = inc(v, "mv_")

        @self.on("tensor")
        def _(t):
            wait(t, "mv_")
            inc(t.transpose(self.px[0:16, 0:128], self.S_loc[:, :], self.identf[:, :]),
                "mt")

        @self.on("vector")
        def _(v):
            wait(v, "mt")
            v.tensor_copy(out=self.ag_sb[0:16, :], in_=self.px[0:16, 0:128])
            v.drain()
            inc(v, "mv_")

        @self.on("scalar")
        def _(s):
            wait(s, "mv_")
            dinc(s.dma_start(out=self.ag_in[:, :], in_=self.ag_sb[0:16, :]), "dci")

        @self.on("gpsimd")
        def _(g):
            wait(g, "dci")
            inc(g.collective_compute("AllGather", ALU.bypass,
                                     replica_groups=[list(range(8))],
                                     ins=[self.ag_in.opt()], outs=[self.ag_out.opt()]),
                "mc")
            wait(g, "mc")
            dinc(g.dma_start(out=self.ag_sb[:, :], in_=self.ag_out[:, :]), "dcc")

        @self.on("tensor")
        def _(t):
            wait(t, "dcc")
            inc(t.matmul(self.px[0:128, 0:16], self.ag_sb[:, :], self.sel_sb[:, :],
                         start=True, stop=True), "mt")

        @self.on("vector")
        def _(v):
            wait(v, "mt")
            v.tensor_copy(out=self.S_prev[:, :], in_=self.px[0:128, 0:16])
            v.drain()
            self.sprev_ready = inc(v, "mv_")

    def p4(self):
        inc, wait = self.inc, self.wait
        prev = {}
        for st in range(32):
            @self.on("scalar")
            def _(s, st=st):
                if st == 0:
                    wait(s, "mv_", self.r1_scales)
                    wait(s, "mt", self.p2_mt_last)
                s.activation(out=self.iscr[:, (st % 2) * 1024:(st % 2) * 1024 + 1024],
                             in_=self.arenaA[:, st * 1024:(st + 1) * 1024],
                             func=AF.Identity, scale=self.scales[:, 1:2],
                             bias=self.consts[:, 0:1])
                s.drain()
                s.activation(out=self.arenaA[:, st * 1024:(st + 1) * 1024],
                             in_=self.iscr[:, (st % 2) * 1024:(st % 2) * 1024 + 1024],
                             func=AF.Identity, scale=self.scales[:, 0:1],
                             bias=self.scales[:, 17:18])
                s.drain()
                prev[st] = inc(s, "ms")
        self.p4_done = ("ms", self.CNT["ms"])

    def p5(self):
        inc, dinc, wait = self.inc, self.dinc, self.wait
        mt_of = {}
        mvred = {}
        ev_mm = {}
        spillq = []
        self.p65_init()
        self.ring_init()
        att_next = [0]
        git = -1
        gsg = -1

        for wi, (wT, spill, mcols) in enumerate(
                [(self.wuT_sp, self.up_sp, self.upmaxc),
                 (self.wgT_sp, self.gate_sp, self.gatemaxc)]):
            for ic in range(IC):
                git += 1
                slot = git % 2
                dws = "dw0" if slot == 0 else "dw1"

                @self.on("sync")
                def _(s, ic=ic, slot=slot, dws=dws, git=git, wT=wT):
                    if git < 2:
                        wait(s, "dq0", self.dq_spill[0])
                        wait(s, "dq1", self.dq_spill[1])
                    if git >= 2:
                        wait(s, "mt", mt_of[(git - 2, 1)])
                    dinc(s.dma_start(out=self.wstream[:, slot, :, :],
                                     in_=wT[ic, :, :, :].rearrange("h p q -> p h q")),
                         dws)

                for g2 in range(2):
                    banks = self.ralloc(4)
                    bmarks = [self.rmark[b] for b in banks]
                    t0 = g2 * 2048

                    @self.on("tensor")
                    def _(t, ic=ic, slot=slot, dws=dws, git=git, g2=g2,
                          banks=banks, bmarks=bmarks, t0=t0):
                        if g2 == 0:
                            wait(t, dws)
                            if git == 0:
                                wait(t, "ms")
                                wait(t, "mv_")
                                wait(t, "mt", self.p2_mt_last)
                        for m in bmarks:
                            if m is not None:
                                wait(t, m[0], m[1])
                        for hc in range(HC):
                            for j in range(4):
                                li = t.matmul(self.rbank[banks[j]],
                                              self.wstream[:, slot, hc, :],
                                              self.xnA(hc, t0 + j * 512, 512),
                                              start=(hc == 0), stop=(hc == HC - 1))
                        mt_of[(git, g2)] = inc(li, "mt")

                    for pj in range(2):
                        gsg += 1
                        esl = gsg % 4
                        des = DE_SEMS[esl]
                        b0_, b1_ = banks[2 * pj], banks[2 * pj + 1]
                        tt0 = t0 + pj * 1024

                        @self.on("scalar")
                        def _(s, ic=ic, git=git, g2=g2, gsg=gsg, esl=esl, des=des,
                              b0_=b0_, b1_=b1_, tt0=tt0, spill=spill):
                            wait(s, "mt", mt_of[(git, g2)])
                            le = self.last_e.get(esl)
                            if le is not None:
                                wait(s, le[0], le[1])
                            if gsg >= 4:
                                wait(s, "mv_", mvred[gsg - 4])
                            s.activation(out=self.estage[:, esl, 0:512],
                                         in_=self.rbank[b0_], func=AF.Identity)
                            s.activation(out=self.estage[:, esl, 512:1024],
                                         in_=self.rbank[b1_], func=AF.Identity)
                            m = ("ms", inc(s, "ms"))
                            self.rmark[b0_] = m
                            self.rmark[b1_] = m
                            ev_mm[gsg] = m[1]
                        spillq.append((ic, gsg, esl, des, tt0, spill))
                        if len(spillq) >= 3:
                            sic, sgsg, sesl, sdes, stt0, sspill = spillq.pop(0)

                            @self.on("sync")
                            def _(s, sic=sic, sgsg=sgsg, sesl=sesl, sdes=sdes,
                                  stt0=stt0, sspill=sspill):
                                wait(s, "ms", ev_mm[sgsg])
                                self.last_e[sesl] = (sdes, dinc(s.dma_start(
                                    out=sspill[sic, :, stt0:stt0 + 1024],
                                    in_=self.estage[:, sesl, 0:1024]), sdes))

                        @self.on("vector")
                        def _(v, gsg=gsg, esl=esl, mcols=mcols, git=git, g2=g2,
                              pj=pj, wi=wi):
                            m = self.rmark[0]
                            wait(v, "ms", self.CNT["ms"])
                            lit = (git - wi * IC) * 4 + g2 * 2 + pj
                            v.tensor_reduce(out=mcols[:, lit:lit + 1],
                                            in_=self.estage[:, esl, 0:1024],
                                            axis=AX, op=ALU.max,
                                            apply_absolute_value=True)
                            mvred[gsg] = inc(v, "mv_")

                want = ((git + 1) * NTT) // (2 * IC)
                while att_next[0] < want:
                    self.p65_front(att_next[0])
                    if att_next[0] >= 1:
                        self.p65_back(att_next[0] - 1)
                    att_next[0] += 1

        while spillq:
            sic, sgsg, sesl, sdes, stt0, sspill = spillq.pop(0)

            @self.on("sync")
            def _(s, sic=sic, sgsg=sgsg, sesl=sesl, sdes=sdes, stt0=stt0,
                  sspill=sspill):
                wait(s, "ms", ev_mm[sgsg])
                self.last_e[sesl] = (sdes, dinc(s.dma_start(
                    out=sspill[sic, :, stt0:stt0 + 1024],
                    in_=self.estage[:, sesl, 0:1024]), sdes))

        while att_next[0] < NTT:
            self.p65_front(att_next[0])
            if att_next[0] >= 1:
                self.p65_back(att_next[0] - 1)
            att_next[0] += 1
        self.p65_back(NTT - 1)
        self.p65_finish()
        self.p5_done_ms = self.CNT["ms"]
        self.p5_done_mv = self.CNT["mv_"]
        self.p5_done_de = tuple(self.CNT[d] for d in DE_SEMS)

    def r2(self):
        inc, wait = self.inc, self.wait
        self.xpart_max(self.upmaxc[:, 0:96], 1)
        self.xpart_max(self.gatemaxc[:, 0:96], 2)
        self.all_reduce_max()
        self.bcast_scale(1, 2, 3)
        self.bcast_scale(2, 4, 5)

        @self.on("vector")
        def _(v):
            wait(v, "mv_")
            v.tensor_scalar(out=self.scales[:, 10:11], in0=self.scales[:, 4:5],
                            scalar1=-CQ, scalar2=None, op0=ALU.mult)
            v.tensor_scalar(out=self.scales[:, 11:12], in0=self.scales[:, 4:5],
                            scalar1=1.0 / 65536.0, scalar2=None, op0=ALU.mult)
            v.tensor_scalar(out=self.scales[:, 16:17], in0=self.scales[:, 2:3],
                            scalar1=-CQ, scalar2=None, op0=ALU.mult)
            v.drain()
            self.r2_scales = inc(v, "mv_")

    def p6(self):
        inc, dinc, wait = self.inc, self.dinc, self.wait
        gi_mark = {}
        sig_mark = {}
        g2_mark = {}
        done = {}
        NQ = IC * 4

        def load(ic):
            slot = ic % 2
            das = "da0" if slot == 0 else "da1"

            @self.on("sync")
            def _(s, ic=ic, slot=slot, das=das):
                if ic == 0:
                    wait(s, "mv_", self.p65_done_mv)
                    for dd, vv in zip(DE_SEMS, self.p5_done_de):
                        wait(s, dd, vv)
                if ic >= 2:
                    wait(s, "mv_", done[ic - 2])
                dinc(s.dma_start(out=self.gbuf[:, slot, :],
                                 in_=self.gate_sp[ic, :, :]), das)

        def gi_stage(Q):
            ic, q = Q // 4, Q % 4
            slot = ic % 2
            das = "da0" if slot == 0 else "da1"
            Qp = Q % 2
            o0 = q * 1024

            @self.on("vector")
            def _(v, ic=ic, slot=slot, das=das, o0=o0, Q=Q, Qp=Qp):
                if o0 == 0:
                    wait(v, das)
                if Q >= 2:
                    wait(v, "ms", sig_mark[Q - 2])
                v.tensor_scalar(out=self.iscr[:, Qp * 1024:Qp * 1024 + 1024],
                                in0=self.gbuf[:, slot, o0:o0 + 1024],
                                scalar1=self.scales[:, 5:6],
                                scalar2=self.consts[:, 0:1],
                                op0=ALU.mult, op1=ALU.add)
                gi_mark[Q] = inc(v, "mv_")

        def sig_stage(Q):
            Qp = Q % 2

            @self.on("scalar")
            def _(s, Q=Q, Qp=Qp):
                wait(s, "mv_", gi_mark[Q])
                if Q >= 2:
                    wait(s, "mv_", g2_mark[Q - 2])
                s.activation(out=self.sigb[:, Qp, :],
                             in_=self.iscr[:, Qp * 1024:Qp * 1024 + 1024],
                             func=AF.Sigmoid, scale=self.scales[:, 4:5],
                             bias=self.scales[:, 10:11])
                s.drain()
                s.activation(out=self.svb[:, Qp, :].bitcast(I32),
                             in_=self.sigb[:, Qp, :],
                             func=AF.Identity, scale=65536.0,
                             bias=self.consts[:, 1:2])
                sig_mark[Q] = inc(s, "ms")

        def tail_stage(Q):
            ic, q = Q // 4, Q % 4
            Qp = Q % 2
            o0 = q * 1024

            @self.on("vector")
            def _(v, ic=ic, o0=o0, Q=Q, Qp=Qp):
                wait(v, "ms", sig_mark[Q])
                Ep = Q % 4
                le = self.last_e[Ep]
                wait(v, le[0], le[1])
                v.tensor_scalar(out=self.sigb[:, Qp, :],
                                in0=self.svb[:, Qp, :].bitcast(I32),
                                scalar1=65535.0, scalar2=self.scales[:, 11:12],
                                op0=ALU.min, op1=ALU.mult)
                v.drain()
                v.scalar_tensor_tensor(out=self.estage[:, Ep, 0:1024],
                                       in0=self.iscr[:, Qp * 1024:Qp * 1024 + 1024],
                                       scalar=CQ, in1=self.sigb[:, Qp, :],
                                       op0=ALU.subtract, op1=ALU.mult)
                v.drain()
                v.tensor_reduce(out=self.g2maxc[:, Q:Q + 1],
                                in_=self.estage[:, Ep, 0:1024], axis=AX,
                                op=ALU.max, apply_absolute_value=True)
                g2_mark[Q] = done[ic] = inc(v, "mv_")

        def dma_stage(Q):
            ic, q = Q // 4, Q % 4
            Qp = Q % 2
            o0 = q * 1024

            @self.on("sync")
            def _(s, ic=ic, o0=o0, Q=Q, Qp=Qp):
                wait(s, "mv_", g2_mark[Q])
                Ep = Q % 4
                des = DE_SEMS[Ep]
                self.last_e[Ep] = (des, dinc(s.dma_start(
                    out=self.g2_sp[ic, :, o0:o0 + 1024],
                    in_=self.estage[:, Ep, 0:1024]), des))

        for Q in range(NQ):
            if Q % 4 == 0:
                load(Q // 4)
            gi_stage(Q)
            sig_stage(Q)
            if Q >= 1:
                tail_stage(Q - 1)
            if Q >= 2:
                dma_stage(Q - 2)
        tail_stage(NQ - 1)
        dma_stage(NQ - 2)
        dma_stage(NQ - 1)

        self.p6_done = self.CNT["mv_"]
        self.p6_de = tuple(self.CNT[d] for d in DE_SEMS)

    def r3(self):
        self.xpart_max(self.g2maxc[:, 0:96], 3)
        self.all_reduce_max()
        self.bcast_scale(3, 6, 7)
        self.r3_scales = self.CNT["mv_"]

    def p65_init(self):
        self._a65 = {"blk_done": {}, "ev_prev": {}, "mt65": {}, "dma": {}}

    def p65_front(self, tc):
        inc, dinc, wait = self.inc, self.dinc, self.wait
        blk_done = self._a65["blk_done"]
        mt65 = self._a65["mt65"]
        slot = tc % 2
        das = "da0" if slot == 0 else "da1"
        dbs = "db0" if slot == 0 else "db1"
        t0 = tc * 128

        @self.on("sync")
        def _(s, tc=tc, slot=slot, das=das, dbs=dbs, t0=t0):
            if tc == 0:
                wait(s, "mv_", self.sprev_ready)
                for dd, vv in zip(DE_SEMS, self.p2_done_marks[1]):
                    wait(s, dd, vv)
                wait(s, "dk0", self.p2_done_marks[2])
                wait(s, "dk1", self.p2_done_marks[3])
            if tc >= 2:
                wait(s, "mv_", blk_done[tc - 2])
            dinc(s.dma_start(out=self.att_u[:, slot, :],
                             in_=self.u_att[tc, :, :, :].rearrange("p c n -> p (c n)")),
                 das)
            dinc(s.dma_start(out=self.att_r[:, slot, :],
                             in_=self.r_att[tc, :, :, :].rearrange("p c n -> p (c n)")),
                 dbs)

        @self.on("vector")
        def _(v, tc=tc, slot=slot, das=das, dbs=dbs):
            wait(v, das)
            wait(v, dbs)
            for c in range(16):
                init = self.S_prev[:, c:c + 1] if tc == 0 else self.carries[:, c:c + 1]
                v.tensor_tensor_scan(
                    out=self.att_y[:, c * 128:(c + 1) * 128],
                    data0=self.lam_t[:, c % 8:c % 8 + 1].broadcast_to((128, 128)),
                    data1=self.att_u[:, slot, c * 128:(c + 1) * 128],
                    initial=init, op0=ALU.mult, op1=ALU.add)
            v.drain()
            v.tensor_copy(out=self.carries[:, :],
                          in_=self.att_y[:, :].rearrange("p (c n) -> p c n", n=128)
                          [:, :, 127:128].rearrange("p c n -> p (c n)"))
            v.drain()
            for c in range(8):
                v.scalar_tensor_tensor(
                    out=self.att_n[:, c * 128:(c + 1) * 128],
                    in0=self.att_u[:, slot, c * 128:(c + 1) * 128],
                    scalar=self.gamma_t[:, c:c + 1],
                    in1=self.att_y[:, c * 128:(c + 1) * 128],
                    op0=ALU.mult, op1=ALU.add)
            for c in range(8):
                v.scalar_tensor_tensor(
                    out=self.att_d[:, c * 128:(c + 1) * 128],
                    in0=self.att_u[:, slot, (8 + c) * 128:(9 + c) * 128],
                    scalar=self.gamma_t[:, c:c + 1],
                    in1=self.att_y[:, (8 + c) * 128:(9 + c) * 128],
                    op0=ALU.mult, op1=ALU.add)
            v.drain()
            v.tensor_scalar(out=self.att_d[:, :], in0=self.att_d[:, :], scalar1=1e-6,
                            scalar2=None, op0=ALU.add)
            v.drain()
            v.reciprocal(out=self.att_y[:, 0:1024], in_=self.att_d[:, :])
            v.drain()
            v.tensor_tensor(out=self.att_n[:, :], in0=self.att_n[:, :],
                            in1=self.att_y[:, 0:1024], op=ALU.mult)
            v.drain()
            if tc >= 2:
                wait(v, "mt", mt65[tc - 2])
            v.tensor_tensor(out=self.ratio[:, slot, :, :].rearrange("p c n -> p (c n)"),
                            in0=self.att_n[:, :], in1=self.att_r[:, slot, :],
                            op=ALU.mult)
            blk_done[tc] = inc(v, "mv_")

    def p65_back(self, tc):
        inc, dinc, wait = self.inc, self.dinc, self.wait
        blk_done = self._a65["blk_done"]
        ev_prev = self._a65["ev_prev"]
        mt65 = self._a65["mt65"]
        adma = self._a65["dma"]
        slot = tc % 2

        banks = self.ralloc(2)
        bmarks = [self.rmark[b] for b in banks]

        @self.on("tensor")
        def _(t, tc=tc, banks=banks, bmarks=bmarks):
            wait(t, "mv_", blk_done[tc])
            if tc == 0:
                wait(t, "dg", self.dg_wo)
            for m in bmarks:
                if m is not None:
                    wait(t, m[0], m[1])
            for c in range(8):
                t.matmul(self.rbank[banks[0]], self.ratio[:, slot, c, :],
                         self.wo_sb[:, c * 1024:c * 1024 + 512],
                         start=(c == 0), stop=False)
                t.matmul(self.rbank[banks[1]], self.ratio[:, slot, c, :],
                         self.wo_sb[:, c * 1024 + 512:(c + 1) * 1024],
                         start=(c == 0), stop=False)
            t.matmul(self.rbank[banks[0]], self.ones_b[0:1, 0:128],
                     self.bo_row[0:1, 0:512], start=False, stop=True)
            li = t.matmul(self.rbank[banks[1]], self.ones_b[0:1, 0:128],
                          self.bo_row[0:1, 512:1024], start=False, stop=True)
            mt65[tc] = inc(li, "mt")

        @self.on("scalar")
        def _(s, tc=tc, slot=slot, banks=banks):
            wait(s, "mt", mt65[tc])
            if tc >= 2:
                wait(s, adma[tc - 2][0], adma[tc - 2][1])
            s.activation(out=self.xnstage[:, slot, 0:512],
                         in_=self.rbank[banks[0]], func=AF.Identity)
            s.activation(out=self.xnstage[:, slot, 512:1024],
                         in_=self.rbank[banks[1]], func=AF.Identity)
            m = ("ms", inc(s, "ms"))
            self.rmark[banks[0]] = m
            self.rmark[banks[1]] = m
            ev_prev[tc] = m[1]

        @self.on("sync")
        def _(s, tc=tc, slot=slot):
            wait(s, "ms", ev_prev[tc])
            dys = "dy0" if slot == 0 else "dy1"
            adma[tc] = (dys, dinc(s.dma_start(
                out=self.att_sp[tc, :, :], in_=self.xnstage[:, slot, :]), dys))

    def p65_finish(self):
        self.p65_done_ms = self.CNT["ms"]
        self.p65_done_mv = self.CNT["mv_"]
        self.p65_att_dy = (self.CNT["dy0"], self.CNT["dy1"])

    def p675(self):
        inc, dinc, wait = self.inc, self.dinc, self.wait
        done = {}
        ci_mark = {}
        pv_mark = {}

        def spill(Q):
            ic, q = Q // 4, Q % 4

            @self.on("sync")
            def _(s, ic=ic, q=q, Q=Q):
                wait(s, "mv_", pv_mark[Q])
                Ep = Q % 4
                des = DE_SEMS[Ep]
                self.last_e[Ep] = (des, dinc(s.dma_start(
                    out=self.o_att[q * 8:(q + 1) * 8, :, ic, :]
                    .rearrange("tc p n -> p tc n"),
                    in_=self.estage[:, Ep, 0:1024]
                    .rearrange("p (tc n) -> p tc n", n=128)), des))

        for ic in range(IC):
            slot = ic % 2
            das = "da0" if slot == 0 else "da1"
            dbs = "db0" if slot == 0 else "db1"

            @self.on("sync")
            def _(s, ic=ic, slot=slot, das=das, dbs=dbs):
                if ic == 0:
                    for dd, vv in zip(DE_SEMS, self.p6_de):
                        wait(s, dd, vv)
                    wait(s, "dy0", self.p65_att_dy[0])
                    wait(s, "dy1", self.p65_att_dy[1])
                if ic >= 2:
                    wait(s, "mv_", done[ic - 2])
                dinc(s.dma_start(out=self.g2in[:, slot, :], in_=self.g2_sp[ic, :, :]), das)
                dinc(s.dma_start(out=self.upin[:, slot, :], in_=self.up_sp[ic, :, :]), dbs)

            for q in range(4):
                Q = ic * 4 + q
                Qp = Q % 2
                o0 = q * 1024

                @self.on("scalar")
                def _(s, ic=ic, slot=slot, das=das, dbs=dbs, o0=o0, Q=Q, Qp=Qp):
                    if Q == 0:
                        wait(s, "mv_", self.r3_scales)
                    if o0 == 0:
                        wait(s, das)
                        wait(s, dbs)
                    if Q >= 2:
                        wait(s, "mv_", pv_mark[Q - 2])
                    s.activation(out=self.iscr[:, Qp * 2048 + 1024:Qp * 2048 + 2048],
                                 in_=self.upin[:, slot, o0:o0 + 1024],
                                 func=AF.Identity, scale=self.scales[:, 3:4],
                                 bias=self.consts[:, 0:1])
                    s.activation(out=self.iscr[:, Qp * 2048:Qp * 2048 + 1024],
                                 in_=self.g2in[:, slot, o0:o0 + 1024],
                                 func=AF.Identity, scale=self.scales[:, 7:8],
                                 bias=self.consts[:, 0:1])
                    if Q % 2 == 0:
                        s.drain()
                        s.activation(out=self.upq_t[:, Qp * 1024:Qp * 1024 + 1024],
                                     in_=self.iscr[:, Qp * 2048 + 1024:Qp * 2048 + 2048],
                                     func=AF.Identity, scale=self.scales[:, 2:3],
                                     bias=self.scales[:, 16:17])
                    ci_mark[Q] = inc(s, "ms")

                @self.on("vector")
                def _(v, ic=ic, o0=o0, Q=Q, Qp=Qp):
                    wait(v, "ms", ci_mark[Q])
                    Ep = Q % 4
                    le = self.last_e[Ep]
                    wait(v, le[0], le[1])
                    if Q % 2 == 1:
                        v.tensor_scalar(out=self.upq_t[:, Qp * 1024:Qp * 1024 + 1024],
                                        in0=self.iscr[:, Qp * 2048 + 1024:Qp * 2048 + 2048],
                                        scalar1=CQ, scalar2=self.scales[:, 2:3],
                                        op0=ALU.subtract, op1=ALU.mult)
                        v.drain()
                    v.scalar_tensor_tensor(out=self.estage[:, Ep, 0:1024],
                                           in0=self.iscr[:, Qp * 2048:Qp * 2048 + 1024],
                                           scalar=CQ,
                                           in1=self.upq_t[:, Qp * 1024:Qp * 1024 + 1024],
                                           op0=ALU.subtract, op1=ALU.mult)
                    v.drain()
                    v.tensor_reduce(out=self.pmaxc[:, Q:Q + 1],
                                    in_=self.estage[:, Ep, 0:1024], axis=AX,
                                    op=ALU.max, apply_absolute_value=True)
                    pv_mark[Q] = done[ic] = inc(v, "mv_")

                if Q >= 2:
                    spill(Q - 2)
        spill(IC * 4 - 2)
        spill(IC * 4 - 1)

        self.p675_done = self.CNT["mv_"]
        self.p675_de = tuple(self.CNT[d] for d in DE_SEMS)

    def r4(self):
        inc, wait = self.inc, self.wait
        self.xpart_max(self.pmaxc[:, 0:96], 4)
        self.all_reduce_max()
        self.bcast_scale(4, 13, 14, raw_col=12)

        @self.on("vector")
        def _(v):
            wait(v, "mv_")
            v.tensor_tensor(out=self.scales[:, 15:16], in0=self.scales[:, 6:7],
                            in1=self.scales[:, 12:13], op=ALU.mult)
            v.drain()
            v.tensor_scalar(out=self.scales[:, 8:9], in0=self.scales[:, 15:16],
                            scalar1=1.0 / QMAX, scalar2=1e-12, op0=ALU.mult, op1=ALU.max)
            v.drain()
            v.reciprocal(out=self.scales[:, 15:16], in_=self.scales[:, 8:9])
            v.drain()
            v.tensor_tensor(out=self.scales[:, 9:10], in0=self.scales[:, 6:7],
                            in1=self.scales[:, 15:16], op=ALU.mult)
            v.drain()
            self.r4_scales = inc(v, "mv_")

    def p7(self):
        inc, dinc, wait = self.inc, self.dinc, self.wait
        oq_mark = {}
        mm_mark = {}
        ev_mark = {}
        dma_mark = {}
        att_mark = {}
        pa2 = self.bigps[:, 0:1024]

        def loads(tc):
            slot = tc % 2
            das = "da0" if slot == 0 else "da1"
            dbs = "db0" if slot == 0 else "db1"
            dxs = "dx0" if slot == 0 else "dx1"
            t0 = tc * 128

            @self.on("sync")
            def _(s, tc=tc, slot=slot, das=das, dbs=dbs, dxs=dxs, t0=t0):
                if tc == 0:
                    wait(s, "mv_", self.p675_done)
                    for dd, vv in zip(DE_SEMS, self.p675_de):
                        wait(s, dd, vv)
                    wait(s, "dy0", self.p65_att_dy[0])
                    wait(s, "dy1", self.p65_att_dy[1])
                if tc >= 2:
                    wait(s, "mv_", ev_mark[tc - 2])
                dinc(s.dma_start(out=self.ostage[:, slot, :],
                                 in_=self.o_att[tc, :, :, :]
                                 .rearrange("p c n -> p (c n)")), das)
                dinc(s.dma_start(out=self.xld[:, slot, :], in_=self.x_d[t0:t0 + 128, :]),
                     dxs)
                dinc(s.dma_start(out=self.attnA(tc), in_=self.att_sp[tc, :, :]), dbs)

        def oquant(tc):
            slot = tc % 2
            das = "da0" if slot == 0 else "da1"
            for hf in range(2):
                h0 = hf * 1536

                @self.on("scalar")
                def _(s, tc=tc, slot=slot, das=das, h0=h0, hf=hf):
                    if tc == 0 and hf == 0:
                        wait(s, "mv_", self.r4_scales)
                    if hf == 0:
                        wait(s, das)
                        if tc >= 2:
                            wait(s, "mt", mm_mark[tc - 2])
                    if tc >= 1 or hf >= 1:
                        wait(s, "mv_")
                    s.activation(out=self.iscr[:, 0:1536],
                                 in_=self.ostage[:, slot, h0:h0 + 1536],
                                 func=AF.Identity, scale=self.scales[:, 9:10],
                                 bias=self.consts[:, 0:1])
                    s.drain()
                    inc(s, "ms")

                @self.on("vector")
                def _(v, tc=tc, h0=h0):
                    wait(v, "ms")
                    oq = self.oq if tc % 2 == 0 else self.oq2
                    v.tensor_scalar(out=oq[:, h0:h0 + 1536], in0=self.iscr[:, 0:1536],
                                    scalar1=CQ, scalar2=self.scales[:, 8:9],
                                    op0=ALU.subtract, op1=ALU.mult)
                    oq_mark[tc] = inc(v, "mv_")

        def mms(tc):
            @self.on("tensor")
            def _(t, tc=tc):
                wait(t, "mv_", oq_mark[tc])
                if tc == 0:
                    wait(t, "ms", self.p65_done_ms)
                if tc >= 2:
                    wait(t, "mv_", ev_mark[tc - 2])
                oq = self.oq if tc % 2 == 0 else self.oq2
                ps = self.pa if tc % 2 == 0 else pa2
                for ic in range(IC):
                    t.matmul(ps[:, 0:512], oq[:, ic * 128:(ic + 1) * 128],
                             self.wdT[:, ic * 1024:ic * 1024 + 512],
                             start=(ic == 0), stop=(ic == IC - 1))
                    li = t.matmul(ps[:, 512:1024], oq[:, ic * 128:(ic + 1) * 128],
                                  self.wdT[:, ic * 1024 + 512:(ic + 1) * 1024],
                                  start=(ic == 0), stop=(ic == IC - 1))
                mm_mark[tc] = inc(li, "mt")

        def evac(tc):
            slot = tc % 2
            dxs = "dx0" if slot == 0 else "dx1"
            dbs = "db0" if slot == 0 else "db1"

            @self.on("vector")
            def _(v, tc=tc, slot=slot, dxs=dxs, dbs=dbs):
                wait(v, "mt", mm_mark[tc])
                wait(v, dxs)
                wait(v, dbs)
                if tc >= 2:
                    wait(v, dma_mark[tc - 2][0], dma_mark[tc - 2][1])
                ps = self.pa if tc % 2 == 0 else pa2
                v.scalar_tensor_tensor(out=self.outst[:, slot, :], in0=ps[:, :],
                                       scalar=1.0, in1=self.attnA(tc),
                                       op0=ALU.mult, op1=ALU.add)
                v.drain()
                v.tensor_tensor(out=self.outst[:, slot, :], in0=self.outst[:, slot, :],
                                in1=self.xld[:, slot, :], op=ALU.add)
                ev_mark[tc] = inc(v, "mv_")

        def store(tc):
            slot = tc % 2
            t0 = tc * 128

            @self.on("sync")
            def _(s, tc=tc, slot=slot, t0=t0):
                wait(s, "mv_", ev_mark[tc])
                dys = "dy0" if slot == 0 else "dy1"
                dma_mark[tc] = (dys, dinc(s.dma_start(
                    out=self.y_d[t0:t0 + 128, :], in_=self.outst[:, slot, :]), dys))

        for tc in range(NTT):
            loads(tc)
            oquant(tc)
            mms(tc)
            if tc >= 1:
                evac(tc - 1)
                store(tc - 1)
        evac(NTT - 1)
        store(NTT - 1)

    def fin(self):
        @self.on("sync")
        def _(s):
            self.wait(s, "dy0")
            self.wait(s, "dy1")


def build_nc():
    b = Builder()
    nc = b.build()
    b.ctx.close()
    return nc


_NC_CACHE = None


def kernel(L0_attn_inputs, L0_attn_W_rkv, L0_attn_b_rkv, L0_attn_W_o, L0_attn_b_o,
           L0_attn_nu_log, L0_attn_gamma_log, L0_ffn_w_gate, L0_ffn_w_up, L0_ffn_w_down):
    global _NC_CACHE
    x = np.ascontiguousarray(np.asarray(L0_attn_inputs, dtype=np.float32))
    wrkv = np.ascontiguousarray(np.asarray(L0_attn_W_rkv, dtype=np.float32))
    brkv = np.ascontiguousarray(np.asarray(L0_attn_b_rkv, dtype=np.float32))
    wo = np.ascontiguousarray(np.asarray(L0_attn_W_o, dtype=np.float32))
    bo = np.ascontiguousarray(np.asarray(L0_attn_b_o, dtype=np.float32))
    nulog = np.ascontiguousarray(np.asarray(L0_attn_nu_log, dtype=np.float32))
    glog = np.ascontiguousarray(np.asarray(L0_attn_gamma_log, dtype=np.float32))
    wg = np.ascontiguousarray(np.asarray(L0_ffn_w_gate, dtype=np.float32))
    wu = np.ascontiguousarray(np.asarray(L0_ffn_w_up, dtype=np.float32))
    wd = np.ascontiguousarray(np.asarray(L0_ffn_w_down, dtype=np.float32))

    if _NC_CACHE is None:
        _NC_CACHE = build_nc()
    nc = _NC_CACHE

    in_maps = []
    for c in range(8):
        bb, hh = c // 2, c % 2
        sel = np.zeros((128, 16), np.float32)
        if hh == 1:
            for g in range(16):
                sel[16 * (c - 1) + g, g] = 1.0
        in_maps.append({
            "x": np.ascontiguousarray(x[bb, hh * T:(hh + 1) * T, :]),
            "wrkv": wrkv, "brkv": brkv, "wo": wo, "bo": bo,
            "nulog": nulog, "glog": glog, "wg": wg, "wu": wu, "wd": wd,
            "sel": sel,
        })
    res = run_bass_kernel_spmd(nc, in_maps, core_ids=list(range(8)))
    out = np.empty((B, L, H), dtype=np.float32)
    for c in range(8):
        bb, hh = c // 2, c % 2
        out[bb, hh * T:(hh + 1) * T, :] = res.results[c]["y"]
    return out


# version 21
# speedup vs baseline: 1.0446x; 1.0246x over previous
import numpy as np
from contextlib import ExitStack
import concourse.bass as bass
import concourse.mybir as mybir
from concourse.bass_utils import run_bass_kernel_spmd

F32 = mybir.dt.float32
BF16 = mybir.dt.bfloat16
I32 = mybir.dt.int32
AF = mybir.ActivationFunctionType
ALU = mybir.AluOpType
AX = mybir.AxisListType.X

B, L, H, U3 = 4, 8192, 1024, 3072
T = 4096
HC = 8
UC = 24
IC = 24
NT = 8
NTT = 32
CQ = 65536.0
DE_SEMS = ["de0", "de1", "de2", "de3"]
QMAX = 32767.0


class _RecInstr:
    __slots__ = ("_rq", "_rid")

    def __init__(self, rq, rid):
        self._rq = rq
        self._rid = rid

    def __getattr__(self, name):
        def call(*args, **kwargs):
            self._rq.append(("chain", self._rid, name, args, kwargs))
            return self
        return call


class _RecEngine:
    __slots__ = ("_name", "_q", "_n")

    def __init__(self, name):
        self._name = name
        self._q = []
        self._n = 0

    def __getattr__(self, name):
        if name == "then_inc":
            raise AttributeError(name)

        def call(*args, **kwargs):
            rid = self._n
            self._n += 1
            self._q.append(("call", rid, name, args, kwargs))
            return _RecInstr(self._q, rid)
        return call


class Builder:
    def __init__(self):
        self.nc = bass.Bass("TRN2", target_bir_lowering=False, debug=False)
        self.ctx = ExitStack()
        self.CNT = {}
        self.SEM = {}
        self._rec = {}

    def sem(self, name):
        self.SEM[name] = self.ctx.enter_context(self.nc.semaphore(name))
        self.CNT[name] = 0

    def inc(self, bi, name, amt=1):
        if not hasattr(bi, "then_inc"):
            bi = bi.drain()
        bi.then_inc(self.SEM[name], amt)
        self.CNT[name] += amt
        return self.CNT[name]

    def dinc(self, bi, name):
        return self.inc(bi, name, 16)

    def wait(self, eng, name, val=None):
        eng.wait_ge(self.SEM[name], self.CNT[name] if val is None else val)

    def sb(self, name, shape, dt):
        return self.ctx.enter_context(self.nc.sbuf_tensor(name, shape, dt))

    def on(self, engine_name):
        rec = self._rec.setdefault(engine_name, _RecEngine(engine_name))

        def deco(fn, *a, **k):
            fn(rec, *a, **k)
        return deco

    def flush_engines(self):
        for name, rec in self._rec.items():
            q = rec._q

            def replay(eng, q=q):
                objs = {}
                for kind, rid, meth, args, kwargs in q:
                    if kind == "call":
                        objs[rid] = getattr(eng, meth)(*args, **kwargs)
                    else:
                        objs[rid] = getattr(objs[rid], meth)(*args, **kwargs)
            getattr(self.block, name)(replay)
        self._rec = {}

    def build(self):
        nc = self.nc
        self.x_d = nc.dram_tensor("x", [T, H], F32, kind="ExternalInput").ap()
        self.wrkv_d = nc.dram_tensor("wrkv", [H, U3], F32, kind="ExternalInput").ap()
        self.brkv_d = nc.dram_tensor("brkv", [U3], F32, kind="ExternalInput").ap()
        self.wo_d = nc.dram_tensor("wo", [H, H], F32, kind="ExternalInput").ap()
        self.bo_d = nc.dram_tensor("bo", [H], F32, kind="ExternalInput").ap()
        self.nulog_d = nc.dram_tensor("nulog", [H], F32, kind="ExternalInput").ap()
        self.glog_d = nc.dram_tensor("glog", [H], F32, kind="ExternalInput").ap()
        self.wg_d = nc.dram_tensor("wg", [U3, H], F32, kind="ExternalInput").ap()
        self.wu_d = nc.dram_tensor("wu", [U3, H], F32, kind="ExternalInput").ap()
        self.wd_d = nc.dram_tensor("wd", [H, U3], F32, kind="ExternalInput").ap()
        self.sel_d = nc.dram_tensor("sel", [128, 16], F32, kind="ExternalInput").ap()
        self.y_d = nc.dram_tensor("y", [T, H], F32, kind="ExternalOutput").ap()
        dr = lambda n, s, d: nc.dram_tensor(n, s, d).ap()
        self.u_sp = dr("u_sp", [16, 128, T], BF16)
        self.r_sp = dr("r_sp", [8, 128, T], BF16)
        self.up_sp = dr("up_sp", [IC, 128, T], BF16)
        self.gate_sp = dr("gate_sp", [IC, 128, T], BF16)
        self.g2_sp = dr("g2_sp", [IC, 128, T], BF16)
        self.o_sp = dr("o_sp", [IC, 128, T], BF16)
        self.wuT_sp = dr("wuT_sp", [IC, HC, 128, 128], BF16)
        self.att_sp = dr("att_sp", [NTT, 128, 1024], BF16)
        self.u_att = dr("u_att", [NTT, 128, 16, 128], BF16)
        self.r_att = dr("r_att", [NTT, 128, 8, 128], BF16)
        self.o_att = dr("o_att", [NTT, 128, IC, 128], BF16)
        self.wgT_sp = dr("wgT_sp", [IC, HC, 128, 128], BF16)
        self.cc_in = dr("cc_in", [1, 16], F32)
        self.cc_out = dr("cc_out", [1, 16], F32)
        self.ag_in = dr("ag_in", [16, 128], F32)
        self.ag_out = dr("ag_out", [128, 128], F32)

        sb = self.sb
        self.arenaA = sb("arenaA", [128, 8 * T], BF16)
        self.wdT = sb("wdT", [128, IC * 1024], BF16)
        self.wo_sb = sb("wo_sb", [128, HC * 1024], BF16)
        self.wkB = sb("wkB", [128, 8192], F32)
        self.ident = sb("ident", [128, 128], BF16)
        self.identf = sb("identf", [128, 128], F32)
        self.ones_b = sb("ones_b", [1, 128], BF16)
        self.ones_f = sb("ones_f", [1, 128], F32)
        self.bo_row = sb("bo_row", [1, 1024], BF16)
        self.consts = sb("consts", [128, 8], F32)
        self.brkv_t = sb("brkv_t", [128, 24], F32)
        self.nulog_t = sb("nulog_t", [128, 8], F32)
        self.glog_t = sb("glog_t", [128, 8], F32)
        self.lam_t = sb("lam_t", [128, 8], F32)
        self.gamma_t = sb("gamma_t", [128, 8], F32)
        self.stats = sb("stats", [128, 2, 12], F32)
        self.mvt = sb("mvt", [128, 2, 2], F32)
        self.tiny = sb("tiny", [128, 12], F32)
        self.maxc = sb("maxc", [128, 36], F32)
        self.upmaxc = sb("upmaxc", [128, 96], F32)
        self.gatemaxc = sb("gatemaxc", [128, 96], F32)
        self.g2maxc = sb("g2maxc", [128, 96], F32)
        self.pmaxc = sb("pmaxc", [128, 96], F32)
        self.mx1 = sb("mx1", [128, 1], F32)
        self.cc_sb = sb("cc_sb", [1, 16], F32)
        self.ccr_sb = sb("ccr_sb", [1, 16], F32)
        self.scales = sb("scales", [128, 20], F32)
        self.S_loc = sb("S_loc", [128, 16], F32)
        self.S_prev = sb("S_prev", [128, 16], F32)
        self.carries = sb("carries", [128, 16], F32)
        self.ag_sb = sb("ag_sb", [128, 128], F32)
        self.agT_sb = None
        self.sel_sb = sb("sel_sb", [128, 16], F32)
        self.xld = sb("xld", [128, 2, 1024], F32)
        self.xnstage = sb("xnstage", [128, 2, 1024], BF16)
        self.wstream = sb("wstream", [128, 2, HC, 128], BF16)
        self.estage = sb("estage", [128, 4, 1024], BF16)
        self.iscr = sb("iscr", [128, 4096], I32)
        self.kstage = self.wkB[:, :].bitcast(BF16).rearrange(
            "p (c t n) -> p c t n", c=8, t=4)
        self.pdqst = self.xnstage
        self.bmax = sb("bmax", [128, 1, 96], F32)
        self.binv = sb("binv", [128, 1, 96], F32)
        self.bs = sb("bs", [128, 1, 96], F32)

        wkb = self.wkB[:, :]
        self.wf = wkb[:, 0:6144].rearrange("p (s n) -> p s n", s=2)
        self.wq = wkb[:, 6144:7680].bitcast(BF16)
        self.ubuf = wkb[:, 0:4096].bitcast(BF16).rearrange("p (s n) -> p s n", s=2)
        self.ydisc = wkb[:, 4096:8192]
        self.gbuf = self.ubuf
        self.sigb = wkb[:, 4096:6144].rearrange("p (s n) -> p s n", s=2)
        self.svb = wkb[:, 6144:8192].rearrange("p (s n) -> p s n", s=2)
        self.att_u = wkb[:, 0:2048].bitcast(BF16).rearrange("p (s n) -> p s n", s=2)
        self.att_r = wkb[:, 2048:3072].bitcast(BF16).rearrange("p (s n) -> p s n", s=2)
        self.att_y = wkb[:, 3072:5120]
        self.att_n = wkb[:, 5120:6144]
        self.att_d = wkb[:, 6144:7168]
        self.att_c = wkb[:, 7168:8192]
        self.ratio = self.att_c[:, :].bitcast(BF16).rearrange("p (s c n) -> p s c n", s=2, c=8)
        self.g2in = wkb[:, 0:4096].bitcast(BF16).rearrange("p (s n) -> p s n", s=2)
        self.upin = wkb[:, 4096:8192].bitcast(BF16).rearrange("p (s n) -> p s n", s=2)
        self.ostage = wkb[:, 0:3072].bitcast(BF16).rearrange("p (s n) -> p s n", s=2)
        self.oq = wkb[:, 3072:4608].bitcast(BF16)
        self.oq2 = wkb[:, 6656:8192].bitcast(BF16)
        self.outst = wkb[:, 4608:6656].rearrange("p (s n) -> p s n", s=2)
        self.upq_t = self.xnstage[:, :, :].rearrange("p s n -> p (s n)")

        self.bigps = self.ctx.enter_context(nc.psum_tensor("bigps", [128, 3584], F32))
        self.pt = self.ctx.enter_context(nc.psum_tensor("pt", [128, 1024], BF16))
        self.pa = self.bigps[:, 2048:3072]
        self.fdq = self.bigps[:, 2048:3072]
        self.px = self.bigps[:, 3072:3584]

        for n in ["dx0", "dx1", "dw0", "dw1", "dwg0", "dwg1", "da0", "da1", "db0", "db1",
                  "dsp", "de0", "de1", "de2", "de3", "dci", "dq0", "dq1", "dy0", "dy1", "dy", "dg", "dcc", "dk", "dk0", "dk1", "mt", "mv_", "ms", "mg", "mc"]:
            self.sem(n)

        self.block = self.ctx.enter_context(nc.Block())

        self.p0()
        self.p1()
        self.r1_start()
        self.dq_init()
        self.p2()
        self.pdq_down()
        self.r1_finish()
        self.p3()
        self.p4()
        self.p5()
        self.r2()
        self.p6()
        self.r3()
        self.p675()
        self.r4()
        self.p7()
        self.fin()
        self.flush_engines()
        return nc

    def bank(self, i):
        return self.bigps[:, i * 512:(i + 1) * 512]

    def ring_init(self):
        self.rbank = [self.bigps[:, i * 512:(i + 1) * 512] for i in range(7)]
        self.rbank.append(self.pt[:, :].bitcast(F32))
        self.rnext = 0
        self.rmark = {i: None for i in range(8)}

    def ralloc(self, n):
        idx = [(self.rnext + j) % 8 for j in range(n)]
        self.rnext = (self.rnext + n) % 8
        return idx

    def xnA(self, hc, t0, n):
        return self.arenaA[:, hc * T + t0: hc * T + t0 + n]

    def attnA(self, tc):
        return self.arenaA[:, tc * 1024:(tc + 1) * 1024]

    def p0(self):
        inc, dinc, wait = self.inc, self.dinc, self.wait

        @self.on("gpsimd")
        def _(g):
            g.iota(out=self.iscr[:, 0:128], pattern=[[1, 128]], base=0, channel_multiplier=-1)
            g.memset(self.consts[:, 0:1], 65536.5)
            g.memset(self.consts[:, 1:2], 0.5)
            g.memset(self.consts[:, 2:3], 8.0)
            g.memset(self.ones_b[:, :], 1.0)
            g.memset(self.cc_sb[:, :], 0.0)
            g.memset(self.ones_f[:, :], 1.0)
            g.drain()
            g.tensor_scalar(out=self.ident[:, :], in0=self.iscr[:, 0:128], scalar1=0.0,
                            scalar2=None, op0=ALU.is_equal)
            g.tensor_scalar(out=self.identf[:, :], in0=self.iscr[:, 0:128], scalar1=0.0,
                            scalar2=None, op0=ALU.is_equal)
            g.drain()
            self.mg_init = inc(g, "mg")
            dinc(g.dma_start(out=self.wo_sb.rearrange("p (c n) -> p c n", n=1024),
                        in_=self.wo_d.rearrange("(c p) n -> p c n", p=128)), "dg")
            self.dg_wo = dinc(g.dma_start(out=self.bo_row[:, :], in_=self.bo_d[None, :]), "dg")

        @self.on("sync")
        def _(s):
            dinc(s.dma_start(out=self.brkv_t[:, :], in_=self.brkv_d.rearrange("(c p) -> p c", p=128), allow_slow_non_contiguous=True), "da0")
            dinc(s.dma_start(out=self.nulog_t[:, :], in_=self.nulog_d.rearrange("(c p) -> p c", p=128), allow_slow_non_contiguous=True), "da0")
            dinc(s.dma_start(out=self.glog_t[:, :], in_=self.glog_d.rearrange("(c p) -> p c", p=128), allow_slow_non_contiguous=True), "da0")
            self.da_p0 = dinc(s.dma_start(out=self.sel_sb[:, :], in_=self.sel_d[:, :]), "da0")

        @self.on("scalar")
        def _(s):
            wait(s, "da0", self.da_p0)
            s.activation(out=self.tiny[:, 0:8], in_=self.nulog_t[:, :], func=AF.Exp)
            s.drain()
            s.activation(out=self.lam_t[:, :], in_=self.tiny[:, 0:8], func=AF.Exp, scale=-1.0)
            s.drain()
            inc(s, "ms")

        @self.on("vector")
        def _(v):
            wait(v, "da0", self.da_p0)
            v.tensor_tensor(out=self.glog_t[:, :], in0=self.nulog_t[:, :],
                            in1=self.glog_t[:, :], op=ALU.add)
            v.drain()
            inc(v, "mv_")

        @self.on("scalar")
        def _(s):
            wait(s, "mv_")
            s.activation(out=self.gamma_t[:, :], in_=self.glog_t[:, :], func=AF.Exp)
            s.drain()
            inc(s, "ms")

        @self.on("vector")
        def _(v):
            wait(v, "ms")
            v.tensor_scalar(out=self.gamma_t[:, :], in0=self.gamma_t[:, :], scalar1=-1.0,
                            scalar2=None, op0=ALU.add)
            v.drain()
            self.mv_gamma = inc(v, "mv_")

    def p1(self):
        inc, dinc, wait = self.inc, self.dinc, self.wait
        st_mark = {}
        ln_mark = {}
        xn_mark = {}
        tp_mark = {}
        ev_mark = {}
        xnA_v = self.arenaA[:, :].rearrange("p (c t) -> p c t", t=T)

        def load(tt):
            slot = tt % 2
            dxs = "dx0" if slot == 0 else "dx1"

            @self.on("sync")
            def _(s, tt=tt, slot=slot, dxs=dxs):
                if tt >= 2:
                    wait(s, "mv_", xn_mark[tt - 2])
                dinc(s.dma_start(out=self.xld[:, slot, :],
                                 in_=self.x_d[tt * 128:(tt + 1) * 128, :]), dxs)

        def stats(tt):
            slot = tt % 2
            dxs = "dx0" if slot == 0 else "dx1"

            @self.on("vector")
            def _(v, tt=tt, slot=slot, dxs=dxs):
                wait(v, dxs)
                if tt >= 2:
                    wait(v, "ms", ln_mark[tt - 2])
                v.bn_stats(out=self.stats[:, slot, 0:6], in_=self.xld[:, slot, 0:512])
                v.bn_stats(out=self.stats[:, slot, 6:12], in_=self.xld[:, slot, 512:1024])
                v.drain()
                v.bn_aggr(out=self.mvt[:, slot, :], in_=self.stats[:, slot, :])
                v.drain()
                v.tensor_scalar(out=self.tiny[:, slot:slot + 1],
                                in0=self.mvt[:, slot, 1:2], scalar1=1e-5,
                                scalar2=None, op0=ALU.add)
                st_mark[tt] = inc(v, "mv_")

        def lnexp(tt):
            slot = tt % 2

            @self.on("scalar")
            def _(s, tt=tt, slot=slot):
                wait(s, "mv_", st_mark[tt])
                if tt >= 2:
                    wait(s, "mv_", xn_mark[tt - 2])
                s.activation(out=self.tiny[:, 2 + slot:3 + slot],
                             in_=self.tiny[:, slot:slot + 1], func=AF.Ln)
                s.drain()
                s.activation(out=self.tiny[:, 4 + slot:5 + slot],
                             in_=self.tiny[:, 2 + slot:3 + slot], func=AF.Exp,
                             scale=-0.5)
                s.drain()
                s.activation(out=self.tiny[:, 6 + slot:7 + slot],
                             in_=self.tiny[:, 4 + slot:5 + slot], func=AF.Identity,
                             scale=-1.0)
                s.drain()
                s.activation(out=self.tiny[:, 8 + slot:9 + slot],
                             in_=self.mvt[:, slot, 0:1], func=AF.Identity,
                             scale=self.tiny[:, 6 + slot:7 + slot])
                s.drain()
                s.activation(out=self.xnstage[:, slot, :], in_=self.xld[:, slot, :],
                             func=AF.Identity, scale=self.tiny[:, 4 + slot:5 + slot],
                             bias=self.tiny[:, 8 + slot:9 + slot])
                ln_mark[tt] = inc(s, "ms")

        def xn(tt):
            slot = tt % 2

            @self.on("vector")
            def _(v, tt=tt, slot=slot):
                wait(v, "ms", ln_mark[tt])
                v.tensor_reduce(out=self.maxc[:, tt:tt + 1], in_=self.xnstage[:, slot, :],
                                axis=AX, op=ALU.max, apply_absolute_value=True)
                xn_mark[tt] = inc(v, "mv_")

        def transpose(tt):
            slot = tt % 2

            @self.on("tensor")
            def _(t, tt=tt, slot=slot):
                if tt == 0:
                    wait(t, "mg", self.mg_init)
                wait(t, "mv_", xn_mark[tt])
                if tt >= 1:
                    wait(t, "ms", ev_mark[tt - 1])
                for hc in range(HC):
                    li = t.transpose(self.pt[:, hc * 128:(hc + 1) * 128],
                                     self.xnstage[:, slot, hc * 128:(hc + 1) * 128],
                                     self.ident[:, :])
                tp_mark[tt] = inc(li, "mt")

        def evac(tt):
            @self.on("scalar")
            def _(s, tt=tt):
                wait(s, "mt", tp_mark[tt])
                s.activation(out=xnA_v[:, :, tt * 128:(tt + 1) * 128],
                             in_=self.pt[:, :].rearrange("p (c q) -> p c q", q=128),
                             func=AF.Identity)
                ev_mark[tt] = inc(s, "ms")

        for tt in range(NTT):
            load(tt)
            stats(tt)
            lnexp(tt)
            if tt >= 1:
                xn(tt - 1)
                transpose(tt - 1)
                evac(tt - 1)
        xn(NTT - 1)
        transpose(NTT - 1)
        evac(NTT - 1)
        self.p1_done = self.CNT["ms"]

    def xpart_max(self, cols_ap, out_col):
        inc, wait = self.inc, self.wait

        @self.on("vector")
        def _(v):
            v.tensor_reduce(out=self.mx1[:, :], in_=cols_ap, axis=AX, op=ALU.max)
            v.drain()
            inc(v, "mv_")

        @self.on("tensor")
        def _(t):
            wait(t, "mv_")
            wait(t, "ms")
            inc(t.transpose(self.px[0:1, 0:128], self.mx1[:, 0:1], self.identf[:, :]),
                "mt")

        @self.on("vector")
        def _(v, out_col=out_col):
            wait(v, "mt")
            v.tensor_reduce(out=self.cc_sb[0:1, out_col:out_col + 1],
                            in_=self.px[0:1, 0:128], axis=AX, op=ALU.max)
            v.drain()
            inc(v, "mv_")

    def all_reduce_max_trigger(self):
        inc, dinc, wait = self.inc, self.dinc, self.wait

        @self.on("scalar")
        def _(s):
            wait(s, "mv_")
            dinc(s.dma_start(out=self.cc_in[:, :], in_=self.cc_sb[:, :]), "dci")

        @self.on("gpsimd")
        def _(g):
            wait(g, "dci")
            inc(g.collective_compute("AllReduce", ALU.max,
                                     replica_groups=[list(range(8))],
                                     ins=[self.cc_in.opt()], outs=[self.cc_out.opt()]),
                "mc")

    def all_reduce_max_fetch(self):
        inc, dinc, wait = self.inc, self.dinc, self.wait

        @self.on("gpsimd")
        def _(g):
            wait(g, "mc")
            dinc(g.dma_start(out=self.ccr_sb[:, :], in_=self.cc_out[:, :]), "dcc")

    def all_reduce_max(self):
        self.all_reduce_max_trigger()
        self.all_reduce_max_fetch()

    def bcast_scale(self, src_col, s_col, inv_col, raw_col=None):
        inc, wait = self.inc, self.wait

        @self.on("tensor")
        def _(t):
            wait(t, "dcc")
            wait(t, "mv_")
            wait(t, "ms")
            inc(t.matmul(self.px[0:128, 0:1], self.ones_f[0:1, 0:128],
                         self.ccr_sb[0:1, src_col:src_col + 1], start=True, stop=True),
                "mt")

        @self.on("vector")
        def _(v):
            wait(v, "mt")
            if raw_col is not None:
                v.tensor_copy(out=self.scales[:, raw_col:raw_col + 1], in_=self.px[0:128, 0:1])
            v.tensor_scalar(out=self.scales[:, s_col:s_col + 1], in0=self.px[0:128, 0:1],
                            scalar1=1.0 / QMAX, scalar2=1e-12, op0=ALU.mult, op1=ALU.max)
            v.drain()
            v.reciprocal(out=self.scales[:, inv_col:inv_col + 1],
                         in_=self.scales[:, s_col:s_col + 1])
            v.drain()
            inc(v, "mv_")

    def r1_start(self):
        self.xpart_max(self.maxc[:, 0:NTT], 0)
        self.all_reduce_max_trigger()

    def r1_finish(self):
        self.all_reduce_max_fetch()
        self.bcast_scale(0, 0, 1)

        @self.on("vector")
        def _(v):
            self.wait(v, "mv_")
            v.tensor_scalar(out=self.scales[:, 17:18], in0=self.scales[:, 0:1],
                            scalar1=-CQ, scalar2=None, op0=ALU.mult)
            v.drain()
            self.inc(v, "mv_")
        self.r1_scales = self.CNT["mv_"]

    def p2(self):
        inc, dinc, wait = self.inc, self.dinc, self.wait
        mt_of = {}
        ev_of = {}
        self.last_e = {}
        last_e = self.last_e
        k_sg = {}
        kv_last = {}
        spillq = []
        self._spq = spillq
        kst = self.wkB[:, 0:2048].bitcast(BF16)

        uc_order = list(range(8)) + [x for cc in range(8) for x in (8 + cc, 16 + cc)]
        dq_jobs = [(self.wu_d, icq, self.wuT_sp) for icq in range(IC)] + \
                  [(self.wg_d, icq, self.wgT_sp) for icq in range(IC)]
        dq_next = [0]
        gsg = -1
        for it, uc in enumerate(uc_order):
            slot = it % 2
            dws = "dwg0" if slot == 0 else "dwg1"

            @self.on("gpsimd")
            def _(g, uc=uc, slot=slot, dws=dws, it=it):
                if it >= 2:
                    wait(g, "mt", mt_of[(it - 2) * 4 + 3])
                dinc(g.dma_start(out=self.wstream[:, slot, :, :],
                                 in_=self.wrkv_d[:, uc * 128:(uc + 1) * 128]
                                 .rearrange("(hc p) q -> p hc q", p=128)), dws)

            for sg in range(4):
                gsg += 1
                t0 = sg * 1024
                b0 = (gsg % 2) * 2

                @self.on("tensor")
                def _(t, uc=uc, slot=slot, dws=dws, it=it, sg=sg, gsg=gsg, t0=t0, b0=b0):
                    if sg == 0:
                        wait(t, dws)
                        if it == 0:
                            wait(t, "ms", self.p1_done)
                    if gsg >= 2:
                        sname, scnt = ev_of[gsg - 2]
                        wait(t, sname, scnt)
                    for hc in range(HC):
                        for tl in range(2):
                            li = t.matmul(self.bank(b0 + tl), self.wstream[:, slot, hc, :],
                                          self.xnA(hc, t0 + tl * 512, 512),
                                          start=(hc == 0), stop=(hc == HC - 1))
                    mt_of[gsg] = inc(li, "mt")

                if uc < 8:
                    if sg == 0:
                        while spillq:
                            spillq.pop(0)()

                    @self.on("scalar")
                    def _(s, uc=uc, sg=sg, gsg=gsg, t0=t0, b0=b0):
                        wait(s, "mt", mt_of[gsg])
                        esl = gsg % 4
                        if esl in last_e:
                            wait(s, last_e[esl][0], last_e[esl][1])
                        for tl in range(2):
                            s.activation(out=self.estage[:, esl, tl * 512:(tl + 1) * 512],
                                         in_=self.bank(b0 + tl), func=AF.Sigmoid,
                                         bias=self.brkv_t[:, uc:uc + 1])
                        ev_of[gsg] = ("ms", inc(s, "ms"))

                    def sp_r(uc=uc, sg=sg, gsg=gsg):
                        @self.on("sync")
                        def _(s, uc=uc, sg=sg, gsg=gsg):
                            esl = gsg % 4
                            des = DE_SEMS[esl]
                            wait(s, ev_of[gsg][0], ev_of[gsg][1])
                            dinc(s.dma_start(
                                out=self.r_att[sg * 8:(sg + 1) * 8, :, uc, :]
                                .rearrange("tc p n -> p tc n"),
                                in_=self.estage[:, esl, 0:1024]
                                .rearrange("p (tc n) -> p tc n", n=128)), des)
                            last_e[esl] = (des, self.CNT[des])
                    spillq.append(sp_r)
                elif uc < 16:
                    if sg == 0:
                        while spillq:
                            spillq.pop(0)()

                    @self.on("scalar")
                    def _(s, uc=uc, sg=sg, gsg=gsg, t0=t0, b0=b0):
                        wait(s, "mt", mt_of[gsg])
                        cc = uc - 8
                        if sg == 0 and cc >= 1:
                            wait(s, "mv_", kv_last[cc - 1])
                            dkp = "dk0" if (cc - 1) % 2 == 0 else "dk1"
                            wait(s, dkp, self.CNT[dkp])
                        for tl in range(2):
                            s.activation(out=kst[:, t0 + tl * 512:t0 + (tl + 1) * 512],
                                         in_=self.bank(b0 + tl), func=AF.Exp,
                                         bias=self.brkv_t[:, uc:uc + 1])
                        k_sg[(cc, sg)] = ev_of[gsg] = ("ms", inc(s, "ms"))

                    def sp_k(uc=uc, sg=sg, gsg=gsg, t0=t0):
                        @self.on("sync")
                        def _(s, uc=uc, sg=sg, gsg=gsg, t0=t0):
                            cc = uc - 8
                            wait(s, ev_of[gsg][0], ev_of[gsg][1])
                            dkp = "dk0" if cc % 2 == 0 else "dk1"
                            dinc(s.dma_start(out=self.u_sp[8 + cc, :, t0:t0 + 1024],
                                             in_=kst[:, t0:t0 + 1024]), dkp)
                            dinc(s.dma_start(
                                out=self.u_att[sg * 8:(sg + 1) * 8, :, 8 + cc, :]
                                .rearrange("tc p n -> p tc n"),
                                in_=kst[:, t0:t0 + 1024]
                                .rearrange("p (tc n) -> p tc n", n=128)), dkp)
                    spillq.append(sp_k)
                else:
                    @self.on("vector")
                    def _(v, uc=uc, sg=sg, gsg=gsg, t0=t0, b0=b0):
                        cc = uc - 16
                        wait(v, "mt", mt_of[gsg])
                        sname, scnt = k_sg[(cc, sg)]
                        wait(v, sname, scnt)
                        esl = gsg % 4
                        if esl in last_e:
                            wait(v, last_e[esl][0], last_e[esl][1])
                        for tl in range(2):
                            v.scalar_tensor_tensor(
                                out=self.estage[:, esl, tl * 512:(tl + 1) * 512],
                                in0=self.bank(b0 + tl), scalar=self.brkv_t[:, uc:uc + 1],
                                in1=kst[:, t0 + tl * 512:t0 + (tl + 1) * 512],
                                op0=ALU.add, op1=ALU.mult)
                        kv_last[cc] = inc(v, "mv_")
                        ev_of[gsg] = ("mv_", kv_last[cc])

                    def sp_v(uc=uc, sg=sg, gsg=gsg, t0=t0):
                        @self.on("sync")
                        def _(s, uc=uc, sg=sg, gsg=gsg, t0=t0):
                            cc = uc - 16
                            wait(s, "mv_", kv_last[cc])
                            esl = gsg % 4
                            des = DE_SEMS[esl]
                            dinc(s.dma_start(
                                out=self.u_sp[cc, :, t0:t0 + 1024],
                                in_=self.estage[:, esl, 0:1024]), des)
                            dinc(s.dma_start(
                                out=self.u_att[sg * 8:(sg + 1) * 8, :, cc, :]
                                .rearrange("tc p n -> p tc n"),
                                in_=self.estage[:, esl, 0:1024]
                                .rearrange("p (tc n) -> p tc n", n=128)), des)
                            last_e[esl] = (des, self.CNT[des])
                    spillq.append(sp_v)
                if len(spillq) >= 3:
                    spillq.pop(0)()

            want = ((it + 1) * len(dq_jobs)) // len(uc_order)
            while dq_next[0] < want:
                n = dq_next[0]
                srcw, icq, dest = dq_jobs[n]
                self.dq_front(srcw, icq, 1024, self.wf_ug,
                              self.wq_ug2[n % 3], n % 2)
                if n >= 2:
                    picq = dq_jobs[n - 2][1]
                    self.dq_back(self._dq["g"] - 2, picq, 1024, "hbm",
                                 dq_jobs[n - 2][2], self.wq_ug2[(n - 2) % 3])
                dq_next[0] += 1

        while dq_next[0] < len(dq_jobs):
            n = dq_next[0]
            srcw, icq, dest = dq_jobs[n]
            self.dq_front(srcw, icq, 1024, self.wf_ug, self.wq_ug2[n % 3], n % 2)
            if n >= 2:
                picq = dq_jobs[n - 2][1]
                self.dq_back(self._dq["g"] - 2, picq, 1024, "hbm",
                             dq_jobs[n - 2][2], self.wq_ug2[(n - 2) % 3])
            dq_next[0] += 1
        for n in (len(dq_jobs) - 2, len(dq_jobs) - 1):
            self.dq_back(self._dq["g"] - (len(dq_jobs) - 1 - n), dq_jobs[n][1], 1024,
                         "hbm", dq_jobs[n][2], self.wq_ug2[n % 3])
        while spillq:
            spillq.pop(0)()
        self._spq = None
        self.p2_mt_last = self.CNT["mt"]
        self.p2_ev_ms = self.CNT["ms"]
        self.p2_done_marks = (self.CNT["mv_"], tuple(self.CNT[d] for d in DE_SEMS),
                              self.CNT["dk0"], self.CNT["dk1"])

    def dq_init(self):
        self._dq = {"vdone": {}, "tdone": {}, "sdone": {}, "meta": {}, "g": -1}
        wkb = self.wkB[:, :]
        self.wf_ug = wkb[:, 2048:4096].rearrange("p (s n) -> p s n", s=2)
        self.wq_ug2 = [wkb[:, 4096:4608].bitcast(BF16),
                       wkb[:, 4608:5120].bitcast(BF16),
                       wkb[:, 5120:5632].bitcast(BF16)]

    def dq_chunk(self, srcw, icq, I, kind, dest, wf, wq, islot):
        self.dq_front(srcw, icq, I, wf, wq, islot, guard=1)
        self.dq_back(self._dq["g"], icq, I, kind, dest, wq)

    def dq_front(self, srcw, icq, I, wf, wq, islot, guard=3):
        inc, dinc, wait = self.inc, self.dinc, self.wait
        D = self._dq
        D["g"] += 1
        g = D["g"]
        D["meta"][g] = (icq, I, wq)
        vdone, tdone, sdone = D["vdone"], D["tdone"], D["sdone"]
        nblk = I // 32
        nb = I // 128
        slot = g % 2
        dbs = "db0" if slot == 0 else "db1"

        @self.on("sync")
        def _(s, srcw=srcw, icq=icq, slot=slot, islot=islot, dbs=dbs, I=I, g=g, wf=wf):
            if g >= 2:
                wait(s, "mv_", vdone[g - 2])
            dinc(s.dma_start(out=wf[:, islot, 0:I],
                             in_=srcw[icq * 128:(icq + 1) * 128, :]), dbs)

        @self.on("vector")
        def _(v, slot=slot, islot=islot, dbs=dbs, I=I, nblk=nblk, g=g, wf=wf):
            wait(v, dbs)
            v.tensor_reduce(out=self.bmax[:, 0, 0:nblk],
                            in_=wf[:, islot, 0:I].rearrange("p (b k) -> p b k", k=32),
                            axis=AX, op=ALU.max, apply_absolute_value=True)
            v.drain()
            v.tensor_scalar(out=self.bs[:, 0, 0:nblk],
                            in0=self.bmax[:, 0, 0:nblk],
                            scalar1=1.0 / 7.0, scalar2=1e-12,
                            op0=ALU.mult, op1=ALU.max)
            v.drain()
            v.reciprocal(out=self.binv[:, 0, 0:nblk],
                         in_=self.bs[:, 0, 0:nblk])
            v.drain()
            v.tensor_tensor(out=wf[:, islot, 0:I].rearrange("p (b k) -> p b k", k=32),
                            in0=wf[:, islot, 0:I].rearrange("p (b k) -> p b k", k=32),
                            in1=self.binv[:, 0, 0:nblk].broadcast_to((128, nblk, 32)),
                            op=ALU.mult)
            inc(v, "mv_")

        for q0 in range(0, I, 1024):
            @self.on("vector")
            def _(v, slot=slot, islot=islot, q0=q0, nblk=nblk, g=g, wf=wf, wq=wq,
                  guard=guard):
                if g >= guard and q0 == 0:
                    wait(v, "mt", tdone[g - guard])
                fq = self.fdq[:, 0:1024]
                v.tensor_scalar(out=self.iscr[:, 0:1024],
                                in0=wf[:, islot, q0:q0 + 1024],
                                scalar1=8.0, scalar2=None, op0=ALU.add)
                v.drain()
                v.scalar_tensor_tensor(out=fq, in0=self.iscr[:, 0:1024],
                                       scalar=7.5, in1=wf[:, islot, q0:q0 + 1024],
                                       op0=ALU.subtract, op1=ALU.is_le)
                v.drain()
                v.scalar_tensor_tensor(out=fq, in0=self.iscr[:, 0:1024],
                                       scalar=8.0, in1=fq,
                                       op0=ALU.subtract, op1=ALU.add)
                v.drain()
                b0 = q0 // 32
                v.tensor_tensor(out=wq[:, q0:q0 + 1024]
                                .rearrange("p (b k) -> p b k", k=32),
                                in0=fq.rearrange("p (b k) -> p b k", k=32),
                                in1=self.bs[:, 0, b0:b0 + 32]
                                .broadcast_to((128, 32, 32)),
                                op=ALU.mult)
                inc(v, "mv_")
        vdone[g] = self.CNT["mv_"]

    def dq_back(self, g, icq, I, kind, dest, wq):
        inc, dinc, wait = self.inc, self.dinc, self.wait
        D = self._dq
        vdone, tdone, sdone = D["vdone"], D["tdone"], D["sdone"]
        nb = I // 128
        for gq in range(nb // 8):
            @self.on("tensor")
            def _(t, nb=nb, g=g, gq=gq, wq=wq):
                if gq == 0:
                    wait(t, "mv_", vdone[g])
                    if g >= 1:
                        wait(t, "ms", sdone[g - 1])
                else:
                    wait(t, "ms")
                for bq in range(gq * 8, gq * 8 + 8):
                    li = t.transpose(self.pt[:, (bq % 8) * 128:(bq % 8) * 128 + 128],
                                     wq[:, bq * 128:(bq + 1) * 128],
                                     self.ident[:, :])
                inc(li, "mt")

            if kind != "sbuf" and getattr(self, "_spq", None) is not None:
                while self._spq:
                    self._spq.pop(0)()

            @self.on("scalar")
            def _(s, icq=icq, nb=nb, g=g, gq=gq, kind=kind, dest=dest):
                wait(s, "mt")
                if kind == "sbuf":
                    s.activation(
                        out=self.wdT.rearrange("p (b h) -> p b h", h=1024)
                        [:, gq * 8:(gq + 1) * 8, icq * 128:(icq + 1) * 128],
                        in_=self.pt[:, :].rearrange("p (b q) -> p b q", q=128),
                        func=AF.Identity)
                    inc(s, "ms")
                else:
                    wait(s, "dq0", self.CNT["dq0"])
                    wait(s, "dq1", self.CNT["dq1"])
                    for half in range(2):
                        s.activation(out=self.pdqst[:, half, 0:512],
                                     in_=self.pt[:, half * 512:(half + 1) * 512],
                                     func=AF.Identity)
                    inc(s, "ms")

            if kind != "sbuf":
                dqm = self.CNT["ms"]

                def sp_dq(icq=icq, g=g, gq=gq, dest=dest, dqm=dqm):
                    @self.on("sync")
                    def _(s, icq=icq, g=g, gq=gq, dest=dest, dqm=dqm):
                        wait(s, "ms", dqm)
                        for half in range(2):
                            dws = "dq0" if half == 0 else "dq1"
                            dinc(s.dma_start(
                                out=dest[icq, gq * 8 + half * 4:gq * 8 + half * 4 + 4, :, :]
                                .rearrange("h p q -> p h q"),
                                in_=self.pdqst[:, half, 0:512]
                                .rearrange("p (h q) -> p h q", q=128)), dws)
                if getattr(self, "_spq", None) is not None:
                    self._spq.append(sp_dq)
                else:
                    sp_dq()
        tdone[g] = self.CNT["mt"]
        sdone[g] = self.CNT["ms"]

    def pdq_down(self):
        p2_ev_last, p2_de, p2_dk0, p2_dk1 = self.p2_done_marks
        wait = self.wait

        @self.on("sync")
        def _(s):
            wait(s, "mv_", p2_ev_last)
            wait(s, "ms", self.p2_ev_ms)
            wait(s, "dk0", p2_dk0)
            wait(s, "dk1", p2_dk1)
            for dd, vv in zip(DE_SEMS, p2_de):
                wait(s, dd, vv)
            wait(s, "mt", self._dq["tdone"][self._dq["g"]])
        wf_dn = self.wkB[:, 0:6144].rearrange("p (s n) -> p s n", s=2)
        wq_dn = self.wkB[:, 6144:7680].bitcast(BF16)
        for icq in range(HC):
            self.dq_chunk(self.wd_d, icq, 3072, "sbuf", None, wf_dn, wq_dn, icq % 2)
        self.pdq_done = self.CNT["mv_"]
        self.dq_spill = (self.CNT["dq0"], self.CNT["dq1"])

    def p3(self):
        inc, dinc, wait = self.inc, self.dinc, self.wait
        scan_done = {}

        for c in range(16):
            slot = c % 2
            das = "da0" if slot == 0 else "da1"

            @self.on("sync")
            def _(s, c=c, slot=slot, das=das):
                if c == 0:
                    wait(s, "mv_", self.pdq_done)
                    for dd, vv in zip(DE_SEMS, self.p2_done_marks[1]):
                        wait(s, dd, vv)
                    wait(s, "dk0", self.p2_done_marks[2])
                    wait(s, "dk1", self.p2_done_marks[3])
                if c >= 2:
                    wait(s, "mv_", scan_done[c - 2])
                dinc(s.dma_start(out=self.ubuf[:, slot, :], in_=self.u_sp[c, :, :]), das)

            @self.on("vector")
            def _(v, c=c, slot=slot, das=das):
                wait(v, das)
                v.tensor_tensor_scan(out=self.ydisc[:, :],
                                     data0=self.lam_t[:, c % 8:c % 8 + 1]
                                     .broadcast_to((128, T)),
                                     data1=self.ubuf[:, slot, :],
                                     initial=0.0, op0=ALU.mult, op1=ALU.add)
                v.drain()
                v.tensor_copy(out=self.S_loc[:, c:c + 1], in_=self.ydisc[:, T - 1:T])
                v.drain()
                scan_done[c] = inc(v, "mv_")

        @self.on("tensor")
        def _(t):
            wait(t, "mv_")
            inc(t.transpose(self.px[0:16, 0:128], self.S_loc[:, :], self.identf[:, :]),
                "mt")

        @self.on("vector")
        def _(v):
            wait(v, "mt")
            v.tensor_copy(out=self.ag_sb[0:16, :], in_=self.px[0:16, 0:128])
            v.drain()
            inc(v, "mv_")

        @self.on("scalar")
        def _(s):
            wait(s, "mv_")
            dinc(s.dma_start(out=self.ag_in[:, :], in_=self.ag_sb[0:16, :]), "dci")

        @self.on("gpsimd")
        def _(g):
            wait(g, "dci")
            inc(g.collective_compute("AllGather", ALU.bypass,
                                     replica_groups=[list(range(8))],
                                     ins=[self.ag_in.opt()], outs=[self.ag_out.opt()]),
                "mc")
            wait(g, "mc")
            dinc(g.dma_start(out=self.ag_sb[:, :], in_=self.ag_out[:, :]), "dcc")

        @self.on("tensor")
        def _(t):
            wait(t, "dcc")
            inc(t.matmul(self.px[0:128, 0:16], self.ag_sb[:, :], self.sel_sb[:, :],
                         start=True, stop=True), "mt")

        @self.on("vector")
        def _(v):
            wait(v, "mt")
            v.tensor_copy(out=self.S_prev[:, :], in_=self.px[0:128, 0:16])
            v.drain()
            self.sprev_ready = inc(v, "mv_")

    def p4(self):
        inc, wait = self.inc, self.wait
        prev = {}
        for st in range(32):
            @self.on("scalar")
            def _(s, st=st):
                if st == 0:
                    wait(s, "mv_", self.r1_scales)
                    wait(s, "mt", self.p2_mt_last)
                s.activation(out=self.iscr[:, (st % 2) * 1024:(st % 2) * 1024 + 1024],
                             in_=self.arenaA[:, st * 1024:(st + 1) * 1024],
                             func=AF.Identity, scale=self.scales[:, 1:2],
                             bias=self.consts[:, 0:1])
                s.drain()
                s.activation(out=self.arenaA[:, st * 1024:(st + 1) * 1024],
                             in_=self.iscr[:, (st % 2) * 1024:(st % 2) * 1024 + 1024],
                             func=AF.Identity, scale=self.scales[:, 0:1],
                             bias=self.scales[:, 17:18])
                s.drain()
                prev[st] = inc(s, "ms")
        self.p4_done = ("ms", self.CNT["ms"])

    def p5(self):
        inc, dinc, wait = self.inc, self.dinc, self.wait
        mt_of = {}
        mvred = {}
        ev_mm = {}
        spillq = []
        self.p65_init()
        self.ring_init()
        att_next = [0]
        git = -1
        gsg = -1

        for wi, (wT, spill, mcols) in enumerate(
                [(self.wuT_sp, self.up_sp, self.upmaxc),
                 (self.wgT_sp, self.gate_sp, self.gatemaxc)]):
            for ic in range(IC):
                git += 1
                slot = git % 2
                dws = "dw0" if slot == 0 else "dw1"

                @self.on("sync")
                def _(s, ic=ic, slot=slot, dws=dws, git=git, wT=wT):
                    if git < 2:
                        wait(s, "dq0", self.dq_spill[0])
                        wait(s, "dq1", self.dq_spill[1])
                    if git >= 2:
                        wait(s, "mt", mt_of[(git - 2, 1)])
                    dinc(s.dma_start(out=self.wstream[:, slot, :, :],
                                     in_=wT[ic, :, :, :].rearrange("h p q -> p h q")),
                         dws)

                for g2 in range(2):
                    banks = self.ralloc(4)
                    bmarks = [self.rmark[b] for b in banks]
                    t0 = g2 * 2048

                    @self.on("tensor")
                    def _(t, ic=ic, slot=slot, dws=dws, git=git, g2=g2,
                          banks=banks, bmarks=bmarks, t0=t0):
                        if g2 == 0:
                            wait(t, dws)
                            if git == 0:
                                wait(t, "ms")
                                wait(t, "mv_")
                                wait(t, "mt", self.p2_mt_last)
                        for m in bmarks:
                            if m is not None:
                                wait(t, m[0], m[1])
                        for hc in range(HC):
                            for j in range(4):
                                li = t.matmul(self.rbank[banks[j]],
                                              self.wstream[:, slot, hc, :],
                                              self.xnA(hc, t0 + j * 512, 512),
                                              start=(hc == 0), stop=(hc == HC - 1))
                        mt_of[(git, g2)] = inc(li, "mt")

                    for pj in range(2):
                        gsg += 1
                        esl = gsg % 4
                        des = DE_SEMS[esl]
                        b0_, b1_ = banks[2 * pj], banks[2 * pj + 1]
                        tt0 = t0 + pj * 1024

                        @self.on("scalar")
                        def _(s, ic=ic, git=git, g2=g2, gsg=gsg, esl=esl, des=des,
                              b0_=b0_, b1_=b1_, tt0=tt0, spill=spill):
                            wait(s, "mt", mt_of[(git, g2)])
                            le = self.last_e.get(esl)
                            if le is not None:
                                wait(s, le[0], le[1])
                            if gsg >= 4:
                                wait(s, "mv_", mvred[gsg - 4])
                            s.activation(out=self.estage[:, esl, 0:512],
                                         in_=self.rbank[b0_], func=AF.Identity)
                            s.activation(out=self.estage[:, esl, 512:1024],
                                         in_=self.rbank[b1_], func=AF.Identity)
                            m = ("ms", inc(s, "ms"))
                            self.rmark[b0_] = m
                            self.rmark[b1_] = m
                            ev_mm[gsg] = m[1]
                        spillq.append((ic, gsg, esl, des, tt0, spill))
                        if len(spillq) >= 3:
                            sic, sgsg, sesl, sdes, stt0, sspill = spillq.pop(0)

                            @self.on("sync")
                            def _(s, sic=sic, sgsg=sgsg, sesl=sesl, sdes=sdes,
                                  stt0=stt0, sspill=sspill):
                                wait(s, "ms", ev_mm[sgsg])
                                self.last_e[sesl] = (sdes, dinc(s.dma_start(
                                    out=sspill[sic, :, stt0:stt0 + 1024],
                                    in_=self.estage[:, sesl, 0:1024]), sdes))

                        @self.on("vector")
                        def _(v, gsg=gsg, esl=esl, mcols=mcols, git=git, g2=g2,
                              pj=pj, wi=wi):
                            m = self.rmark[0]
                            wait(v, "ms", self.CNT["ms"])
                            lit = (git - wi * IC) * 4 + g2 * 2 + pj
                            v.tensor_reduce(out=mcols[:, lit:lit + 1],
                                            in_=self.estage[:, esl, 0:1024],
                                            axis=AX, op=ALU.max,
                                            apply_absolute_value=True)
                            mvred[gsg] = inc(v, "mv_")

                want = ((git + 1) * NTT) // (2 * IC)
                while att_next[0] < want:
                    self.p65_front(att_next[0])
                    if att_next[0] >= 1:
                        self.p65_back(att_next[0] - 1)
                    att_next[0] += 1

        while spillq:
            sic, sgsg, sesl, sdes, stt0, sspill = spillq.pop(0)

            @self.on("sync")
            def _(s, sic=sic, sgsg=sgsg, sesl=sesl, sdes=sdes, stt0=stt0,
                  sspill=sspill):
                wait(s, "ms", ev_mm[sgsg])
                self.last_e[sesl] = (sdes, dinc(s.dma_start(
                    out=sspill[sic, :, stt0:stt0 + 1024],
                    in_=self.estage[:, sesl, 0:1024]), sdes))

        while att_next[0] < NTT:
            self.p65_front(att_next[0])
            if att_next[0] >= 1:
                self.p65_back(att_next[0] - 1)
            att_next[0] += 1
        self.p65_back(NTT - 1)
        self.p65_finish()
        self.p5_done_ms = self.CNT["ms"]
        self.p5_done_mv = self.CNT["mv_"]
        self.p5_done_de = tuple(self.CNT[d] for d in DE_SEMS)

    def r2(self):
        inc, wait = self.inc, self.wait
        self.xpart_max(self.upmaxc[:, 0:96], 1)
        self.xpart_max(self.gatemaxc[:, 0:96], 2)
        self.all_reduce_max()
        self.bcast_scale(1, 2, 3)
        self.bcast_scale(2, 4, 5)

        @self.on("vector")
        def _(v):
            wait(v, "mv_")
            v.tensor_scalar(out=self.scales[:, 10:11], in0=self.scales[:, 4:5],
                            scalar1=-CQ, scalar2=None, op0=ALU.mult)
            v.tensor_scalar(out=self.scales[:, 11:12], in0=self.scales[:, 4:5],
                            scalar1=1.0 / 65536.0, scalar2=None, op0=ALU.mult)
            v.tensor_scalar(out=self.scales[:, 16:17], in0=self.scales[:, 2:3],
                            scalar1=-CQ, scalar2=None, op0=ALU.mult)
            v.drain()
            self.r2_scales = inc(v, "mv_")

    def p6(self):
        inc, dinc, wait = self.inc, self.dinc, self.wait
        gi_mark = {}
        sig_mark = {}
        g2_mark = {}
        done = {}
        W = 2048
        NQ = IC * 2
        arf = self.arenaA[:, :].bitcast(F32)
        sigb = [arf[:, 0:W], arf[:, W:2 * W]]
        svb = [arf[:, 2 * W:3 * W], arf[:, 3 * W:4 * W]]
        esf = self.estage[:, :, :].rearrange("p a n -> p (a n)")
        est = [esf[:, 0:W], esf[:, W:2 * W]]
        le6 = {0: [self.last_e[0], self.last_e[1]],
               1: [self.last_e[2], self.last_e[3]]}

        def load(ic):
            slot = ic % 2
            das = "da0" if slot == 0 else "da1"

            @self.on("sync")
            def _(s, ic=ic, slot=slot, das=das):
                if ic == 0:
                    wait(s, "mv_", self.p65_done_mv)
                    for dd, vv in zip(DE_SEMS, self.p5_done_de):
                        wait(s, dd, vv)
                if ic >= 2:
                    wait(s, "mv_", done[ic - 2])
                dinc(s.dma_start(out=self.gbuf[:, slot, :],
                                 in_=self.gate_sp[ic, :, :]), das)

        def gi_stage(Q):
            ic, q = Q // 2, Q % 2
            slot = ic % 2
            das = "da0" if slot == 0 else "da1"
            Qp = Q % 2
            o0 = q * W

            @self.on("vector")
            def _(v, ic=ic, slot=slot, das=das, o0=o0, Q=Q, Qp=Qp):
                if o0 == 0:
                    wait(v, das)
                if Q >= 2:
                    wait(v, "ms", sig_mark[Q - 2])
                v.tensor_scalar(out=self.iscr[:, Qp * W:Qp * W + W],
                                in0=self.gbuf[:, slot, o0:o0 + W],
                                scalar1=self.scales[:, 5:6],
                                scalar2=self.consts[:, 0:1],
                                op0=ALU.mult, op1=ALU.add)
                gi_mark[Q] = inc(v, "mv_")

        def sig_stage(Q):
            Qp = Q % 2

            @self.on("scalar")
            def _(s, Q=Q, Qp=Qp):
                wait(s, "mv_", gi_mark[Q])
                if Q >= 2:
                    wait(s, "mv_", g2_mark[Q - 2])
                s.activation(out=sigb[Qp], in_=self.iscr[:, Qp * W:Qp * W + W],
                             func=AF.Sigmoid, scale=self.scales[:, 4:5],
                             bias=self.scales[:, 10:11])
                s.drain()
                s.activation(out=svb[Qp].bitcast(I32), in_=sigb[Qp],
                             func=AF.Identity, scale=65536.0,
                             bias=self.consts[:, 1:2])
                sig_mark[Q] = inc(s, "ms")

        def tail_stage(Q):
            Qp = Q % 2
            ic = Q // 2

            @self.on("vector")
            def _(v, ic=ic, Q=Q, Qp=Qp):
                wait(v, "ms", sig_mark[Q])
                for le in le6[Qp]:
                    wait(v, le[0], le[1])
                v.tensor_scalar(out=sigb[Qp], in0=svb[Qp].bitcast(I32),
                                scalar1=65535.0, scalar2=self.scales[:, 11:12],
                                op0=ALU.min, op1=ALU.mult)
                v.drain()
                v.scalar_tensor_tensor(out=est[Qp],
                                       in0=self.iscr[:, Qp * W:Qp * W + W],
                                       scalar=CQ, in1=sigb[Qp],
                                       op0=ALU.subtract, op1=ALU.mult)
                v.drain()
                v.tensor_reduce(out=self.g2maxc[:, Q:Q + 1],
                                in_=est[Qp], axis=AX,
                                op=ALU.max, apply_absolute_value=True)
                g2_mark[Q] = done[ic] = inc(v, "mv_")

        def dma_stage(Q):
            ic, q = Q // 2, Q % 2
            Qp = Q % 2
            o0 = q * W

            @self.on("sync")
            def _(s, ic=ic, o0=o0, Q=Q, Qp=Qp):
                wait(s, "mv_", g2_mark[Q])
                des = DE_SEMS[Qp]
                m = (des, dinc(s.dma_start(
                    out=self.g2_sp[ic, :, o0:o0 + W],
                    in_=est[Qp]), des))
                le6[Qp] = [m]
                self.last_e[Qp] = m

        for Q in range(NQ):
            if Q % 2 == 0:
                load(Q // 2)
            gi_stage(Q)
            sig_stage(Q)
            if Q >= 1:
                tail_stage(Q - 1)
            if Q >= 2:
                dma_stage(Q - 2)
        tail_stage(NQ - 1)
        dma_stage(NQ - 2)
        dma_stage(NQ - 1)

        self.p6_done = self.CNT["mv_"]
        self.p6_de = tuple(self.CNT[d] for d in DE_SEMS)

    def r3(self):
        self.xpart_max(self.g2maxc[:, 0:48], 3)
        self.all_reduce_max()
        self.bcast_scale(3, 6, 7)
        self.r3_scales = self.CNT["mv_"]

    def p65_init(self):
        self._a65 = {"blk_done": {}, "ev_prev": {}, "mt65": {}, "dma": {}}

    def p65_front(self, tc):
        inc, dinc, wait = self.inc, self.dinc, self.wait
        blk_done = self._a65["blk_done"]
        mt65 = self._a65["mt65"]
        slot = tc % 2
        das = "da0" if slot == 0 else "da1"
        dbs = "db0" if slot == 0 else "db1"
        t0 = tc * 128

        @self.on("sync")
        def _(s, tc=tc, slot=slot, das=das, dbs=dbs, t0=t0):
            if tc == 0:
                wait(s, "mv_", self.sprev_ready)
                for dd, vv in zip(DE_SEMS, self.p2_done_marks[1]):
                    wait(s, dd, vv)
                wait(s, "dk0", self.p2_done_marks[2])
                wait(s, "dk1", self.p2_done_marks[3])
            if tc >= 2:
                wait(s, "mv_", blk_done[tc - 2])
            dinc(s.dma_start(out=self.att_u[:, slot, :],
                             in_=self.u_att[tc, :, :, :].rearrange("p c n -> p (c n)")),
                 das)
            dinc(s.dma_start(out=self.att_r[:, slot, :],
                             in_=self.r_att[tc, :, :, :].rearrange("p c n -> p (c n)")),
                 dbs)

        @self.on("vector")
        def _(v, tc=tc, slot=slot, das=das, dbs=dbs):
            wait(v, das)
            wait(v, dbs)
            for c in range(16):
                init = self.S_prev[:, c:c + 1] if tc == 0 else self.carries[:, c:c + 1]
                v.tensor_tensor_scan(
                    out=self.att_y[:, c * 128:(c + 1) * 128],
                    data0=self.lam_t[:, c % 8:c % 8 + 1].broadcast_to((128, 128)),
                    data1=self.att_u[:, slot, c * 128:(c + 1) * 128],
                    initial=init, op0=ALU.mult, op1=ALU.add)
            v.drain()
            v.tensor_copy(out=self.carries[:, :],
                          in_=self.att_y[:, :].rearrange("p (c n) -> p c n", n=128)
                          [:, :, 127:128].rearrange("p c n -> p (c n)"))
            v.drain()
            for c in range(8):
                v.scalar_tensor_tensor(
                    out=self.att_n[:, c * 128:(c + 1) * 128],
                    in0=self.att_u[:, slot, c * 128:(c + 1) * 128],
                    scalar=self.gamma_t[:, c:c + 1],
                    in1=self.att_y[:, c * 128:(c + 1) * 128],
                    op0=ALU.mult, op1=ALU.add)
            for c in range(8):
                v.scalar_tensor_tensor(
                    out=self.att_d[:, c * 128:(c + 1) * 128],
                    in0=self.att_u[:, slot, (8 + c) * 128:(9 + c) * 128],
                    scalar=self.gamma_t[:, c:c + 1],
                    in1=self.att_y[:, (8 + c) * 128:(9 + c) * 128],
                    op0=ALU.mult, op1=ALU.add)
            v.drain()
            v.tensor_scalar(out=self.att_d[:, :], in0=self.att_d[:, :], scalar1=1e-6,
                            scalar2=None, op0=ALU.add)
            v.drain()
            v.reciprocal(out=self.att_y[:, 0:1024], in_=self.att_d[:, :])
            v.drain()
            v.tensor_tensor(out=self.att_n[:, :], in0=self.att_n[:, :],
                            in1=self.att_y[:, 0:1024], op=ALU.mult)
            v.drain()
            if tc >= 2:
                wait(v, "mt", mt65[tc - 2])
            v.tensor_tensor(out=self.ratio[:, slot, :, :].rearrange("p c n -> p (c n)"),
                            in0=self.att_n[:, :], in1=self.att_r[:, slot, :],
                            op=ALU.mult)
            blk_done[tc] = inc(v, "mv_")

    def p65_back(self, tc):
        inc, dinc, wait = self.inc, self.dinc, self.wait
        blk_done = self._a65["blk_done"]
        ev_prev = self._a65["ev_prev"]
        mt65 = self._a65["mt65"]
        adma = self._a65["dma"]
        slot = tc % 2

        banks = self.ralloc(2)
        bmarks = [self.rmark[b] for b in banks]

        @self.on("tensor")
        def _(t, tc=tc, banks=banks, bmarks=bmarks):
            wait(t, "mv_", blk_done[tc])
            if tc == 0:
                wait(t, "dg", self.dg_wo)
            for m in bmarks:
                if m is not None:
                    wait(t, m[0], m[1])
            for c in range(8):
                t.matmul(self.rbank[banks[0]], self.ratio[:, slot, c, :],
                         self.wo_sb[:, c * 1024:c * 1024 + 512],
                         start=(c == 0), stop=False)
                t.matmul(self.rbank[banks[1]], self.ratio[:, slot, c, :],
                         self.wo_sb[:, c * 1024 + 512:(c + 1) * 1024],
                         start=(c == 0), stop=False)
            t.matmul(self.rbank[banks[0]], self.ones_b[0:1, 0:128],
                     self.bo_row[0:1, 0:512], start=False, stop=True)
            li = t.matmul(self.rbank[banks[1]], self.ones_b[0:1, 0:128],
                          self.bo_row[0:1, 512:1024], start=False, stop=True)
            mt65[tc] = inc(li, "mt")

        @self.on("scalar")
        def _(s, tc=tc, slot=slot, banks=banks):
            wait(s, "mt", mt65[tc])
            if tc >= 2:
                wait(s, adma[tc - 2][0], adma[tc - 2][1])
            s.activation(out=self.xnstage[:, slot, 0:512],
                         in_=self.rbank[banks[0]], func=AF.Identity)
            s.activation(out=self.xnstage[:, slot, 512:1024],
                         in_=self.rbank[banks[1]], func=AF.Identity)
            m = ("ms", inc(s, "ms"))
            self.rmark[banks[0]] = m
            self.rmark[banks[1]] = m
            ev_prev[tc] = m[1]

        @self.on("sync")
        def _(s, tc=tc, slot=slot):
            wait(s, "ms", ev_prev[tc])
            dys = "dy0" if slot == 0 else "dy1"
            adma[tc] = (dys, dinc(s.dma_start(
                out=self.att_sp[tc, :, :], in_=self.xnstage[:, slot, :]), dys))

    def p65_finish(self):
        self.p65_done_ms = self.CNT["ms"]
        self.p65_done_mv = self.CNT["mv_"]
        self.p65_att_dy = (self.CNT["dy0"], self.CNT["dy1"])

    def p675(self):
        inc, dinc, wait = self.inc, self.dinc, self.wait
        done = {}
        ci_mark = {}
        pv_mark = {}

        def spill(Q):
            ic, q = Q // 4, Q % 4

            @self.on("sync")
            def _(s, ic=ic, q=q, Q=Q):
                wait(s, "mv_", pv_mark[Q])
                Ep = Q % 4
                des = DE_SEMS[Ep]
                self.last_e[Ep] = (des, dinc(s.dma_start(
                    out=self.o_att[q * 8:(q + 1) * 8, :, ic, :]
                    .rearrange("tc p n -> p tc n"),
                    in_=self.estage[:, Ep, 0:1024]
                    .rearrange("p (tc n) -> p tc n", n=128)), des))

        for ic in range(IC):
            slot = ic % 2
            das = "da0" if slot == 0 else "da1"
            dbs = "db0" if slot == 0 else "db1"

            @self.on("sync")
            def _(s, ic=ic, slot=slot, das=das, dbs=dbs):
                if ic == 0:
                    for dd, vv in zip(DE_SEMS, self.p6_de):
                        wait(s, dd, vv)
                    wait(s, "dy0", self.p65_att_dy[0])
                    wait(s, "dy1", self.p65_att_dy[1])
                if ic >= 2:
                    wait(s, "mv_", done[ic - 2])
                dinc(s.dma_start(out=self.g2in[:, slot, :], in_=self.g2_sp[ic, :, :]), das)
                dinc(s.dma_start(out=self.upin[:, slot, :], in_=self.up_sp[ic, :, :]), dbs)

            for q in range(4):
                Q = ic * 4 + q
                Qp = Q % 2
                o0 = q * 1024

                @self.on("scalar")
                def _(s, ic=ic, slot=slot, das=das, dbs=dbs, o0=o0, Q=Q, Qp=Qp):
                    if Q == 0:
                        wait(s, "mv_", self.r3_scales)
                    if o0 == 0:
                        wait(s, das)
                        wait(s, dbs)
                    if Q >= 2:
                        wait(s, "mv_", pv_mark[Q - 2])
                    s.activation(out=self.iscr[:, Qp * 2048 + 1024:Qp * 2048 + 2048],
                                 in_=self.upin[:, slot, o0:o0 + 1024],
                                 func=AF.Identity, scale=self.scales[:, 3:4],
                                 bias=self.consts[:, 0:1])
                    s.activation(out=self.iscr[:, Qp * 2048:Qp * 2048 + 1024],
                                 in_=self.g2in[:, slot, o0:o0 + 1024],
                                 func=AF.Identity, scale=self.scales[:, 7:8],
                                 bias=self.consts[:, 0:1])
                    if Q % 2 == 0:
                        s.drain()
                        s.activation(out=self.upq_t[:, Qp * 1024:Qp * 1024 + 1024],
                                     in_=self.iscr[:, Qp * 2048 + 1024:Qp * 2048 + 2048],
                                     func=AF.Identity, scale=self.scales[:, 2:3],
                                     bias=self.scales[:, 16:17])
                    ci_mark[Q] = inc(s, "ms")

                @self.on("vector")
                def _(v, ic=ic, o0=o0, Q=Q, Qp=Qp):
                    wait(v, "ms", ci_mark[Q])
                    Ep = Q % 4
                    le = self.last_e[Ep]
                    wait(v, le[0], le[1])
                    if Q % 2 == 1:
                        v.tensor_scalar(out=self.upq_t[:, Qp * 1024:Qp * 1024 + 1024],
                                        in0=self.iscr[:, Qp * 2048 + 1024:Qp * 2048 + 2048],
                                        scalar1=CQ, scalar2=self.scales[:, 2:3],
                                        op0=ALU.subtract, op1=ALU.mult)
                        v.drain()
                    v.scalar_tensor_tensor(out=self.estage[:, Ep, 0:1024],
                                           in0=self.iscr[:, Qp * 2048:Qp * 2048 + 1024],
                                           scalar=CQ,
                                           in1=self.upq_t[:, Qp * 1024:Qp * 1024 + 1024],
                                           op0=ALU.subtract, op1=ALU.mult)
                    v.drain()
                    v.tensor_reduce(out=self.pmaxc[:, Q:Q + 1],
                                    in_=self.estage[:, Ep, 0:1024], axis=AX,
                                    op=ALU.max, apply_absolute_value=True)
                    pv_mark[Q] = done[ic] = inc(v, "mv_")

                if Q >= 2:
                    spill(Q - 2)
        spill(IC * 4 - 2)
        spill(IC * 4 - 1)

        self.p675_done = self.CNT["mv_"]
        self.p675_de = tuple(self.CNT[d] for d in DE_SEMS)

    def r4(self):
        inc, wait = self.inc, self.wait
        self.xpart_max(self.pmaxc[:, 0:96], 4)
        self.all_reduce_max()
        self.bcast_scale(4, 13, 14, raw_col=12)

        @self.on("vector")
        def _(v):
            wait(v, "mv_")
            v.tensor_tensor(out=self.scales[:, 15:16], in0=self.scales[:, 6:7],
                            in1=self.scales[:, 12:13], op=ALU.mult)
            v.drain()
            v.tensor_scalar(out=self.scales[:, 8:9], in0=self.scales[:, 15:16],
                            scalar1=1.0 / QMAX, scalar2=1e-12, op0=ALU.mult, op1=ALU.max)
            v.drain()
            v.reciprocal(out=self.scales[:, 15:16], in_=self.scales[:, 8:9])
            v.drain()
            v.tensor_tensor(out=self.scales[:, 9:10], in0=self.scales[:, 6:7],
                            in1=self.scales[:, 15:16], op=ALU.mult)
            v.drain()
            self.r4_scales = inc(v, "mv_")

    def p7(self):
        inc, dinc, wait = self.inc, self.dinc, self.wait
        oq_mark = {}
        mm_mark = {}
        ev_mark = {}
        dma_mark = {}
        att_mark = {}
        pa2 = self.bigps[:, 0:1024]

        def loads(tc):
            slot = tc % 2
            das = "da0" if slot == 0 else "da1"
            dbs = "db0" if slot == 0 else "db1"
            dxs = "dx0" if slot == 0 else "dx1"
            t0 = tc * 128

            @self.on("sync")
            def _(s, tc=tc, slot=slot, das=das, dbs=dbs, dxs=dxs, t0=t0):
                if tc == 0:
                    wait(s, "mv_", self.p675_done)
                    for dd, vv in zip(DE_SEMS, self.p675_de):
                        wait(s, dd, vv)
                    wait(s, "dy0", self.p65_att_dy[0])
                    wait(s, "dy1", self.p65_att_dy[1])
                if tc >= 2:
                    wait(s, "mv_", ev_mark[tc - 2])
                dinc(s.dma_start(out=self.ostage[:, slot, :],
                                 in_=self.o_att[tc, :, :, :]
                                 .rearrange("p c n -> p (c n)")), das)
                dinc(s.dma_start(out=self.xld[:, slot, :], in_=self.x_d[t0:t0 + 128, :]),
                     dxs)
                dinc(s.dma_start(out=self.attnA(tc), in_=self.att_sp[tc, :, :]), dbs)

        def oquant(tc):
            slot = tc % 2
            das = "da0" if slot == 0 else "da1"
            for hf in range(2):
                h0 = hf * 1536

                @self.on("scalar")
                def _(s, tc=tc, slot=slot, das=das, h0=h0, hf=hf):
                    if tc == 0 and hf == 0:
                        wait(s, "mv_", self.r4_scales)
                    if hf == 0:
                        wait(s, das)
                        if tc >= 2:
                            wait(s, "mt", mm_mark[tc - 2])
                    if tc >= 1 or hf >= 1:
                        wait(s, "mv_")
                    s.activation(out=self.iscr[:, 0:1536],
                                 in_=self.ostage[:, slot, h0:h0 + 1536],
                                 func=AF.Identity, scale=self.scales[:, 9:10],
                                 bias=self.consts[:, 0:1])
                    s.drain()
                    inc(s, "ms")

                @self.on("vector")
                def _(v, tc=tc, h0=h0):
                    wait(v, "ms")
                    oq = self.oq if tc % 2 == 0 else self.oq2
                    v.tensor_scalar(out=oq[:, h0:h0 + 1536], in0=self.iscr[:, 0:1536],
                                    scalar1=CQ, scalar2=self.scales[:, 8:9],
                                    op0=ALU.subtract, op1=ALU.mult)
                    oq_mark[tc] = inc(v, "mv_")

        def mms(tc):
            @self.on("tensor")
            def _(t, tc=tc):
                wait(t, "mv_", oq_mark[tc])
                if tc == 0:
                    wait(t, "ms", self.p65_done_ms)
                if tc >= 2:
                    wait(t, "mv_", ev_mark[tc - 2])
                oq = self.oq if tc % 2 == 0 else self.oq2
                ps = self.pa if tc % 2 == 0 else pa2
                for ic in range(IC):
                    t.matmul(ps[:, 0:512], oq[:, ic * 128:(ic + 1) * 128],
                             self.wdT[:, ic * 1024:ic * 1024 + 512],
                             start=(ic == 0), stop=(ic == IC - 1))
                    li = t.matmul(ps[:, 512:1024], oq[:, ic * 128:(ic + 1) * 128],
                                  self.wdT[:, ic * 1024 + 512:(ic + 1) * 1024],
                                  start=(ic == 0), stop=(ic == IC - 1))
                mm_mark[tc] = inc(li, "mt")

        def evac(tc):
            slot = tc % 2
            dxs = "dx0" if slot == 0 else "dx1"
            dbs = "db0" if slot == 0 else "db1"

            @self.on("vector")
            def _(v, tc=tc, slot=slot, dxs=dxs, dbs=dbs):
                wait(v, "mt", mm_mark[tc])
                wait(v, dxs)
                wait(v, dbs)
                if tc >= 2:
                    wait(v, dma_mark[tc - 2][0], dma_mark[tc - 2][1])
                ps = self.pa if tc % 2 == 0 else pa2
                v.scalar_tensor_tensor(out=self.outst[:, slot, :], in0=ps[:, :],
                                       scalar=1.0, in1=self.attnA(tc),
                                       op0=ALU.mult, op1=ALU.add)
                v.drain()
                v.tensor_tensor(out=self.outst[:, slot, :], in0=self.outst[:, slot, :],
                                in1=self.xld[:, slot, :], op=ALU.add)
                ev_mark[tc] = inc(v, "mv_")

        def store(tc):
            slot = tc % 2
            t0 = tc * 128

            @self.on("sync")
            def _(s, tc=tc, slot=slot, t0=t0):
                wait(s, "mv_", ev_mark[tc])
                dys = "dy0" if slot == 0 else "dy1"
                dma_mark[tc] = (dys, dinc(s.dma_start(
                    out=self.y_d[t0:t0 + 128, :], in_=self.outst[:, slot, :]), dys))

        for tc in range(NTT):
            loads(tc)
            oquant(tc)
            mms(tc)
            if tc >= 1:
                evac(tc - 1)
                store(tc - 1)
        evac(NTT - 1)
        store(NTT - 1)

    def fin(self):
        @self.on("sync")
        def _(s):
            self.wait(s, "dy0")
            self.wait(s, "dy1")


def build_nc():
    b = Builder()
    nc = b.build()
    b.ctx.close()
    return nc


_NC_CACHE = None


def kernel(L0_attn_inputs, L0_attn_W_rkv, L0_attn_b_rkv, L0_attn_W_o, L0_attn_b_o,
           L0_attn_nu_log, L0_attn_gamma_log, L0_ffn_w_gate, L0_ffn_w_up, L0_ffn_w_down):
    global _NC_CACHE
    x = np.ascontiguousarray(np.asarray(L0_attn_inputs, dtype=np.float32))
    wrkv = np.ascontiguousarray(np.asarray(L0_attn_W_rkv, dtype=np.float32))
    brkv = np.ascontiguousarray(np.asarray(L0_attn_b_rkv, dtype=np.float32))
    wo = np.ascontiguousarray(np.asarray(L0_attn_W_o, dtype=np.float32))
    bo = np.ascontiguousarray(np.asarray(L0_attn_b_o, dtype=np.float32))
    nulog = np.ascontiguousarray(np.asarray(L0_attn_nu_log, dtype=np.float32))
    glog = np.ascontiguousarray(np.asarray(L0_attn_gamma_log, dtype=np.float32))
    wg = np.ascontiguousarray(np.asarray(L0_ffn_w_gate, dtype=np.float32))
    wu = np.ascontiguousarray(np.asarray(L0_ffn_w_up, dtype=np.float32))
    wd = np.ascontiguousarray(np.asarray(L0_ffn_w_down, dtype=np.float32))

    if _NC_CACHE is None:
        _NC_CACHE = build_nc()
    nc = _NC_CACHE

    in_maps = []
    for c in range(8):
        bb, hh = c // 2, c % 2
        sel = np.zeros((128, 16), np.float32)
        if hh == 1:
            for g in range(16):
                sel[16 * (c - 1) + g, g] = 1.0
        in_maps.append({
            "x": np.ascontiguousarray(x[bb, hh * T:(hh + 1) * T, :]),
            "wrkv": wrkv, "brkv": brkv, "wo": wo, "bo": bo,
            "nulog": nulog, "glog": glog, "wg": wg, "wu": wu, "wd": wd,
            "sel": sel,
        })
    res = run_bass_kernel_spmd(nc, in_maps, core_ids=list(range(8)))
    out = np.empty((B, L, H), dtype=np.float32)
    for c in range(8):
        bb, hh = c // 2, c % 2
        out[bb, hh * T:(hh + 1) * T, :] = res.results[c]["y"]
    return out


# version 22
# speedup vs baseline: 1.0738x; 1.0280x over previous
import numpy as np
from contextlib import ExitStack
import concourse.bass as bass
import concourse.mybir as mybir
from concourse.bass_utils import run_bass_kernel_spmd

F32 = mybir.dt.float32
BF16 = mybir.dt.bfloat16
I32 = mybir.dt.int32
AF = mybir.ActivationFunctionType
ALU = mybir.AluOpType
AX = mybir.AxisListType.X

B, L, H, U3 = 4, 8192, 1024, 3072
T = 4096
HC = 8
UC = 24
IC = 24
NT = 8
NTT = 32
CQ = 65536.0
DE_SEMS = ["de0", "de1", "de2", "de3"]
QMAX = 32767.0


class _RecInstr:
    __slots__ = ("_rq", "_rid")

    def __init__(self, rq, rid):
        self._rq = rq
        self._rid = rid

    def __getattr__(self, name):
        def call(*args, **kwargs):
            self._rq.append(("chain", self._rid, name, args, kwargs))
            return self
        return call


class _RecEngine:
    __slots__ = ("_name", "_q", "_n")

    def __init__(self, name):
        self._name = name
        self._q = []
        self._n = 0

    def __getattr__(self, name):
        if name == "then_inc":
            raise AttributeError(name)

        def call(*args, **kwargs):
            rid = self._n
            self._n += 1
            self._q.append(("call", rid, name, args, kwargs))
            return _RecInstr(self._q, rid)
        return call


class Builder:
    def __init__(self):
        self.nc = bass.Bass("TRN2", target_bir_lowering=False, debug=False)
        self.ctx = ExitStack()
        self.CNT = {}
        self.SEM = {}
        self._rec = {}

    def sem(self, name):
        self.SEM[name] = self.ctx.enter_context(self.nc.semaphore(name))
        self.CNT[name] = 0

    def inc(self, bi, name, amt=1):
        if not hasattr(bi, "then_inc"):
            bi = bi.drain()
        bi.then_inc(self.SEM[name], amt)
        self.CNT[name] += amt
        return self.CNT[name]

    def dinc(self, bi, name):
        return self.inc(bi, name, 16)

    def wait(self, eng, name, val=None):
        eng.wait_ge(self.SEM[name], self.CNT[name] if val is None else val)

    def sb(self, name, shape, dt):
        return self.ctx.enter_context(self.nc.sbuf_tensor(name, shape, dt))

    def on(self, engine_name):
        rec = self._rec.setdefault(engine_name, _RecEngine(engine_name))

        def deco(fn, *a, **k):
            fn(rec, *a, **k)
        return deco

    def flush_engines(self):
        for name, rec in self._rec.items():
            q = rec._q

            def replay(eng, q=q):
                objs = {}
                for kind, rid, meth, args, kwargs in q:
                    if kind == "call":
                        objs[rid] = getattr(eng, meth)(*args, **kwargs)
                    else:
                        objs[rid] = getattr(objs[rid], meth)(*args, **kwargs)
            getattr(self.block, name)(replay)
        self._rec = {}

    def build(self):
        nc = self.nc
        self.x_d = nc.dram_tensor("x", [T, H], F32, kind="ExternalInput").ap()
        self.wrkv_d = nc.dram_tensor("wrkv", [H, U3], F32, kind="ExternalInput").ap()
        self.brkv_d = nc.dram_tensor("brkv", [U3], F32, kind="ExternalInput").ap()
        self.wo_d = nc.dram_tensor("wo", [H, H], F32, kind="ExternalInput").ap()
        self.bo_d = nc.dram_tensor("bo", [H], F32, kind="ExternalInput").ap()
        self.nulog_d = nc.dram_tensor("nulog", [H], F32, kind="ExternalInput").ap()
        self.glog_d = nc.dram_tensor("glog", [H], F32, kind="ExternalInput").ap()
        self.wg_d = nc.dram_tensor("wg", [U3, H], F32, kind="ExternalInput").ap()
        self.wu_d = nc.dram_tensor("wu", [U3, H], F32, kind="ExternalInput").ap()
        self.wd_d = nc.dram_tensor("wd", [H, U3], F32, kind="ExternalInput").ap()
        self.sel_d = nc.dram_tensor("sel", [128, 16], F32, kind="ExternalInput").ap()
        self.y_d = nc.dram_tensor("y", [T, H], F32, kind="ExternalOutput").ap()
        dr = lambda n, s, d: nc.dram_tensor(n, s, d).ap()
        self.u_sp = dr("u_sp", [16, 128, T], BF16)
        self.r_sp = dr("r_sp", [8, 128, T], BF16)
        self.up_sp = dr("up_sp", [IC, 128, T], BF16)
        self.gate_sp = dr("gate_sp", [IC, 128, T], BF16)
        self.g2_sp = dr("g2_sp", [IC, 128, T], BF16)
        self.o_sp = dr("o_sp", [IC, 128, T], BF16)
        self.wuT_sp = dr("wuT_sp", [IC, HC, 128, 128], BF16)
        self.att_sp = dr("att_sp", [NTT, 128, 1024], BF16)
        self.u_att = dr("u_att", [NTT, 128, 16, 128], BF16)
        self.r_att = dr("r_att", [NTT, 128, 8, 128], BF16)
        self.o_att = dr("o_att", [NTT, 128, IC, 128], BF16)
        self.wgT_sp = dr("wgT_sp", [IC, HC, 128, 128], BF16)
        self.cc_in = dr("cc_in", [1, 16], F32)
        self.cc_out = dr("cc_out", [1, 16], F32)
        self.ag_in = dr("ag_in", [16, 128], F32)
        self.ag_out = dr("ag_out", [128, 128], F32)

        sb = self.sb
        self.arenaA = sb("arenaA", [128, 8 * T], BF16)
        self.wdT = sb("wdT", [128, IC * 1024], BF16)
        self.wo_sb = sb("wo_sb", [128, HC * 1024], BF16)
        self.wkB = sb("wkB", [128, 8192], F32)
        self.ident = sb("ident", [128, 128], BF16)
        self.identf = sb("identf", [128, 128], F32)
        self.ones_b = sb("ones_b", [1, 128], BF16)
        self.ones_f = sb("ones_f", [1, 128], F32)
        self.bo_row = sb("bo_row", [1, 1024], BF16)
        self.consts = sb("consts", [128, 8], F32)
        self.brkv_t = sb("brkv_t", [128, 24], F32)
        self.nulog_t = sb("nulog_t", [128, 8], F32)
        self.glog_t = sb("glog_t", [128, 8], F32)
        self.lam_t = sb("lam_t", [128, 8], F32)
        self.gamma_t = sb("gamma_t", [128, 8], F32)
        self.stats = sb("stats", [128, 2, 12], F32)
        self.mvt = sb("mvt", [128, 2, 2], F32)
        self.tiny = sb("tiny", [128, 12], F32)
        self.maxc = sb("maxc", [128, 36], F32)
        self.upmaxc = sb("upmaxc", [128, 96], F32)
        self.gatemaxc = sb("gatemaxc", [128, 96], F32)
        self.g2maxc = sb("g2maxc", [128, 96], F32)
        self.pmaxc = sb("pmaxc", [128, 96], F32)
        self.mx1 = sb("mx1", [128, 1], F32)
        self.cc_sb = sb("cc_sb", [1, 16], F32)
        self.ccr_sb = sb("ccr_sb", [1, 16], F32)
        self.scales = sb("scales", [128, 20], F32)
        self.S_loc = sb("S_loc", [128, 16], F32)
        self.S_prev = sb("S_prev", [128, 16], F32)
        self.carries = sb("carries", [128, 16], F32)
        self.ag_sb = sb("ag_sb", [128, 128], F32)
        self.agT_sb = None
        self.sel_sb = sb("sel_sb", [128, 16], F32)
        self.xld = sb("xld", [128, 2, 1024], F32)
        self.xnstage = sb("xnstage", [128, 2, 1024], BF16)
        self.wstream = sb("wstream", [128, 2, HC, 128], BF16)
        self.estage = sb("estage", [128, 4, 1024], BF16)
        self.iscr = sb("iscr", [128, 4096], I32)
        self.kstage = self.wkB[:, :].bitcast(BF16).rearrange(
            "p (c t n) -> p c t n", c=8, t=4)
        self.pdqst = self.xnstage
        self.bmax = sb("bmax", [128, 1, 96], F32)
        self.binv = sb("binv", [128, 1, 96], F32)
        self.bs = sb("bs", [128, 1, 96], F32)

        wkb = self.wkB[:, :]
        self.wf = wkb[:, 0:6144].rearrange("p (s n) -> p s n", s=2)
        self.wq = wkb[:, 6144:7680].bitcast(BF16)
        self.ubuf = wkb[:, 0:4096].bitcast(BF16).rearrange("p (s n) -> p s n", s=2)
        self.ydisc = wkb[:, 4096:8192]
        self.gbuf = self.ubuf
        self.sigb = wkb[:, 4096:6144].rearrange("p (s n) -> p s n", s=2)
        self.svb = wkb[:, 6144:8192].rearrange("p (s n) -> p s n", s=2)
        self.att_u = wkb[:, 0:2048].bitcast(BF16).rearrange("p (s n) -> p s n", s=2)
        self.att_r = wkb[:, 2048:3072].bitcast(BF16).rearrange("p (s n) -> p s n", s=2)
        self.att_y = wkb[:, 3072:5120]
        self.att_n = wkb[:, 5120:6144]
        self.att_d = wkb[:, 6144:7168]
        self.att_c = wkb[:, 7168:8192]
        self.ratio = self.att_c[:, :].bitcast(BF16).rearrange("p (s c n) -> p s c n", s=2, c=8)
        self.g2in = wkb[:, 0:4096].bitcast(BF16).rearrange("p (s n) -> p s n", s=2)
        self.upin = wkb[:, 4096:8192].bitcast(BF16).rearrange("p (s n) -> p s n", s=2)
        self.ostage = wkb[:, 0:3072].bitcast(BF16).rearrange("p (s n) -> p s n", s=2)
        self.oq = wkb[:, 3072:4608].bitcast(BF16)
        self.oq2 = wkb[:, 6656:8192].bitcast(BF16)
        self.outst = wkb[:, 4608:6656].rearrange("p (s n) -> p s n", s=2)
        self.upq_t = self.xnstage[:, :, :].rearrange("p s n -> p (s n)")

        self.bigps = self.ctx.enter_context(nc.psum_tensor("bigps", [128, 3584], F32))
        self.pt = self.ctx.enter_context(nc.psum_tensor("pt", [128, 1024], BF16))
        self.pa = self.bigps[:, 2048:3072]
        self.fdq = self.bigps[:, 2048:3072]
        self.px = self.bigps[:, 3072:3584]

        for n in ["dx0", "dx1", "dw0", "dw1", "dwg0", "dwg1", "da0", "da1", "db0", "db1",
                  "dsp", "de0", "de1", "de2", "de3", "dci", "dq0", "dq1", "dy0", "dy1", "dy", "dg", "dcc", "dk", "dk0", "dk1", "mt", "mv_", "ms", "mg", "mc"]:
            self.sem(n)

        self.block = self.ctx.enter_context(nc.Block())

        self.p0()
        self.p1()
        self.r1_start()
        self.dq_init()
        self.p2()
        self.pdq_down()
        self.r1_finish()
        self.p3()
        self.p4()
        self.p5()
        self.r2()
        self.p6()
        self.r3()
        self.p675()
        self.r4()
        self.p7()
        self.fin()
        self.flush_engines()
        return nc

    def bank(self, i):
        return self.bigps[:, i * 512:(i + 1) * 512]

    def ring_init(self):
        self.rbank = [self.bigps[:, i * 512:(i + 1) * 512] for i in range(7)]
        self.rbank.append(self.pt[:, :].bitcast(F32))
        self.rnext = 0
        self.rmark = {i: None for i in range(8)}

    def ralloc(self, n):
        idx = [(self.rnext + j) % 8 for j in range(n)]
        self.rnext = (self.rnext + n) % 8
        return idx

    def xnA(self, hc, t0, n):
        return self.arenaA[:, hc * T + t0: hc * T + t0 + n]

    def attnA(self, tc):
        return self.arenaA[:, tc * 1024:(tc + 1) * 1024]

    def p0(self):
        inc, dinc, wait = self.inc, self.dinc, self.wait

        @self.on("gpsimd")
        def _(g):
            g.iota(out=self.iscr[:, 0:128], pattern=[[1, 128]], base=0, channel_multiplier=-1)
            g.memset(self.consts[:, 0:1], 65536.5)
            g.memset(self.consts[:, 1:2], 0.5)
            g.memset(self.consts[:, 2:3], 8.0)
            g.memset(self.ones_b[:, :], 1.0)
            g.memset(self.cc_sb[:, :], 0.0)
            g.memset(self.ones_f[:, :], 1.0)
            g.drain()
            g.tensor_scalar(out=self.ident[:, :], in0=self.iscr[:, 0:128], scalar1=0.0,
                            scalar2=None, op0=ALU.is_equal)
            g.tensor_scalar(out=self.identf[:, :], in0=self.iscr[:, 0:128], scalar1=0.0,
                            scalar2=None, op0=ALU.is_equal)
            g.drain()
            self.mg_init = inc(g, "mg")
            dinc(g.dma_start(out=self.wo_sb.rearrange("p (c n) -> p c n", n=1024),
                        in_=self.wo_d.rearrange("(c p) n -> p c n", p=128)), "dg")
            self.dg_wo = dinc(g.dma_start(out=self.bo_row[:, :], in_=self.bo_d[None, :]), "dg")

        @self.on("sync")
        def _(s):
            dinc(s.dma_start(out=self.brkv_t[:, :], in_=self.brkv_d.rearrange("(c p) -> p c", p=128), allow_slow_non_contiguous=True), "da0")
            dinc(s.dma_start(out=self.nulog_t[:, :], in_=self.nulog_d.rearrange("(c p) -> p c", p=128), allow_slow_non_contiguous=True), "da0")
            dinc(s.dma_start(out=self.glog_t[:, :], in_=self.glog_d.rearrange("(c p) -> p c", p=128), allow_slow_non_contiguous=True), "da0")
            self.da_p0 = dinc(s.dma_start(out=self.sel_sb[:, :], in_=self.sel_d[:, :]), "da0")

        @self.on("scalar")
        def _(s):
            wait(s, "da0", self.da_p0)
            s.activation(out=self.tiny[:, 0:8], in_=self.nulog_t[:, :], func=AF.Exp)
            s.drain()
            s.activation(out=self.lam_t[:, :], in_=self.tiny[:, 0:8], func=AF.Exp, scale=-1.0)
            s.drain()
            inc(s, "ms")

        @self.on("vector")
        def _(v):
            wait(v, "da0", self.da_p0)
            v.tensor_tensor(out=self.glog_t[:, :], in0=self.nulog_t[:, :],
                            in1=self.glog_t[:, :], op=ALU.add)
            v.drain()
            inc(v, "mv_")

        @self.on("scalar")
        def _(s):
            wait(s, "mv_")
            s.activation(out=self.gamma_t[:, :], in_=self.glog_t[:, :], func=AF.Exp)
            s.drain()
            inc(s, "ms")

        @self.on("vector")
        def _(v):
            wait(v, "ms")
            v.tensor_scalar(out=self.gamma_t[:, :], in0=self.gamma_t[:, :], scalar1=-1.0,
                            scalar2=None, op0=ALU.add)
            v.drain()
            self.mv_gamma = inc(v, "mv_")

    def p1(self):
        inc, dinc, wait = self.inc, self.dinc, self.wait
        st_mark = {}
        ln_mark = {}
        xn_mark = {}
        tp_mark = {}
        ev_mark = {}
        xnA_v = self.arenaA[:, :].rearrange("p (c t) -> p c t", t=T)

        def load(tt):
            slot = tt % 2
            dxs = "dx0" if slot == 0 else "dx1"

            @self.on("sync")
            def _(s, tt=tt, slot=slot, dxs=dxs):
                if tt >= 2:
                    wait(s, "mv_", xn_mark[tt - 2])
                dinc(s.dma_start(out=self.xld[:, slot, :],
                                 in_=self.x_d[tt * 128:(tt + 1) * 128, :]), dxs)

        def stats(tt):
            slot = tt % 2
            dxs = "dx0" if slot == 0 else "dx1"

            @self.on("vector")
            def _(v, tt=tt, slot=slot, dxs=dxs):
                wait(v, dxs)
                if tt >= 2:
                    wait(v, "ms", ln_mark[tt - 2])
                v.bn_stats(out=self.stats[:, slot, 0:6], in_=self.xld[:, slot, 0:512])
                v.bn_stats(out=self.stats[:, slot, 6:12], in_=self.xld[:, slot, 512:1024])
                v.drain()
                v.bn_aggr(out=self.mvt[:, slot, :], in_=self.stats[:, slot, :])
                v.drain()
                v.tensor_scalar(out=self.tiny[:, slot:slot + 1],
                                in0=self.mvt[:, slot, 1:2], scalar1=1e-5,
                                scalar2=None, op0=ALU.add)
                st_mark[tt] = inc(v, "mv_")

        def lnexp(tt):
            slot = tt % 2

            @self.on("scalar")
            def _(s, tt=tt, slot=slot):
                wait(s, "mv_", st_mark[tt])
                if tt >= 2:
                    wait(s, "mv_", xn_mark[tt - 2])
                s.activation(out=self.tiny[:, 2 + slot:3 + slot],
                             in_=self.tiny[:, slot:slot + 1], func=AF.Ln)
                s.drain()
                s.activation(out=self.tiny[:, 4 + slot:5 + slot],
                             in_=self.tiny[:, 2 + slot:3 + slot], func=AF.Exp,
                             scale=-0.5)
                s.drain()
                s.activation(out=self.tiny[:, 6 + slot:7 + slot],
                             in_=self.tiny[:, 4 + slot:5 + slot], func=AF.Identity,
                             scale=-1.0)
                s.drain()
                s.activation(out=self.tiny[:, 8 + slot:9 + slot],
                             in_=self.mvt[:, slot, 0:1], func=AF.Identity,
                             scale=self.tiny[:, 6 + slot:7 + slot])
                s.drain()
                s.activation(out=self.xnstage[:, slot, :], in_=self.xld[:, slot, :],
                             func=AF.Identity, scale=self.tiny[:, 4 + slot:5 + slot],
                             bias=self.tiny[:, 8 + slot:9 + slot])
                ln_mark[tt] = inc(s, "ms")

        def xn(tt):
            slot = tt % 2

            @self.on("vector")
            def _(v, tt=tt, slot=slot):
                wait(v, "ms", ln_mark[tt])
                v.tensor_reduce(out=self.maxc[:, tt:tt + 1], in_=self.xnstage[:, slot, :],
                                axis=AX, op=ALU.max, apply_absolute_value=True)
                xn_mark[tt] = inc(v, "mv_")

        def transpose(tt):
            slot = tt % 2

            @self.on("tensor")
            def _(t, tt=tt, slot=slot):
                if tt == 0:
                    wait(t, "mg", self.mg_init)
                wait(t, "mv_", xn_mark[tt])
                if tt >= 1:
                    wait(t, "ms", ev_mark[tt - 1])
                for hc in range(HC):
                    li = t.transpose(self.pt[:, hc * 128:(hc + 1) * 128],
                                     self.xnstage[:, slot, hc * 128:(hc + 1) * 128],
                                     self.ident[:, :])
                tp_mark[tt] = inc(li, "mt")

        def evac(tt):
            @self.on("scalar")
            def _(s, tt=tt):
                wait(s, "mt", tp_mark[tt])
                s.activation(out=xnA_v[:, :, tt * 128:(tt + 1) * 128],
                             in_=self.pt[:, :].rearrange("p (c q) -> p c q", q=128),
                             func=AF.Identity)
                ev_mark[tt] = inc(s, "ms")

        for tt in range(NTT):
            load(tt)
            stats(tt)
            lnexp(tt)
            if tt >= 1:
                xn(tt - 1)
                transpose(tt - 1)
                evac(tt - 1)
        xn(NTT - 1)
        transpose(NTT - 1)
        evac(NTT - 1)
        self.p1_done = self.CNT["ms"]

    def xpart_max(self, cols_ap, out_col):
        inc, wait = self.inc, self.wait

        @self.on("vector")
        def _(v):
            v.tensor_reduce(out=self.mx1[:, :], in_=cols_ap, axis=AX, op=ALU.max)
            v.drain()
            inc(v, "mv_")

        @self.on("tensor")
        def _(t):
            wait(t, "mv_")
            wait(t, "ms")
            inc(t.transpose(self.px[0:1, 0:128], self.mx1[:, 0:1], self.identf[:, :]),
                "mt")

        @self.on("vector")
        def _(v, out_col=out_col):
            wait(v, "mt")
            v.tensor_reduce(out=self.cc_sb[0:1, out_col:out_col + 1],
                            in_=self.px[0:1, 0:128], axis=AX, op=ALU.max)
            v.drain()
            inc(v, "mv_")

    def all_reduce_max_trigger(self):
        inc, dinc, wait = self.inc, self.dinc, self.wait

        @self.on("scalar")
        def _(s):
            wait(s, "mv_")
            dinc(s.dma_start(out=self.cc_in[:, :], in_=self.cc_sb[:, :]), "dci")

        @self.on("gpsimd")
        def _(g):
            wait(g, "dci")
            inc(g.collective_compute("AllReduce", ALU.max,
                                     replica_groups=[list(range(8))],
                                     ins=[self.cc_in.opt()], outs=[self.cc_out.opt()]),
                "mc")

    def all_reduce_max_fetch(self):
        inc, dinc, wait = self.inc, self.dinc, self.wait

        @self.on("gpsimd")
        def _(g):
            wait(g, "mc")
            dinc(g.dma_start(out=self.ccr_sb[:, :], in_=self.cc_out[:, :]), "dcc")

    def all_reduce_max(self):
        self.all_reduce_max_trigger()
        self.all_reduce_max_fetch()

    def bcast_scale(self, src_col, s_col, inv_col, raw_col=None):
        inc, wait = self.inc, self.wait

        @self.on("tensor")
        def _(t):
            wait(t, "dcc")
            wait(t, "mv_")
            wait(t, "ms")
            inc(t.matmul(self.px[0:128, 0:1], self.ones_f[0:1, 0:128],
                         self.ccr_sb[0:1, src_col:src_col + 1], start=True, stop=True),
                "mt")

        @self.on("vector")
        def _(v):
            wait(v, "mt")
            if raw_col is not None:
                v.tensor_copy(out=self.scales[:, raw_col:raw_col + 1], in_=self.px[0:128, 0:1])
            v.tensor_scalar(out=self.scales[:, s_col:s_col + 1], in0=self.px[0:128, 0:1],
                            scalar1=1.0 / QMAX, scalar2=1e-12, op0=ALU.mult, op1=ALU.max)
            v.drain()
            v.reciprocal(out=self.scales[:, inv_col:inv_col + 1],
                         in_=self.scales[:, s_col:s_col + 1])
            v.drain()
            inc(v, "mv_")

    def r1_start(self):
        self.xpart_max(self.maxc[:, 0:NTT], 0)
        self.all_reduce_max_trigger()

    def r1_finish(self):
        self.all_reduce_max_fetch()
        self.bcast_scale(0, 0, 1)

        @self.on("vector")
        def _(v):
            self.wait(v, "mv_")
            v.tensor_scalar(out=self.scales[:, 17:18], in0=self.scales[:, 0:1],
                            scalar1=-CQ, scalar2=None, op0=ALU.mult)
            v.drain()
            self.inc(v, "mv_")
        self.r1_scales = self.CNT["mv_"]

    def p2(self):
        inc, dinc, wait = self.inc, self.dinc, self.wait
        mt_of = {}
        ev_of = {}
        self.last_e = {}
        last_e = self.last_e
        k_sg = {}
        kv_last = {}
        spillq = []
        self._spq = spillq
        kst = self.wkB[:, 0:2048].bitcast(BF16)

        uc_order = list(range(8)) + [x for cc in range(8) for x in (8 + cc, 16 + cc)]
        dq_jobs = [(self.wu_d, icq, self.wuT_sp) for icq in range(IC)] + \
                  [(self.wg_d, icq, self.wgT_sp) for icq in range(IC)]
        dq_next = [0]
        gsg = -1
        for it, uc in enumerate(uc_order):
            slot = it % 2
            dws = "dwg0" if slot == 0 else "dwg1"

            @self.on("gpsimd")
            def _(g, uc=uc, slot=slot, dws=dws, it=it):
                if it >= 2:
                    wait(g, "mt", mt_of[(it - 2) * 4 + 3])
                dinc(g.dma_start(out=self.wstream[:, slot, :, :],
                                 in_=self.wrkv_d[:, uc * 128:(uc + 1) * 128]
                                 .rearrange("(hc p) q -> p hc q", p=128)), dws)

            for sg in range(4):
                gsg += 1
                t0 = sg * 1024
                b0 = (gsg % 2) * 2

                @self.on("tensor")
                def _(t, uc=uc, slot=slot, dws=dws, it=it, sg=sg, gsg=gsg, t0=t0, b0=b0):
                    if sg == 0:
                        wait(t, dws)
                        if it == 0:
                            wait(t, "ms", self.p1_done)
                    if gsg >= 2:
                        sname, scnt = ev_of[gsg - 2]
                        wait(t, sname, scnt)
                    for hc in range(HC):
                        for tl in range(2):
                            li = t.matmul(self.bank(b0 + tl), self.wstream[:, slot, hc, :],
                                          self.xnA(hc, t0 + tl * 512, 512),
                                          start=(hc == 0), stop=(hc == HC - 1))
                    mt_of[gsg] = inc(li, "mt")

                if uc < 8:
                    if sg == 0:
                        while spillq:
                            spillq.pop(0)()

                    @self.on("scalar")
                    def _(s, uc=uc, sg=sg, gsg=gsg, t0=t0, b0=b0):
                        wait(s, "mt", mt_of[gsg])
                        esl = gsg % 4
                        if esl in last_e:
                            wait(s, last_e[esl][0], last_e[esl][1])
                        for tl in range(2):
                            s.activation(out=self.estage[:, esl, tl * 512:(tl + 1) * 512],
                                         in_=self.bank(b0 + tl), func=AF.Sigmoid,
                                         bias=self.brkv_t[:, uc:uc + 1])
                        ev_of[gsg] = ("ms", inc(s, "ms"))

                    def sp_r(uc=uc, sg=sg, gsg=gsg):
                        @self.on("sync")
                        def _(s, uc=uc, sg=sg, gsg=gsg):
                            esl = gsg % 4
                            des = DE_SEMS[esl]
                            wait(s, ev_of[gsg][0], ev_of[gsg][1])
                            dinc(s.dma_start(
                                out=self.r_att[sg * 8:(sg + 1) * 8, :, uc, :]
                                .rearrange("tc p n -> p tc n"),
                                in_=self.estage[:, esl, 0:1024]
                                .rearrange("p (tc n) -> p tc n", n=128)), des)
                            last_e[esl] = (des, self.CNT[des])
                    spillq.append(sp_r)
                elif uc < 16:
                    if sg == 0:
                        while spillq:
                            spillq.pop(0)()

                    @self.on("scalar")
                    def _(s, uc=uc, sg=sg, gsg=gsg, t0=t0, b0=b0):
                        wait(s, "mt", mt_of[gsg])
                        cc = uc - 8
                        if sg == 0 and cc >= 1:
                            wait(s, "mv_", kv_last[cc - 1])
                            dkp = "dk0" if (cc - 1) % 2 == 0 else "dk1"
                            wait(s, dkp, self.CNT[dkp])
                        for tl in range(2):
                            s.activation(out=kst[:, t0 + tl * 512:t0 + (tl + 1) * 512],
                                         in_=self.bank(b0 + tl), func=AF.Exp,
                                         bias=self.brkv_t[:, uc:uc + 1])
                        k_sg[(cc, sg)] = ev_of[gsg] = ("ms", inc(s, "ms"))

                    def sp_k(uc=uc, sg=sg, gsg=gsg, t0=t0):
                        @self.on("sync")
                        def _(s, uc=uc, sg=sg, gsg=gsg, t0=t0):
                            cc = uc - 8
                            wait(s, ev_of[gsg][0], ev_of[gsg][1])
                            dkp = "dk0" if cc % 2 == 0 else "dk1"
                            dinc(s.dma_start(out=self.u_sp[8 + cc, :, t0:t0 + 1024],
                                             in_=kst[:, t0:t0 + 1024]), dkp)
                            dinc(s.dma_start(
                                out=self.u_att[sg * 8:(sg + 1) * 8, :, 8 + cc, :]
                                .rearrange("tc p n -> p tc n"),
                                in_=kst[:, t0:t0 + 1024]
                                .rearrange("p (tc n) -> p tc n", n=128)), dkp)
                    spillq.append(sp_k)
                else:
                    @self.on("vector")
                    def _(v, uc=uc, sg=sg, gsg=gsg, t0=t0, b0=b0):
                        cc = uc - 16
                        wait(v, "mt", mt_of[gsg])
                        sname, scnt = k_sg[(cc, sg)]
                        wait(v, sname, scnt)
                        esl = gsg % 4
                        if esl in last_e:
                            wait(v, last_e[esl][0], last_e[esl][1])
                        for tl in range(2):
                            v.scalar_tensor_tensor(
                                out=self.estage[:, esl, tl * 512:(tl + 1) * 512],
                                in0=self.bank(b0 + tl), scalar=self.brkv_t[:, uc:uc + 1],
                                in1=kst[:, t0 + tl * 512:t0 + (tl + 1) * 512],
                                op0=ALU.add, op1=ALU.mult)
                        kv_last[cc] = inc(v, "mv_")
                        ev_of[gsg] = ("mv_", kv_last[cc])

                    def sp_v(uc=uc, sg=sg, gsg=gsg, t0=t0):
                        @self.on("sync")
                        def _(s, uc=uc, sg=sg, gsg=gsg, t0=t0):
                            cc = uc - 16
                            wait(s, "mv_", kv_last[cc])
                            esl = gsg % 4
                            des = DE_SEMS[esl]
                            dinc(s.dma_start(
                                out=self.u_sp[cc, :, t0:t0 + 1024],
                                in_=self.estage[:, esl, 0:1024]), des)
                            dinc(s.dma_start(
                                out=self.u_att[sg * 8:(sg + 1) * 8, :, cc, :]
                                .rearrange("tc p n -> p tc n"),
                                in_=self.estage[:, esl, 0:1024]
                                .rearrange("p (tc n) -> p tc n", n=128)), des)
                            last_e[esl] = (des, self.CNT[des])
                    spillq.append(sp_v)
                if len(spillq) >= 3:
                    spillq.pop(0)()

            want = ((it + 1) * len(dq_jobs)) // len(uc_order)
            while dq_next[0] < want:
                n = dq_next[0]
                srcw, icq, dest = dq_jobs[n]
                self.dq_front(srcw, icq, 1024, self.wf_ug,
                              self.wq_ug2[n % 3], n % 2)
                if n >= 2:
                    picq = dq_jobs[n - 2][1]
                    self.dq_back(self._dq["g"] - 2, picq, 1024, "hbm",
                                 dq_jobs[n - 2][2], self.wq_ug2[(n - 2) % 3])
                dq_next[0] += 1

        while dq_next[0] < len(dq_jobs):
            n = dq_next[0]
            srcw, icq, dest = dq_jobs[n]
            self.dq_front(srcw, icq, 1024, self.wf_ug, self.wq_ug2[n % 3], n % 2)
            if n >= 2:
                picq = dq_jobs[n - 2][1]
                self.dq_back(self._dq["g"] - 2, picq, 1024, "hbm",
                             dq_jobs[n - 2][2], self.wq_ug2[(n - 2) % 3])
            dq_next[0] += 1
        for n in (len(dq_jobs) - 2, len(dq_jobs) - 1):
            self.dq_back(self._dq["g"] - (len(dq_jobs) - 1 - n), dq_jobs[n][1], 1024,
                         "hbm", dq_jobs[n][2], self.wq_ug2[n % 3])
        while spillq:
            spillq.pop(0)()
        self._spq = None
        self.p2_mt_last = self.CNT["mt"]
        self.p2_ev_ms = self.CNT["ms"]
        self.p2_done_marks = (self.CNT["mv_"], tuple(self.CNT[d] for d in DE_SEMS),
                              self.CNT["dk0"], self.CNT["dk1"])

    def dq_init(self):
        self._dq = {"vdone": {}, "tdone": {}, "sdone": {}, "meta": {}, "g": -1}
        wkb = self.wkB[:, :]
        self.wf_ug = wkb[:, 2048:4096].rearrange("p (s n) -> p s n", s=2)
        self.wq_ug2 = [wkb[:, 4096:4608].bitcast(BF16),
                       wkb[:, 4608:5120].bitcast(BF16),
                       wkb[:, 5120:5632].bitcast(BF16)]

    def dq_chunk(self, srcw, icq, I, kind, dest, wf, wq, islot):
        self.dq_front(srcw, icq, I, wf, wq, islot, guard=1)
        self.dq_back(self._dq["g"], icq, I, kind, dest, wq)

    def dq_front(self, srcw, icq, I, wf, wq, islot, guard=3):
        inc, dinc, wait = self.inc, self.dinc, self.wait
        D = self._dq
        D["g"] += 1
        g = D["g"]
        D["meta"][g] = (icq, I, wq)
        vdone, tdone, sdone = D["vdone"], D["tdone"], D["sdone"]
        nblk = I // 32
        nb = I // 128
        slot = g % 2
        dbs = "db0" if slot == 0 else "db1"

        @self.on("sync")
        def _(s, srcw=srcw, icq=icq, slot=slot, islot=islot, dbs=dbs, I=I, g=g, wf=wf):
            if g >= 2:
                wait(s, "mv_", vdone[g - 2])
            dinc(s.dma_start(out=wf[:, islot, 0:I],
                             in_=srcw[icq * 128:(icq + 1) * 128, :]), dbs)

        @self.on("vector")
        def _(v, slot=slot, islot=islot, dbs=dbs, I=I, nblk=nblk, g=g, wf=wf):
            wait(v, dbs)
            v.tensor_reduce(out=self.bmax[:, 0, 0:nblk],
                            in_=wf[:, islot, 0:I].rearrange("p (b k) -> p b k", k=32),
                            axis=AX, op=ALU.max, apply_absolute_value=True)
            v.drain()
            v.tensor_scalar(out=self.bs[:, 0, 0:nblk],
                            in0=self.bmax[:, 0, 0:nblk],
                            scalar1=1.0 / 7.0, scalar2=1e-12,
                            op0=ALU.mult, op1=ALU.max)
            v.drain()
            v.reciprocal(out=self.binv[:, 0, 0:nblk],
                         in_=self.bs[:, 0, 0:nblk])
            v.drain()
            v.tensor_tensor(out=wf[:, islot, 0:I].rearrange("p (b k) -> p b k", k=32),
                            in0=wf[:, islot, 0:I].rearrange("p (b k) -> p b k", k=32),
                            in1=self.binv[:, 0, 0:nblk].broadcast_to((128, nblk, 32)),
                            op=ALU.mult)
            inc(v, "mv_")

        for q0 in range(0, I, 1024):
            @self.on("vector")
            def _(v, slot=slot, islot=islot, q0=q0, nblk=nblk, g=g, wf=wf, wq=wq,
                  guard=guard):
                if g >= guard and q0 == 0:
                    wait(v, "mt", tdone[g - guard])
                fq = self.fdq[:, 0:1024]
                v.tensor_scalar(out=self.iscr[:, 0:1024],
                                in0=wf[:, islot, q0:q0 + 1024],
                                scalar1=8.0, scalar2=None, op0=ALU.add)
                v.drain()
                v.scalar_tensor_tensor(out=fq, in0=self.iscr[:, 0:1024],
                                       scalar=7.5, in1=wf[:, islot, q0:q0 + 1024],
                                       op0=ALU.subtract, op1=ALU.is_le)
                v.drain()
                v.scalar_tensor_tensor(out=fq, in0=self.iscr[:, 0:1024],
                                       scalar=8.0, in1=fq,
                                       op0=ALU.subtract, op1=ALU.add)
                v.drain()
                b0 = q0 // 32
                v.tensor_tensor(out=wq[:, q0:q0 + 1024]
                                .rearrange("p (b k) -> p b k", k=32),
                                in0=fq.rearrange("p (b k) -> p b k", k=32),
                                in1=self.bs[:, 0, b0:b0 + 32]
                                .broadcast_to((128, 32, 32)),
                                op=ALU.mult)
                inc(v, "mv_")
        vdone[g] = self.CNT["mv_"]

    def dq_back(self, g, icq, I, kind, dest, wq):
        inc, dinc, wait = self.inc, self.dinc, self.wait
        D = self._dq
        vdone, tdone, sdone = D["vdone"], D["tdone"], D["sdone"]
        nb = I // 128
        for gq in range(nb // 8):
            @self.on("tensor")
            def _(t, nb=nb, g=g, gq=gq, wq=wq):
                if gq == 0:
                    wait(t, "mv_", vdone[g])
                    if g >= 1:
                        wait(t, "ms", sdone[g - 1])
                else:
                    wait(t, "ms")
                for bq in range(gq * 8, gq * 8 + 8):
                    li = t.transpose(self.pt[:, (bq % 8) * 128:(bq % 8) * 128 + 128],
                                     wq[:, bq * 128:(bq + 1) * 128],
                                     self.ident[:, :])
                inc(li, "mt")

            if kind != "sbuf" and getattr(self, "_spq", None) is not None:
                while self._spq:
                    self._spq.pop(0)()

            @self.on("scalar")
            def _(s, icq=icq, nb=nb, g=g, gq=gq, kind=kind, dest=dest):
                wait(s, "mt")
                if kind == "sbuf":
                    s.activation(
                        out=self.wdT.rearrange("p (b h) -> p b h", h=1024)
                        [:, gq * 8:(gq + 1) * 8, icq * 128:(icq + 1) * 128],
                        in_=self.pt[:, :].rearrange("p (b q) -> p b q", q=128),
                        func=AF.Identity)
                    inc(s, "ms")
                else:
                    wait(s, "dq0", self.CNT["dq0"])
                    wait(s, "dq1", self.CNT["dq1"])
                    for half in range(2):
                        s.activation(out=self.pdqst[:, half, 0:512],
                                     in_=self.pt[:, half * 512:(half + 1) * 512],
                                     func=AF.Identity)
                    inc(s, "ms")

            if kind != "sbuf":
                dqm = self.CNT["ms"]

                def sp_dq(icq=icq, g=g, gq=gq, dest=dest, dqm=dqm):
                    @self.on("sync")
                    def _(s, icq=icq, g=g, gq=gq, dest=dest, dqm=dqm):
                        wait(s, "ms", dqm)
                        for half in range(2):
                            dws = "dq0" if half == 0 else "dq1"
                            dinc(s.dma_start(
                                out=dest[icq, gq * 8 + half * 4:gq * 8 + half * 4 + 4, :, :]
                                .rearrange("h p q -> p h q"),
                                in_=self.pdqst[:, half, 0:512]
                                .rearrange("p (h q) -> p h q", q=128)), dws)
                if getattr(self, "_spq", None) is not None:
                    self._spq.append(sp_dq)
                else:
                    sp_dq()
        tdone[g] = self.CNT["mt"]
        sdone[g] = self.CNT["ms"]

    def pdq_down(self):
        p2_ev_last, p2_de, p2_dk0, p2_dk1 = self.p2_done_marks
        wait = self.wait

        @self.on("sync")
        def _(s):
            wait(s, "mv_", p2_ev_last)
            wait(s, "ms", self.p2_ev_ms)
            wait(s, "dk0", p2_dk0)
            wait(s, "dk1", p2_dk1)
            for dd, vv in zip(DE_SEMS, p2_de):
                wait(s, dd, vv)
            wait(s, "mt", self._dq["tdone"][self._dq["g"]])
        wf_dn = self.wkB[:, 0:6144].rearrange("p (s n) -> p s n", s=2)
        wq_dn = self.wkB[:, 6144:7680].bitcast(BF16)
        for icq in range(HC):
            self.dq_chunk(self.wd_d, icq, 3072, "sbuf", None, wf_dn, wq_dn, icq % 2)
        self.pdq_done = self.CNT["mv_"]
        self.dq_spill = (self.CNT["dq0"], self.CNT["dq1"])

    def p3(self):
        inc, dinc, wait = self.inc, self.dinc, self.wait
        scan_done = {}

        for c in range(16):
            slot = c % 2
            das = "da0" if slot == 0 else "da1"

            @self.on("sync")
            def _(s, c=c, slot=slot, das=das):
                if c == 0:
                    wait(s, "mv_", self.pdq_done)
                    for dd, vv in zip(DE_SEMS, self.p2_done_marks[1]):
                        wait(s, dd, vv)
                    wait(s, "dk0", self.p2_done_marks[2])
                    wait(s, "dk1", self.p2_done_marks[3])
                if c >= 2:
                    wait(s, "mv_", scan_done[c - 2])
                dinc(s.dma_start(out=self.ubuf[:, slot, :], in_=self.u_sp[c, :, :]), das)

            @self.on("vector")
            def _(v, c=c, slot=slot, das=das):
                wait(v, das)
                v.tensor_tensor_scan(out=self.ydisc[:, :],
                                     data0=self.lam_t[:, c % 8:c % 8 + 1]
                                     .broadcast_to((128, T)),
                                     data1=self.ubuf[:, slot, :],
                                     initial=0.0, op0=ALU.mult, op1=ALU.add)
                v.drain()
                v.tensor_copy(out=self.S_loc[:, c:c + 1], in_=self.ydisc[:, T - 1:T])
                v.drain()
                scan_done[c] = inc(v, "mv_")

        @self.on("tensor")
        def _(t):
            wait(t, "mv_")
            inc(t.transpose(self.px[0:16, 0:128], self.S_loc[:, :], self.identf[:, :]),
                "mt")

        @self.on("vector")
        def _(v):
            wait(v, "mt")
            v.tensor_copy(out=self.ag_sb[0:16, :], in_=self.px[0:16, 0:128])
            v.drain()
            inc(v, "mv_")

        @self.on("scalar")
        def _(s):
            wait(s, "mv_")
            dinc(s.dma_start(out=self.ag_in[:, :], in_=self.ag_sb[0:16, :]), "dci")

        @self.on("gpsimd")
        def _(g):
            wait(g, "dci")
            inc(g.collective_compute("AllGather", ALU.bypass,
                                     replica_groups=[list(range(8))],
                                     ins=[self.ag_in.opt()], outs=[self.ag_out.opt()]),
                "mc")
            wait(g, "mc")
            dinc(g.dma_start(out=self.ag_sb[:, :], in_=self.ag_out[:, :]), "dcc")

        @self.on("tensor")
        def _(t):
            wait(t, "dcc")
            inc(t.matmul(self.px[0:128, 0:16], self.ag_sb[:, :], self.sel_sb[:, :],
                         start=True, stop=True), "mt")

        @self.on("vector")
        def _(v):
            wait(v, "mt")
            v.tensor_copy(out=self.S_prev[:, :], in_=self.px[0:128, 0:16])
            v.drain()
            self.sprev_ready = inc(v, "mv_")

    def p4(self):
        inc, wait = self.inc, self.wait
        prev = {}
        for st in range(32):
            @self.on("scalar")
            def _(s, st=st):
                if st == 0:
                    wait(s, "mv_", self.r1_scales)
                    wait(s, "mt", self.p2_mt_last)
                s.activation(out=self.iscr[:, (st % 2) * 1024:(st % 2) * 1024 + 1024],
                             in_=self.arenaA[:, st * 1024:(st + 1) * 1024],
                             func=AF.Identity, scale=self.scales[:, 1:2],
                             bias=self.consts[:, 0:1])
                s.drain()
                s.activation(out=self.arenaA[:, st * 1024:(st + 1) * 1024],
                             in_=self.iscr[:, (st % 2) * 1024:(st % 2) * 1024 + 1024],
                             func=AF.Identity, scale=self.scales[:, 0:1],
                             bias=self.scales[:, 17:18])
                s.drain()
                prev[st] = inc(s, "ms")
        self.p4_done = ("ms", self.CNT["ms"])

    def p5(self):
        inc, dinc, wait = self.inc, self.dinc, self.wait
        mt_of = {}
        mvred = {}
        ev_mm = {}
        spillq = []
        self.p65_init()
        self.ring_init()
        att_next = [0]
        git = -1
        gsg = -1

        for wi, (wT, spill, mcols) in enumerate(
                [(self.wuT_sp, self.up_sp, self.upmaxc),
                 (self.wgT_sp, self.gate_sp, self.gatemaxc)]):
            for ic in range(IC):
                git += 1
                slot = git % 2
                dws = "dw0" if slot == 0 else "dw1"

                @self.on("sync")
                def _(s, ic=ic, slot=slot, dws=dws, git=git, wT=wT):
                    if git < 2:
                        wait(s, "dq0", self.dq_spill[0])
                        wait(s, "dq1", self.dq_spill[1])
                    if git >= 2:
                        wait(s, "mt", mt_of[(git - 2, 1)])
                    dinc(s.dma_start(out=self.wstream[:, slot, :, :],
                                     in_=wT[ic, :, :, :].rearrange("h p q -> p h q")),
                         dws)

                for g2 in range(2):
                    banks = self.ralloc(4)
                    bmarks = [self.rmark[b] for b in banks]
                    t0 = g2 * 2048

                    @self.on("tensor")
                    def _(t, ic=ic, slot=slot, dws=dws, git=git, g2=g2,
                          banks=banks, bmarks=bmarks, t0=t0):
                        if g2 == 0:
                            wait(t, dws)
                            if git == 0:
                                wait(t, "ms")
                                wait(t, "mv_")
                                wait(t, "mt", self.p2_mt_last)
                        for m in bmarks:
                            if m is not None:
                                wait(t, m[0], m[1])
                        for hc in range(HC):
                            for j in range(4):
                                li = t.matmul(self.rbank[banks[j]],
                                              self.wstream[:, slot, hc, :],
                                              self.xnA(hc, t0 + j * 512, 512),
                                              start=(hc == 0), stop=(hc == HC - 1))
                        mt_of[(git, g2)] = inc(li, "mt")

                    for pj in range(2):
                        gsg += 1
                        esl = gsg % 4
                        des = DE_SEMS[esl]
                        b0_, b1_ = banks[2 * pj], banks[2 * pj + 1]
                        tt0 = t0 + pj * 1024

                        @self.on("scalar")
                        def _(s, ic=ic, git=git, g2=g2, gsg=gsg, esl=esl, des=des,
                              b0_=b0_, b1_=b1_, tt0=tt0, spill=spill):
                            wait(s, "mt", mt_of[(git, g2)])
                            le = self.last_e.get(esl)
                            if le is not None:
                                wait(s, le[0], le[1])
                            if gsg >= 4:
                                wait(s, "mv_", mvred[gsg - 4])
                            s.activation(out=self.estage[:, esl, 0:512],
                                         in_=self.rbank[b0_], func=AF.Identity)
                            s.activation(out=self.estage[:, esl, 512:1024],
                                         in_=self.rbank[b1_], func=AF.Identity)
                            m = ("ms", inc(s, "ms"))
                            self.rmark[b0_] = m
                            self.rmark[b1_] = m
                            ev_mm[gsg] = m[1]
                        spillq.append((ic, gsg, esl, des, tt0, spill))
                        if len(spillq) >= 3:
                            sic, sgsg, sesl, sdes, stt0, sspill = spillq.pop(0)

                            @self.on("sync")
                            def _(s, sic=sic, sgsg=sgsg, sesl=sesl, sdes=sdes,
                                  stt0=stt0, sspill=sspill):
                                wait(s, "ms", ev_mm[sgsg])
                                self.last_e[sesl] = (sdes, dinc(s.dma_start(
                                    out=sspill[sic, :, stt0:stt0 + 1024],
                                    in_=self.estage[:, sesl, 0:1024]), sdes))

                        @self.on("vector")
                        def _(v, gsg=gsg, esl=esl, mcols=mcols, git=git, g2=g2,
                              pj=pj, wi=wi):
                            m = self.rmark[0]
                            wait(v, "ms", self.CNT["ms"])
                            lit = (git - wi * IC) * 4 + g2 * 2 + pj
                            v.tensor_reduce(out=mcols[:, lit:lit + 1],
                                            in_=self.estage[:, esl, 0:1024],
                                            axis=AX, op=ALU.max,
                                            apply_absolute_value=True)
                            mvred[gsg] = inc(v, "mv_")

                want = ((git + 1) * NTT) // (2 * IC)
                while att_next[0] < want:
                    self.p65_front(att_next[0])
                    if att_next[0] >= 1:
                        self.p65_back(att_next[0] - 1)
                    att_next[0] += 1

        while spillq:
            sic, sgsg, sesl, sdes, stt0, sspill = spillq.pop(0)

            @self.on("sync")
            def _(s, sic=sic, sgsg=sgsg, sesl=sesl, sdes=sdes, stt0=stt0,
                  sspill=sspill):
                wait(s, "ms", ev_mm[sgsg])
                self.last_e[sesl] = (sdes, dinc(s.dma_start(
                    out=sspill[sic, :, stt0:stt0 + 1024],
                    in_=self.estage[:, sesl, 0:1024]), sdes))

        while att_next[0] < NTT:
            self.p65_front(att_next[0])
            if att_next[0] >= 1:
                self.p65_back(att_next[0] - 1)
            att_next[0] += 1
        self.p65_back(NTT - 1)
        self.p65_finish()
        self.p5_done_ms = self.CNT["ms"]
        self.p5_done_mv = self.CNT["mv_"]
        self.p5_done_de = tuple(self.CNT[d] for d in DE_SEMS)

    def r2(self):
        inc, wait = self.inc, self.wait
        self.xpart_max(self.upmaxc[:, 0:96], 1)
        self.xpart_max(self.gatemaxc[:, 0:96], 2)
        self.all_reduce_max()
        self.bcast_scale(1, 2, 3)
        self.bcast_scale(2, 4, 5)

        @self.on("vector")
        def _(v):
            wait(v, "mv_")
            v.tensor_scalar(out=self.scales[:, 10:11], in0=self.scales[:, 4:5],
                            scalar1=-CQ, scalar2=None, op0=ALU.mult)
            v.tensor_scalar(out=self.scales[:, 11:12], in0=self.scales[:, 4:5],
                            scalar1=1.0 / 65536.0, scalar2=None, op0=ALU.mult)
            v.tensor_scalar(out=self.scales[:, 16:17], in0=self.scales[:, 2:3],
                            scalar1=-CQ, scalar2=None, op0=ALU.mult)
            v.drain()
            self.r2_scales = inc(v, "mv_")

    def p6(self):
        inc, dinc, wait = self.inc, self.dinc, self.wait
        gi_mark = {}
        sig_mark = {}
        g2_mark = {}
        done = {}
        W = 2048
        NQ = IC * 2
        arf = self.arenaA[:, :].bitcast(F32)
        sigb = [arf[:, 0:W], arf[:, W:2 * W]]
        svb = [arf[:, 2 * W:3 * W], arf[:, 3 * W:4 * W]]
        esf = self.estage[:, :, :].rearrange("p a n -> p (a n)")
        est = [esf[:, 0:W], esf[:, W:2 * W]]
        le6 = {0: [self.last_e[0], self.last_e[1]],
               1: [self.last_e[2], self.last_e[3]]}

        def load(ic):
            slot = ic % 2
            das = "da0" if slot == 0 else "da1"

            @self.on("sync")
            def _(s, ic=ic, slot=slot, das=das):
                if ic == 0:
                    wait(s, "mv_", self.p65_done_mv)
                    for dd, vv in zip(DE_SEMS, self.p5_done_de):
                        wait(s, dd, vv)
                if ic >= 2:
                    wait(s, "mv_", done[ic - 2])
                dinc(s.dma_start(out=self.gbuf[:, slot, :],
                                 in_=self.gate_sp[ic, :, :]), das)

        def gi_stage(Q):
            ic, q = Q // 2, Q % 2
            slot = ic % 2
            das = "da0" if slot == 0 else "da1"
            Qp = Q % 2
            o0 = q * W

            @self.on("vector")
            def _(v, ic=ic, slot=slot, das=das, o0=o0, Q=Q, Qp=Qp):
                if o0 == 0:
                    wait(v, das)
                if Q >= 2:
                    wait(v, "ms", sig_mark[Q - 2])
                v.tensor_scalar(out=self.iscr[:, Qp * W:Qp * W + W],
                                in0=self.gbuf[:, slot, o0:o0 + W],
                                scalar1=self.scales[:, 5:6],
                                scalar2=self.consts[:, 0:1],
                                op0=ALU.mult, op1=ALU.add)
                gi_mark[Q] = inc(v, "mv_")

        def sig_stage(Q):
            Qp = Q % 2

            @self.on("scalar")
            def _(s, Q=Q, Qp=Qp):
                wait(s, "mv_", gi_mark[Q])
                if Q >= 2:
                    wait(s, "mv_", g2_mark[Q - 2])
                s.activation(out=sigb[Qp], in_=self.iscr[:, Qp * W:Qp * W + W],
                             func=AF.Sigmoid, scale=self.scales[:, 4:5],
                             bias=self.scales[:, 10:11])
                s.drain()
                s.activation(out=svb[Qp].bitcast(I32), in_=sigb[Qp],
                             func=AF.Identity, scale=65536.0,
                             bias=self.consts[:, 1:2])
                sig_mark[Q] = inc(s, "ms")

        def tail_stage(Q):
            Qp = Q % 2
            ic = Q // 2

            @self.on("vector")
            def _(v, ic=ic, Q=Q, Qp=Qp):
                wait(v, "ms", sig_mark[Q])
                for le in le6[Qp]:
                    wait(v, le[0], le[1])
                v.tensor_scalar(out=sigb[Qp], in0=svb[Qp].bitcast(I32),
                                scalar1=65535.0, scalar2=self.scales[:, 11:12],
                                op0=ALU.min, op1=ALU.mult)
                v.drain()
                v.scalar_tensor_tensor(out=est[Qp],
                                       in0=self.iscr[:, Qp * W:Qp * W + W],
                                       scalar=CQ, in1=sigb[Qp],
                                       op0=ALU.subtract, op1=ALU.mult)
                v.drain()
                v.tensor_reduce(out=self.g2maxc[:, Q:Q + 1],
                                in_=est[Qp], axis=AX,
                                op=ALU.max, apply_absolute_value=True)
                g2_mark[Q] = done[ic] = inc(v, "mv_")

        def dma_stage(Q):
            ic, q = Q // 2, Q % 2
            Qp = Q % 2
            o0 = q * W

            @self.on("sync")
            def _(s, ic=ic, o0=o0, Q=Q, Qp=Qp):
                wait(s, "mv_", g2_mark[Q])
                des = DE_SEMS[Qp]
                m = (des, dinc(s.dma_start(
                    out=self.g2_sp[ic, :, o0:o0 + W],
                    in_=est[Qp]), des))
                le6[Qp] = [m]
                self.last_e[Qp] = m

        for Q in range(NQ):
            if Q % 2 == 0:
                load(Q // 2)
            gi_stage(Q)
            sig_stage(Q)
            if Q >= 1:
                tail_stage(Q - 1)
            if Q >= 2:
                dma_stage(Q - 2)
        tail_stage(NQ - 1)
        dma_stage(NQ - 2)
        dma_stage(NQ - 1)

        self.p6_done = self.CNT["mv_"]
        self.p6_de = tuple(self.CNT[d] for d in DE_SEMS)

    def r3(self):
        self.xpart_max(self.g2maxc[:, 0:48], 3)
        self.all_reduce_max()
        self.bcast_scale(3, 6, 7)
        self.r3_scales = self.CNT["mv_"]

    def p65_init(self):
        self._a65 = {"blk_done": {}, "ev_prev": {}, "mt65": {}, "dma": {}}

    def p65_front(self, tc):
        inc, dinc, wait = self.inc, self.dinc, self.wait
        blk_done = self._a65["blk_done"]
        mt65 = self._a65["mt65"]
        slot = tc % 2
        das = "da0" if slot == 0 else "da1"
        dbs = "db0" if slot == 0 else "db1"
        t0 = tc * 128

        @self.on("sync")
        def _(s, tc=tc, slot=slot, das=das, dbs=dbs, t0=t0):
            if tc == 0:
                wait(s, "mv_", self.sprev_ready)
                for dd, vv in zip(DE_SEMS, self.p2_done_marks[1]):
                    wait(s, dd, vv)
                wait(s, "dk0", self.p2_done_marks[2])
                wait(s, "dk1", self.p2_done_marks[3])
            if tc >= 2:
                wait(s, "mv_", blk_done[tc - 2])
            dinc(s.dma_start(out=self.att_u[:, slot, :],
                             in_=self.u_att[tc, :, :, :].rearrange("p c n -> p (c n)")),
                 das)
            dinc(s.dma_start(out=self.att_r[:, slot, :],
                             in_=self.r_att[tc, :, :, :].rearrange("p c n -> p (c n)")),
                 dbs)

        @self.on("vector")
        def _(v, tc=tc, slot=slot, das=das, dbs=dbs):
            wait(v, das)
            wait(v, dbs)
            for c in range(16):
                init = self.S_prev[:, c:c + 1] if tc == 0 else self.carries[:, c:c + 1]
                v.tensor_tensor_scan(
                    out=self.att_y[:, c * 128:(c + 1) * 128],
                    data0=self.lam_t[:, c % 8:c % 8 + 1].broadcast_to((128, 128)),
                    data1=self.att_u[:, slot, c * 128:(c + 1) * 128],
                    initial=init, op0=ALU.mult, op1=ALU.add)
            v.drain()
            v.tensor_copy(out=self.carries[:, :],
                          in_=self.att_y[:, :].rearrange("p (c n) -> p c n", n=128)
                          [:, :, 127:128].rearrange("p c n -> p (c n)"))
            v.drain()
            for c in range(8):
                v.scalar_tensor_tensor(
                    out=self.att_n[:, c * 128:(c + 1) * 128],
                    in0=self.att_u[:, slot, c * 128:(c + 1) * 128],
                    scalar=self.gamma_t[:, c:c + 1],
                    in1=self.att_y[:, c * 128:(c + 1) * 128],
                    op0=ALU.mult, op1=ALU.add)
            for c in range(8):
                v.scalar_tensor_tensor(
                    out=self.att_d[:, c * 128:(c + 1) * 128],
                    in0=self.att_u[:, slot, (8 + c) * 128:(9 + c) * 128],
                    scalar=self.gamma_t[:, c:c + 1],
                    in1=self.att_y[:, (8 + c) * 128:(9 + c) * 128],
                    op0=ALU.mult, op1=ALU.add)
            v.drain()
            v.tensor_scalar(out=self.att_d[:, :], in0=self.att_d[:, :], scalar1=1e-6,
                            scalar2=None, op0=ALU.add)
            v.drain()
            v.reciprocal(out=self.att_y[:, 0:1024], in_=self.att_d[:, :])
            v.drain()
            v.tensor_tensor(out=self.att_n[:, :], in0=self.att_n[:, :],
                            in1=self.att_y[:, 0:1024], op=ALU.mult)
            v.drain()
            if tc >= 2:
                wait(v, "mt", mt65[tc - 2])
            v.tensor_tensor(out=self.ratio[:, slot, :, :].rearrange("p c n -> p (c n)"),
                            in0=self.att_n[:, :], in1=self.att_r[:, slot, :],
                            op=ALU.mult)
            blk_done[tc] = inc(v, "mv_")

    def p65_back(self, tc):
        inc, dinc, wait = self.inc, self.dinc, self.wait
        blk_done = self._a65["blk_done"]
        ev_prev = self._a65["ev_prev"]
        mt65 = self._a65["mt65"]
        adma = self._a65["dma"]
        slot = tc % 2

        banks = self.ralloc(2)
        bmarks = [self.rmark[b] for b in banks]

        @self.on("tensor")
        def _(t, tc=tc, banks=banks, bmarks=bmarks):
            wait(t, "mv_", blk_done[tc])
            if tc == 0:
                wait(t, "dg", self.dg_wo)
            for m in bmarks:
                if m is not None:
                    wait(t, m[0], m[1])
            for c in range(8):
                t.matmul(self.rbank[banks[0]], self.ratio[:, slot, c, :],
                         self.wo_sb[:, c * 1024:c * 1024 + 512],
                         start=(c == 0), stop=False)
                t.matmul(self.rbank[banks[1]], self.ratio[:, slot, c, :],
                         self.wo_sb[:, c * 1024 + 512:(c + 1) * 1024],
                         start=(c == 0), stop=False)
            t.matmul(self.rbank[banks[0]], self.ones_b[0:1, 0:128],
                     self.bo_row[0:1, 0:512], start=False, stop=True)
            li = t.matmul(self.rbank[banks[1]], self.ones_b[0:1, 0:128],
                          self.bo_row[0:1, 512:1024], start=False, stop=True)
            mt65[tc] = inc(li, "mt")

        @self.on("scalar")
        def _(s, tc=tc, slot=slot, banks=banks):
            wait(s, "mt", mt65[tc])
            if tc >= 2:
                wait(s, adma[tc - 2][0], adma[tc - 2][1])
            s.activation(out=self.xnstage[:, slot, 0:512],
                         in_=self.rbank[banks[0]], func=AF.Identity)
            s.activation(out=self.xnstage[:, slot, 512:1024],
                         in_=self.rbank[banks[1]], func=AF.Identity)
            m = ("ms", inc(s, "ms"))
            self.rmark[banks[0]] = m
            self.rmark[banks[1]] = m
            ev_prev[tc] = m[1]

        @self.on("sync")
        def _(s, tc=tc, slot=slot):
            wait(s, "ms", ev_prev[tc])
            dys = "dy0" if slot == 0 else "dy1"
            adma[tc] = (dys, dinc(s.dma_start(
                out=self.att_sp[tc, :, :], in_=self.xnstage[:, slot, :]), dys))

    def p65_finish(self):
        self.p65_done_ms = self.CNT["ms"]
        self.p65_done_mv = self.CNT["mv_"]
        self.p65_att_dy = (self.CNT["dy0"], self.CNT["dy1"])

    def p675(self):
        inc, dinc, wait = self.inc, self.dinc, self.wait
        done = {}
        ci_mark = {}
        pv_mark = {}
        W = 2048
        ari = self.arenaA[:, :].bitcast(I32)
        g2i = [ari[:, 0:W], ari[:, 2 * W:3 * W]]
        upi = [ari[:, W:2 * W], ari[:, 3 * W:4 * W]]
        arb = self.arenaA[:, :]
        upq = [arb[:, 16384:16384 + W], arb[:, 16384 + W:16384 + 2 * W]]
        esf = self.estage[:, :, :].rearrange("p a n -> p (a n)")
        est = [esf[:, 0:W], esf[:, W:2 * W]]
        le6 = {0: [self.last_e[0]], 1: [self.last_e[1]]}

        def spill(Q):
            ic, q = Q // 2, Q % 2

            @self.on("sync")
            def _(s, ic=ic, q=q, Q=Q):
                wait(s, "mv_", pv_mark[Q])
                Qp = Q % 2
                des = DE_SEMS[Qp]
                m = (des, dinc(s.dma_start(
                    out=self.o_att[q * 16:(q + 1) * 16, :, ic, :]
                    .rearrange("tc p n -> p tc n"),
                    in_=est[Qp].rearrange("p (tc n) -> p tc n", n=128)), des))
                le6[Qp] = [m]
                self.last_e[Qp] = m

        for ic in range(IC):
            slot = ic % 2
            das = "da0" if slot == 0 else "da1"
            dbs = "db0" if slot == 0 else "db1"

            @self.on("sync")
            def _(s, ic=ic, slot=slot, das=das, dbs=dbs):
                if ic == 0:
                    for dd, vv in zip(DE_SEMS, self.p6_de):
                        wait(s, dd, vv)
                    wait(s, "dy0", self.p65_att_dy[0])
                    wait(s, "dy1", self.p65_att_dy[1])
                if ic >= 2:
                    wait(s, "mv_", done[ic - 2])
                dinc(s.dma_start(out=self.g2in[:, slot, :], in_=self.g2_sp[ic, :, :]), das)
                dinc(s.dma_start(out=self.upin[:, slot, :], in_=self.up_sp[ic, :, :]), dbs)

            for q in range(2):
                Q = ic * 2 + q
                Qp = Q % 2
                o0 = q * W
                if Q >= 2:
                    spill(Q - 2)

                @self.on("scalar")
                def _(s, ic=ic, slot=slot, das=das, dbs=dbs, o0=o0, Q=Q, Qp=Qp):
                    if Q == 0:
                        wait(s, "mv_", self.r3_scales)
                        wait(s, "mv_", self.p6_done)
                    if o0 == 0:
                        wait(s, das)
                        wait(s, dbs)
                    if Q >= 2:
                        wait(s, "mv_", pv_mark[Q - 2])
                    s.activation(out=upi[Qp],
                                 in_=self.upin[:, slot, o0:o0 + W],
                                 func=AF.Identity, scale=self.scales[:, 3:4],
                                 bias=self.consts[:, 0:1])
                    s.activation(out=g2i[Qp],
                                 in_=self.g2in[:, slot, o0:o0 + W],
                                 func=AF.Identity, scale=self.scales[:, 7:8],
                                 bias=self.consts[:, 0:1])
                    if Q % 2 == 0:
                        s.drain()
                        s.activation(out=upq[Qp], in_=upi[Qp],
                                     func=AF.Identity, scale=self.scales[:, 2:3],
                                     bias=self.scales[:, 16:17])
                    ci_mark[Q] = inc(s, "ms")

                @self.on("vector")
                def _(v, ic=ic, o0=o0, Q=Q, Qp=Qp):
                    wait(v, "ms", ci_mark[Q])
                    for le in le6[Qp]:
                        wait(v, le[0], le[1])
                    if Q % 2 == 1:
                        v.tensor_scalar(out=upq[Qp], in0=upi[Qp],
                                        scalar1=CQ, scalar2=self.scales[:, 2:3],
                                        op0=ALU.subtract, op1=ALU.mult)
                        v.drain()
                    v.scalar_tensor_tensor(out=est[Qp], in0=g2i[Qp],
                                           scalar=CQ, in1=upq[Qp],
                                           op0=ALU.subtract, op1=ALU.mult)
                    v.drain()
                    v.tensor_reduce(out=self.pmaxc[:, Q:Q + 1],
                                    in_=est[Qp], axis=AX,
                                    op=ALU.max, apply_absolute_value=True)
                    pv_mark[Q] = done[ic] = inc(v, "mv_")
        spill(IC * 2 - 2)
        spill(IC * 2 - 1)

        self.p675_done = self.CNT["mv_"]
        self.p675_de = tuple(self.CNT[d] for d in DE_SEMS)

    def r4(self):
        inc, wait = self.inc, self.wait
        self.xpart_max(self.pmaxc[:, 0:48], 4)
        self.all_reduce_max()
        self.bcast_scale(4, 13, 14, raw_col=12)

        @self.on("vector")
        def _(v):
            wait(v, "mv_")
            v.tensor_tensor(out=self.scales[:, 15:16], in0=self.scales[:, 6:7],
                            in1=self.scales[:, 12:13], op=ALU.mult)
            v.drain()
            v.tensor_scalar(out=self.scales[:, 8:9], in0=self.scales[:, 15:16],
                            scalar1=1.0 / QMAX, scalar2=1e-12, op0=ALU.mult, op1=ALU.max)
            v.drain()
            v.reciprocal(out=self.scales[:, 15:16], in_=self.scales[:, 8:9])
            v.drain()
            v.tensor_tensor(out=self.scales[:, 9:10], in0=self.scales[:, 6:7],
                            in1=self.scales[:, 15:16], op=ALU.mult)
            v.drain()
            self.r4_scales = inc(v, "mv_")

    def p7(self):
        inc, dinc, wait = self.inc, self.dinc, self.wait
        oq_mark = {}
        mm_mark = {}
        ev_mark = {}
        dma_mark = {}
        att_mark = {}
        pa2 = self.bigps[:, 0:1024]

        def loads(tc):
            slot = tc % 2
            das = "da0" if slot == 0 else "da1"
            dbs = "db0" if slot == 0 else "db1"
            dxs = "dx0" if slot == 0 else "dx1"
            t0 = tc * 128

            @self.on("sync")
            def _(s, tc=tc, slot=slot, das=das, dbs=dbs, dxs=dxs, t0=t0):
                if tc == 0:
                    wait(s, "mv_", self.p675_done)
                    for dd, vv in zip(DE_SEMS, self.p675_de):
                        wait(s, dd, vv)
                    wait(s, "dy0", self.p65_att_dy[0])
                    wait(s, "dy1", self.p65_att_dy[1])
                if tc >= 2:
                    wait(s, "mv_", ev_mark[tc - 2])
                dinc(s.dma_start(out=self.ostage[:, slot, :],
                                 in_=self.o_att[tc, :, :, :]
                                 .rearrange("p c n -> p (c n)")), das)
                dinc(s.dma_start(out=self.xld[:, slot, :], in_=self.x_d[t0:t0 + 128, :]),
                     dxs)
                dinc(s.dma_start(out=self.attnA(tc), in_=self.att_sp[tc, :, :]), dbs)

        def oquant(tc):
            slot = tc % 2
            das = "da0" if slot == 0 else "da1"
            for hf in range(2):
                h0 = hf * 1536

                @self.on("scalar")
                def _(s, tc=tc, slot=slot, das=das, h0=h0, hf=hf):
                    if tc == 0 and hf == 0:
                        wait(s, "mv_", self.r4_scales)
                    if hf == 0:
                        wait(s, das)
                        if tc >= 2:
                            wait(s, "mt", mm_mark[tc - 2])
                    if tc >= 1 or hf >= 1:
                        wait(s, "mv_")
                    s.activation(out=self.iscr[:, 0:1536],
                                 in_=self.ostage[:, slot, h0:h0 + 1536],
                                 func=AF.Identity, scale=self.scales[:, 9:10],
                                 bias=self.consts[:, 0:1])
                    s.drain()
                    inc(s, "ms")

                @self.on("vector")
                def _(v, tc=tc, h0=h0):
                    wait(v, "ms")
                    oq = self.oq if tc % 2 == 0 else self.oq2
                    v.tensor_scalar(out=oq[:, h0:h0 + 1536], in0=self.iscr[:, 0:1536],
                                    scalar1=CQ, scalar2=self.scales[:, 8:9],
                                    op0=ALU.subtract, op1=ALU.mult)
                    oq_mark[tc] = inc(v, "mv_")

        def mms(tc):
            @self.on("tensor")
            def _(t, tc=tc):
                wait(t, "mv_", oq_mark[tc])
                if tc == 0:
                    wait(t, "ms", self.p65_done_ms)
                if tc >= 2:
                    wait(t, "mv_", ev_mark[tc - 2])
                oq = self.oq if tc % 2 == 0 else self.oq2
                ps = self.pa if tc % 2 == 0 else pa2
                for ic in range(IC):
                    t.matmul(ps[:, 0:512], oq[:, ic * 128:(ic + 1) * 128],
                             self.wdT[:, ic * 1024:ic * 1024 + 512],
                             start=(ic == 0), stop=(ic == IC - 1))
                    li = t.matmul(ps[:, 512:1024], oq[:, ic * 128:(ic + 1) * 128],
                                  self.wdT[:, ic * 1024 + 512:(ic + 1) * 1024],
                                  start=(ic == 0), stop=(ic == IC - 1))
                mm_mark[tc] = inc(li, "mt")

        def evac(tc):
            slot = tc % 2
            dxs = "dx0" if slot == 0 else "dx1"
            dbs = "db0" if slot == 0 else "db1"

            @self.on("vector")
            def _(v, tc=tc, slot=slot, dxs=dxs, dbs=dbs):
                wait(v, "mt", mm_mark[tc])
                wait(v, dxs)
                wait(v, dbs)
                if tc >= 2:
                    wait(v, dma_mark[tc - 2][0], dma_mark[tc - 2][1])
                ps = self.pa if tc % 2 == 0 else pa2
                v.scalar_tensor_tensor(out=self.outst[:, slot, :], in0=ps[:, :],
                                       scalar=1.0, in1=self.attnA(tc),
                                       op0=ALU.mult, op1=ALU.add)
                v.drain()
                v.tensor_tensor(out=self.outst[:, slot, :], in0=self.outst[:, slot, :],
                                in1=self.xld[:, slot, :], op=ALU.add)
                ev_mark[tc] = inc(v, "mv_")

        def store(tc):
            slot = tc % 2
            t0 = tc * 128

            @self.on("sync")
            def _(s, tc=tc, slot=slot, t0=t0):
                wait(s, "mv_", ev_mark[tc])
                dys = "dy0" if slot == 0 else "dy1"
                dma_mark[tc] = (dys, dinc(s.dma_start(
                    out=self.y_d[t0:t0 + 128, :], in_=self.outst[:, slot, :]), dys))

        for tc in range(NTT):
            loads(tc)
            oquant(tc)
            mms(tc)
            if tc >= 1:
                evac(tc - 1)
                store(tc - 1)
        evac(NTT - 1)
        store(NTT - 1)

    def fin(self):
        @self.on("sync")
        def _(s):
            self.wait(s, "dy0")
            self.wait(s, "dy1")


def build_nc():
    b = Builder()
    nc = b.build()
    b.ctx.close()
    return nc


_NC_CACHE = None


def kernel(L0_attn_inputs, L0_attn_W_rkv, L0_attn_b_rkv, L0_attn_W_o, L0_attn_b_o,
           L0_attn_nu_log, L0_attn_gamma_log, L0_ffn_w_gate, L0_ffn_w_up, L0_ffn_w_down):
    global _NC_CACHE
    x = np.ascontiguousarray(np.asarray(L0_attn_inputs, dtype=np.float32))
    wrkv = np.ascontiguousarray(np.asarray(L0_attn_W_rkv, dtype=np.float32))
    brkv = np.ascontiguousarray(np.asarray(L0_attn_b_rkv, dtype=np.float32))
    wo = np.ascontiguousarray(np.asarray(L0_attn_W_o, dtype=np.float32))
    bo = np.ascontiguousarray(np.asarray(L0_attn_b_o, dtype=np.float32))
    nulog = np.ascontiguousarray(np.asarray(L0_attn_nu_log, dtype=np.float32))
    glog = np.ascontiguousarray(np.asarray(L0_attn_gamma_log, dtype=np.float32))
    wg = np.ascontiguousarray(np.asarray(L0_ffn_w_gate, dtype=np.float32))
    wu = np.ascontiguousarray(np.asarray(L0_ffn_w_up, dtype=np.float32))
    wd = np.ascontiguousarray(np.asarray(L0_ffn_w_down, dtype=np.float32))

    if _NC_CACHE is None:
        _NC_CACHE = build_nc()
    nc = _NC_CACHE

    in_maps = []
    for c in range(8):
        bb, hh = c // 2, c % 2
        sel = np.zeros((128, 16), np.float32)
        if hh == 1:
            for g in range(16):
                sel[16 * (c - 1) + g, g] = 1.0
        in_maps.append({
            "x": np.ascontiguousarray(x[bb, hh * T:(hh + 1) * T, :]),
            "wrkv": wrkv, "brkv": brkv, "wo": wo, "bo": bo,
            "nulog": nulog, "glog": glog, "wg": wg, "wu": wu, "wd": wd,
            "sel": sel,
        })
    res = run_bass_kernel_spmd(nc, in_maps, core_ids=list(range(8)))
    out = np.empty((B, L, H), dtype=np.float32)
    for c in range(8):
        bb, hh = c // 2, c % 2
        out[bb, hh * T:(hh + 1) * T, :] = res.results[c]["y"]
    return out


# version 23 (final)
# speedup vs baseline: 1.0845x; 1.0100x over previous
import numpy as np
from contextlib import ExitStack
import concourse.bass as bass
import concourse.mybir as mybir
from concourse.bass_utils import run_bass_kernel_spmd

F32 = mybir.dt.float32
BF16 = mybir.dt.bfloat16
I32 = mybir.dt.int32
AF = mybir.ActivationFunctionType
ALU = mybir.AluOpType
AX = mybir.AxisListType.X

B, L, H, U3 = 4, 8192, 1024, 3072
T = 4096
HC = 8
UC = 24
IC = 24
NT = 8
NTT = 32
CQ = 65536.0
DE_SEMS = ["de0", "de1", "de2", "de3"]
QMAX = 32767.0


class _RecInstr:
    __slots__ = ("_rq", "_rid")

    def __init__(self, rq, rid):
        self._rq = rq
        self._rid = rid

    def __getattr__(self, name):
        def call(*args, **kwargs):
            self._rq.append(("chain", self._rid, name, args, kwargs))
            return self
        return call


class _RecEngine:
    __slots__ = ("_name", "_q", "_n")

    def __init__(self, name):
        self._name = name
        self._q = []
        self._n = 0

    def __getattr__(self, name):
        if name == "then_inc":
            raise AttributeError(name)

        def call(*args, **kwargs):
            rid = self._n
            self._n += 1
            self._q.append(("call", rid, name, args, kwargs))
            return _RecInstr(self._q, rid)
        return call


class Builder:
    def __init__(self):
        self.nc = bass.Bass("TRN2", target_bir_lowering=False, debug=False)
        self.ctx = ExitStack()
        self.CNT = {}
        self.SEM = {}
        self._rec = {}

    def sem(self, name):
        self.SEM[name] = self.ctx.enter_context(self.nc.semaphore(name))
        self.CNT[name] = 0

    def inc(self, bi, name, amt=1):
        if not hasattr(bi, "then_inc"):
            bi = bi.drain()
        bi.then_inc(self.SEM[name], amt)
        self.CNT[name] += amt
        return self.CNT[name]

    def dinc(self, bi, name):
        return self.inc(bi, name, 16)

    def wait(self, eng, name, val=None):
        eng.wait_ge(self.SEM[name], self.CNT[name] if val is None else val)

    def sb(self, name, shape, dt):
        return self.ctx.enter_context(self.nc.sbuf_tensor(name, shape, dt))

    def on(self, engine_name):
        rec = self._rec.setdefault(engine_name, _RecEngine(engine_name))

        def deco(fn, *a, **k):
            fn(rec, *a, **k)
        return deco

    def flush_engines(self):
        for name, rec in self._rec.items():
            q = rec._q

            def replay(eng, q=q):
                objs = {}
                for kind, rid, meth, args, kwargs in q:
                    if kind == "call":
                        objs[rid] = getattr(eng, meth)(*args, **kwargs)
                    else:
                        objs[rid] = getattr(objs[rid], meth)(*args, **kwargs)
            getattr(self.block, name)(replay)
        self._rec = {}

    def build(self):
        nc = self.nc
        self.x_d = nc.dram_tensor("x", [T, H], F32, kind="ExternalInput").ap()
        self.wrkv_d = nc.dram_tensor("wrkv", [H, U3], F32, kind="ExternalInput").ap()
        self.brkv_d = nc.dram_tensor("brkv", [U3], F32, kind="ExternalInput").ap()
        self.wo_d = nc.dram_tensor("wo", [H, H], F32, kind="ExternalInput").ap()
        self.bo_d = nc.dram_tensor("bo", [H], F32, kind="ExternalInput").ap()
        self.nulog_d = nc.dram_tensor("nulog", [H], F32, kind="ExternalInput").ap()
        self.glog_d = nc.dram_tensor("glog", [H], F32, kind="ExternalInput").ap()
        self.wg_d = nc.dram_tensor("wg", [U3, H], F32, kind="ExternalInput").ap()
        self.wu_d = nc.dram_tensor("wu", [U3, H], F32, kind="ExternalInput").ap()
        self.wd_d = nc.dram_tensor("wd", [H, U3], F32, kind="ExternalInput").ap()
        self.sel_d = nc.dram_tensor("sel", [128, 16], F32, kind="ExternalInput").ap()
        self.y_d = nc.dram_tensor("y", [T, H], F32, kind="ExternalOutput").ap()
        dr = lambda n, s, d: nc.dram_tensor(n, s, d).ap()
        self.u_sp = dr("u_sp", [16, 128, T], BF16)
        self.r_sp = dr("r_sp", [8, 128, T], BF16)
        self.up_sp = dr("up_sp", [IC, 128, T], BF16)
        self.gate_sp = dr("gate_sp", [IC, 128, T], BF16)
        self.g2_sp = dr("g2_sp", [IC, 128, T], BF16)
        self.o_sp = dr("o_sp", [IC, 128, T], BF16)
        self.wuT_sp = dr("wuT_sp", [IC, HC, 128, 128], BF16)
        self.att_sp = dr("att_sp", [NTT, 128, 1024], BF16)
        self.u_att = dr("u_att", [NTT, 128, 16, 128], BF16)
        self.r_att = dr("r_att", [NTT, 128, 8, 128], BF16)
        self.o_att = dr("o_att", [NTT, 128, IC, 128], BF16)
        self.wgT_sp = dr("wgT_sp", [IC, HC, 128, 128], BF16)
        self.cc_in = dr("cc_in", [1, 16], F32)
        self.cc_out = dr("cc_out", [1, 16], F32)
        self.ag_in = dr("ag_in", [16, 128], F32)
        self.ag_out = dr("ag_out", [128, 128], F32)

        sb = self.sb
        self.arenaA = sb("arenaA", [128, 8 * T], BF16)
        self.wdT = sb("wdT", [128, IC * 1024], BF16)
        self.wo_sb = sb("wo_sb", [128, HC * 1024], BF16)
        self.wkB = sb("wkB", [128, 8192], F32)
        self.ident = sb("ident", [128, 128], BF16)
        self.identf = sb("identf", [128, 128], F32)
        self.ones_b = sb("ones_b", [1, 128], BF16)
        self.ones_f = sb("ones_f", [1, 128], F32)
        self.bo_row = sb("bo_row", [1, 1024], BF16)
        self.consts = sb("consts", [128, 8], F32)
        self.brkv_t = sb("brkv_t", [128, 24], F32)
        self.nulog_t = sb("nulog_t", [128, 8], F32)
        self.glog_t = sb("glog_t", [128, 8], F32)
        self.lam_t = sb("lam_t", [128, 8], F32)
        self.gamma_t = sb("gamma_t", [128, 8], F32)
        self.stats = sb("stats", [128, 2, 12], F32)
        self.mvt = sb("mvt", [128, 2, 2], F32)
        self.tiny = sb("tiny", [128, 12], F32)
        self.maxc = sb("maxc", [128, 36], F32)
        self.upmaxc = sb("upmaxc", [128, 96], F32)
        self.gatemaxc = sb("gatemaxc", [128, 96], F32)
        self.g2maxc = sb("g2maxc", [128, 96], F32)
        self.pmaxc = sb("pmaxc", [128, 96], F32)
        self.mx1 = sb("mx1", [128, 1], F32)
        self.cc_sb = sb("cc_sb", [1, 16], F32)
        self.ccr_sb = sb("ccr_sb", [1, 16], F32)
        self.scales = sb("scales", [128, 20], F32)
        self.S_loc = sb("S_loc", [128, 16], F32)
        self.S_prev = sb("S_prev", [128, 16], F32)
        self.carries = sb("carries", [128, 16], F32)
        self.ag_sb = sb("ag_sb", [128, 128], F32)
        self.agT_sb = None
        self.sel_sb = sb("sel_sb", [128, 16], F32)
        self.xld = sb("xld", [128, 2, 1024], F32)
        self.xnstage = sb("xnstage", [128, 2, 1024], BF16)
        self.wstream = sb("wstream", [128, 2, HC, 128], BF16)
        self.estage = sb("estage", [128, 4, 1024], BF16)
        self.iscr = sb("iscr", [128, 4096], I32)
        self.kstage = self.wkB[:, :].bitcast(BF16).rearrange(
            "p (c t n) -> p c t n", c=8, t=4)
        self.pdqst = self.xnstage
        self.bmax = sb("bmax", [128, 1, 96], F32)
        self.binv = sb("binv", [128, 1, 96], F32)
        self.bs = sb("bs", [128, 1, 96], F32)

        wkb = self.wkB[:, :]
        self.wf = wkb[:, 0:6144].rearrange("p (s n) -> p s n", s=2)
        self.wq = wkb[:, 6144:7680].bitcast(BF16)
        self.ubuf = wkb[:, 0:4096].bitcast(BF16).rearrange("p (s n) -> p s n", s=2)
        self.ydisc = wkb[:, 4096:8192]
        self.gbuf = self.ubuf
        self.sigb = wkb[:, 4096:6144].rearrange("p (s n) -> p s n", s=2)
        self.svb = wkb[:, 6144:8192].rearrange("p (s n) -> p s n", s=2)
        self.att_u = wkb[:, 0:2048].bitcast(BF16).rearrange("p (s n) -> p s n", s=2)
        self.att_r = wkb[:, 2048:3072].bitcast(BF16).rearrange("p (s n) -> p s n", s=2)
        self.att_y = wkb[:, 3072:5120]
        self.att_n = wkb[:, 5120:6144]
        self.att_d = wkb[:, 6144:7168]
        self.att_c = wkb[:, 7168:8192]
        self.ratio = self.att_c[:, :].bitcast(BF16).rearrange("p (s c n) -> p s c n", s=2, c=8)
        self.g2in = wkb[:, 0:4096].bitcast(BF16).rearrange("p (s n) -> p s n", s=2)
        self.upin = wkb[:, 4096:8192].bitcast(BF16).rearrange("p (s n) -> p s n", s=2)
        self.ostage = wkb[:, 0:3072].bitcast(BF16).rearrange("p (s n) -> p s n", s=2)
        self.oq = wkb[:, 3072:4608].bitcast(BF16)
        self.oq2 = wkb[:, 6656:8192].bitcast(BF16)
        self.outst = wkb[:, 4608:6656].rearrange("p (s n) -> p s n", s=2)
        self.upq_t = self.xnstage[:, :, :].rearrange("p s n -> p (s n)")

        self.bigps = self.ctx.enter_context(nc.psum_tensor("bigps", [128, 3584], F32))
        self.pt = self.ctx.enter_context(nc.psum_tensor("pt", [128, 1024], BF16))
        self.pa = self.bigps[:, 2048:3072]
        self.fdq = self.bigps[:, 2048:3072]
        self.px = self.bigps[:, 3072:3584]

        for n in ["dx0", "dx1", "dw0", "dw1", "dwg0", "dwg1", "da0", "da1", "db0", "db1",
                  "dsp", "de0", "de1", "de2", "de3", "dci", "dq0", "dq1", "dy0", "dy1", "dy", "dg", "dcc", "dk", "dk0", "dk1", "mt", "mv_", "ms", "mg", "mc"]:
            self.sem(n)

        self.block = self.ctx.enter_context(nc.Block())

        self.p0()
        self.p1()
        self.r1_start()
        self.dq_init()
        self.p2()
        self.pdq_down()
        self.r1_finish()
        self.p3()
        self.p4()
        self.p5()
        self.r2()
        self.p6()
        self.r3()
        self.p675()
        self.r4()
        self.p7()
        self.fin()
        self.flush_engines()
        return nc

    def bank(self, i):
        return self.bigps[:, i * 512:(i + 1) * 512]

    def ring_init(self):
        self.rbank = [self.bigps[:, i * 512:(i + 1) * 512] for i in range(7)]
        self.rbank.append(self.pt[:, :].bitcast(F32))
        self.rnext = 0
        self.rmark = {i: None for i in range(8)}

    def ralloc(self, n):
        idx = [(self.rnext + j) % 8 for j in range(n)]
        self.rnext = (self.rnext + n) % 8
        return idx

    def xnA(self, hc, t0, n):
        return self.arenaA[:, hc * T + t0: hc * T + t0 + n]

    def attnA(self, tc):
        return self.arenaA[:, tc * 1024:(tc + 1) * 1024]

    def p0(self):
        inc, dinc, wait = self.inc, self.dinc, self.wait

        @self.on("gpsimd")
        def _(g):
            g.iota(out=self.iscr[:, 0:128], pattern=[[1, 128]], base=0, channel_multiplier=-1)
            g.memset(self.consts[:, 0:1], 65536.5)
            g.memset(self.consts[:, 1:2], 0.5)
            g.memset(self.consts[:, 2:3], 8.0)
            g.memset(self.ones_b[:, :], 1.0)
            g.memset(self.cc_sb[:, :], 0.0)
            g.memset(self.ones_f[:, :], 1.0)
            g.drain()
            g.tensor_scalar(out=self.ident[:, :], in0=self.iscr[:, 0:128], scalar1=0.0,
                            scalar2=None, op0=ALU.is_equal)
            g.tensor_scalar(out=self.identf[:, :], in0=self.iscr[:, 0:128], scalar1=0.0,
                            scalar2=None, op0=ALU.is_equal)
            g.drain()
            self.mg_init = inc(g, "mg")
            dinc(g.dma_start(out=self.wo_sb.rearrange("p (c n) -> p c n", n=1024),
                        in_=self.wo_d.rearrange("(c p) n -> p c n", p=128)), "dg")
            self.dg_wo = dinc(g.dma_start(out=self.bo_row[:, :], in_=self.bo_d[None, :]), "dg")

        @self.on("sync")
        def _(s):
            dinc(s.dma_start(out=self.brkv_t[:, :], in_=self.brkv_d.rearrange("(c p) -> p c", p=128), allow_slow_non_contiguous=True), "da0")
            dinc(s.dma_start(out=self.nulog_t[:, :], in_=self.nulog_d.rearrange("(c p) -> p c", p=128), allow_slow_non_contiguous=True), "da0")
            dinc(s.dma_start(out=self.glog_t[:, :], in_=self.glog_d.rearrange("(c p) -> p c", p=128), allow_slow_non_contiguous=True), "da0")
            self.da_p0 = dinc(s.dma_start(out=self.sel_sb[:, :], in_=self.sel_d[:, :]), "da0")

        @self.on("scalar")
        def _(s):
            wait(s, "da0", self.da_p0)
            s.activation(out=self.tiny[:, 0:8], in_=self.nulog_t[:, :], func=AF.Exp)
            s.drain()
            s.activation(out=self.lam_t[:, :], in_=self.tiny[:, 0:8], func=AF.Exp, scale=-1.0)
            s.drain()
            inc(s, "ms")

        @self.on("vector")
        def _(v):
            wait(v, "da0", self.da_p0)
            v.tensor_tensor(out=self.glog_t[:, :], in0=self.nulog_t[:, :],
                            in1=self.glog_t[:, :], op=ALU.add)
            v.drain()
            inc(v, "mv_")

        @self.on("scalar")
        def _(s):
            wait(s, "mv_")
            s.activation(out=self.gamma_t[:, :], in_=self.glog_t[:, :], func=AF.Exp)
            s.drain()
            inc(s, "ms")

        @self.on("vector")
        def _(v):
            wait(v, "ms")
            v.tensor_scalar(out=self.gamma_t[:, :], in0=self.gamma_t[:, :], scalar1=-1.0,
                            scalar2=None, op0=ALU.add)
            v.drain()
            self.mv_gamma = inc(v, "mv_")

    def p1(self):
        inc, dinc, wait = self.inc, self.dinc, self.wait
        st_mark = {}
        ln_mark = {}
        xn_mark = {}
        tp_mark = {}
        ev_mark = {}
        xnA_v = self.arenaA[:, :].rearrange("p (c t) -> p c t", t=T)

        def load(tt):
            slot = tt % 2
            dxs = "dx0" if slot == 0 else "dx1"

            @self.on("sync")
            def _(s, tt=tt, slot=slot, dxs=dxs):
                if tt >= 2:
                    wait(s, "mv_", xn_mark[tt - 2])
                dinc(s.dma_start(out=self.xld[:, slot, :],
                                 in_=self.x_d[tt * 128:(tt + 1) * 128, :]), dxs)

        def stats(tt):
            slot = tt % 2
            dxs = "dx0" if slot == 0 else "dx1"

            @self.on("vector")
            def _(v, tt=tt, slot=slot, dxs=dxs):
                wait(v, dxs)
                if tt >= 2:
                    wait(v, "ms", ln_mark[tt - 2])
                v.bn_stats(out=self.stats[:, slot, 0:6], in_=self.xld[:, slot, 0:512])
                v.bn_stats(out=self.stats[:, slot, 6:12], in_=self.xld[:, slot, 512:1024])
                v.drain()
                v.bn_aggr(out=self.mvt[:, slot, :], in_=self.stats[:, slot, :])
                v.drain()
                v.tensor_scalar(out=self.tiny[:, slot:slot + 1],
                                in0=self.mvt[:, slot, 1:2], scalar1=1e-5,
                                scalar2=None, op0=ALU.add)
                st_mark[tt] = inc(v, "mv_")

        def lnexp(tt):
            slot = tt % 2

            @self.on("scalar")
            def _(s, tt=tt, slot=slot):
                wait(s, "mv_", st_mark[tt])
                if tt >= 2:
                    wait(s, "mv_", xn_mark[tt - 2])
                s.activation(out=self.tiny[:, 2 + slot:3 + slot],
                             in_=self.tiny[:, slot:slot + 1], func=AF.Ln)
                s.drain()
                s.activation(out=self.tiny[:, 4 + slot:5 + slot],
                             in_=self.tiny[:, 2 + slot:3 + slot], func=AF.Exp,
                             scale=-0.5)
                s.drain()
                s.activation(out=self.tiny[:, 6 + slot:7 + slot],
                             in_=self.tiny[:, 4 + slot:5 + slot], func=AF.Identity,
                             scale=-1.0)
                s.drain()
                s.activation(out=self.tiny[:, 8 + slot:9 + slot],
                             in_=self.mvt[:, slot, 0:1], func=AF.Identity,
                             scale=self.tiny[:, 6 + slot:7 + slot])
                s.drain()
                s.activation(out=self.xnstage[:, slot, :], in_=self.xld[:, slot, :],
                             func=AF.Identity, scale=self.tiny[:, 4 + slot:5 + slot],
                             bias=self.tiny[:, 8 + slot:9 + slot])
                ln_mark[tt] = inc(s, "ms")

        def xn(tt):
            slot = tt % 2

            @self.on("vector")
            def _(v, tt=tt, slot=slot):
                wait(v, "ms", ln_mark[tt])
                v.tensor_reduce(out=self.maxc[:, tt:tt + 1], in_=self.xnstage[:, slot, :],
                                axis=AX, op=ALU.max, apply_absolute_value=True)
                xn_mark[tt] = inc(v, "mv_")

        def transpose(tt):
            slot = tt % 2

            @self.on("tensor")
            def _(t, tt=tt, slot=slot):
                if tt == 0:
                    wait(t, "mg", self.mg_init)
                wait(t, "mv_", xn_mark[tt])
                if tt >= 1:
                    wait(t, "ms", ev_mark[tt - 1])
                for hc in range(HC):
                    li = t.transpose(self.pt[:, hc * 128:(hc + 1) * 128],
                                     self.xnstage[:, slot, hc * 128:(hc + 1) * 128],
                                     self.ident[:, :])
                tp_mark[tt] = inc(li, "mt")

        def evac(tt):
            @self.on("scalar")
            def _(s, tt=tt):
                wait(s, "mt", tp_mark[tt])
                s.activation(out=xnA_v[:, :, tt * 128:(tt + 1) * 128],
                             in_=self.pt[:, :].rearrange("p (c q) -> p c q", q=128),
                             func=AF.Identity)
                ev_mark[tt] = inc(s, "ms")

        for tt in range(NTT):
            load(tt)
            stats(tt)
            lnexp(tt)
            if tt >= 1:
                xn(tt - 1)
                transpose(tt - 1)
                evac(tt - 1)
        xn(NTT - 1)
        transpose(NTT - 1)
        evac(NTT - 1)
        self.p1_done = self.CNT["ms"]

    def xpart_max(self, cols_ap, out_col):
        inc, wait = self.inc, self.wait

        @self.on("vector")
        def _(v):
            v.tensor_reduce(out=self.mx1[:, :], in_=cols_ap, axis=AX, op=ALU.max)
            v.drain()
            inc(v, "mv_")

        @self.on("tensor")
        def _(t):
            wait(t, "mv_")
            wait(t, "ms")
            inc(t.transpose(self.px[0:1, 0:128], self.mx1[:, 0:1], self.identf[:, :]),
                "mt")

        @self.on("vector")
        def _(v, out_col=out_col):
            wait(v, "mt")
            v.tensor_reduce(out=self.cc_sb[0:1, out_col:out_col + 1],
                            in_=self.px[0:1, 0:128], axis=AX, op=ALU.max)
            v.drain()
            inc(v, "mv_")

    def all_reduce_max_trigger(self):
        inc, dinc, wait = self.inc, self.dinc, self.wait

        @self.on("scalar")
        def _(s):
            wait(s, "mv_")
            dinc(s.dma_start(out=self.cc_in[:, :], in_=self.cc_sb[:, :]), "dci")

        @self.on("gpsimd")
        def _(g):
            wait(g, "dci")
            inc(g.collective_compute("AllReduce", ALU.max,
                                     replica_groups=[list(range(8))],
                                     ins=[self.cc_in.opt()], outs=[self.cc_out.opt()]),
                "mc")

    def all_reduce_max_fetch(self):
        inc, dinc, wait = self.inc, self.dinc, self.wait

        @self.on("gpsimd")
        def _(g):
            wait(g, "mc")
            dinc(g.dma_start(out=self.ccr_sb[:, :], in_=self.cc_out[:, :]), "dcc")

    def all_reduce_max(self):
        self.all_reduce_max_trigger()
        self.all_reduce_max_fetch()

    def bcast_scale(self, src_col, s_col, inv_col, raw_col=None):
        inc, wait = self.inc, self.wait

        @self.on("tensor")
        def _(t):
            wait(t, "dcc")
            wait(t, "mv_")
            wait(t, "ms")
            inc(t.matmul(self.px[0:128, 0:1], self.ones_f[0:1, 0:128],
                         self.ccr_sb[0:1, src_col:src_col + 1], start=True, stop=True),
                "mt")

        @self.on("vector")
        def _(v):
            wait(v, "mt")
            if raw_col is not None:
                v.tensor_copy(out=self.scales[:, raw_col:raw_col + 1], in_=self.px[0:128, 0:1])
            v.tensor_scalar(out=self.scales[:, s_col:s_col + 1], in0=self.px[0:128, 0:1],
                            scalar1=1.0 / QMAX, scalar2=1e-12, op0=ALU.mult, op1=ALU.max)
            v.drain()
            v.reciprocal(out=self.scales[:, inv_col:inv_col + 1],
                         in_=self.scales[:, s_col:s_col + 1])
            v.drain()
            inc(v, "mv_")

    def r1_start(self):
        self.xpart_max(self.maxc[:, 0:NTT], 0)
        self.all_reduce_max_trigger()

    def r1_finish(self):
        self.all_reduce_max_fetch()
        self.bcast_scale(0, 0, 1)

        @self.on("vector")
        def _(v):
            self.wait(v, "mv_")
            v.tensor_scalar(out=self.scales[:, 17:18], in0=self.scales[:, 0:1],
                            scalar1=-CQ, scalar2=None, op0=ALU.mult)
            v.drain()
            self.inc(v, "mv_")
        self.r1_scales = self.CNT["mv_"]

    def p2(self):
        inc, dinc, wait = self.inc, self.dinc, self.wait
        mt_of = {}
        ev_of = {}
        self.last_e = {}
        last_e = self.last_e
        k_sg = {}
        kv_last = {}
        spillq = []
        self._spq = spillq
        kst = self.wkB[:, 0:2048].bitcast(BF16)

        uc_order = list(range(8)) + [x for cc in range(8) for x in (8 + cc, 16 + cc)]
        dq_jobs = [(self.wu_d, icq, self.wuT_sp) for icq in range(IC)] + \
                  [(self.wg_d, icq, self.wgT_sp) for icq in range(IC)]
        dq_next = [0]
        gsg = -1
        for it, uc in enumerate(uc_order):
            slot = it % 2
            dws = "dwg0" if slot == 0 else "dwg1"

            @self.on("gpsimd")
            def _(g, uc=uc, slot=slot, dws=dws, it=it):
                if it >= 2:
                    wait(g, "mt", mt_of[(it - 2) * 4 + 3])
                dinc(g.dma_start(out=self.wstream[:, slot, :, :],
                                 in_=self.wrkv_d[:, uc * 128:(uc + 1) * 128]
                                 .rearrange("(hc p) q -> p hc q", p=128)), dws)

            for sg in range(4):
                gsg += 1
                t0 = sg * 1024
                b0 = (gsg % 2) * 2

                @self.on("tensor")
                def _(t, uc=uc, slot=slot, dws=dws, it=it, sg=sg, gsg=gsg, t0=t0, b0=b0):
                    if sg == 0:
                        wait(t, dws)
                        if it == 0:
                            wait(t, "ms", self.p1_done)
                    if gsg >= 2:
                        sname, scnt = ev_of[gsg - 2]
                        wait(t, sname, scnt)
                    for hc in range(HC):
                        for tl in range(2):
                            li = t.matmul(self.bank(b0 + tl), self.wstream[:, slot, hc, :],
                                          self.xnA(hc, t0 + tl * 512, 512),
                                          start=(hc == 0), stop=(hc == HC - 1))
                    mt_of[gsg] = inc(li, "mt")

                if uc < 8:
                    if sg == 0:
                        while spillq:
                            spillq.pop(0)()

                    @self.on("scalar")
                    def _(s, uc=uc, sg=sg, gsg=gsg, t0=t0, b0=b0):
                        wait(s, "mt", mt_of[gsg])
                        esl = gsg % 4
                        if esl in last_e:
                            wait(s, last_e[esl][0], last_e[esl][1])
                        for tl in range(2):
                            s.activation(out=self.estage[:, esl, tl * 512:(tl + 1) * 512],
                                         in_=self.bank(b0 + tl), func=AF.Sigmoid,
                                         bias=self.brkv_t[:, uc:uc + 1])
                        ev_of[gsg] = ("ms", inc(s, "ms"))

                    def sp_r(uc=uc, sg=sg, gsg=gsg):
                        @self.on("sync")
                        def _(s, uc=uc, sg=sg, gsg=gsg):
                            esl = gsg % 4
                            des = DE_SEMS[esl]
                            wait(s, ev_of[gsg][0], ev_of[gsg][1])
                            dinc(s.dma_start(
                                out=self.r_att[sg * 8:(sg + 1) * 8, :, uc, :]
                                .rearrange("tc p n -> p tc n"),
                                in_=self.estage[:, esl, 0:1024]
                                .rearrange("p (tc n) -> p tc n", n=128)), des)
                            last_e[esl] = (des, self.CNT[des])
                    spillq.append(sp_r)
                elif uc < 16:
                    if sg == 0:
                        while spillq:
                            spillq.pop(0)()

                    @self.on("scalar")
                    def _(s, uc=uc, sg=sg, gsg=gsg, t0=t0, b0=b0):
                        wait(s, "mt", mt_of[gsg])
                        cc = uc - 8
                        if sg == 0 and cc >= 1:
                            wait(s, "mv_", kv_last[cc - 1])
                            dkp = "dk0" if (cc - 1) % 2 == 0 else "dk1"
                            wait(s, dkp, self.CNT[dkp])
                        for tl in range(2):
                            s.activation(out=kst[:, t0 + tl * 512:t0 + (tl + 1) * 512],
                                         in_=self.bank(b0 + tl), func=AF.Exp,
                                         bias=self.brkv_t[:, uc:uc + 1])
                        k_sg[(cc, sg)] = ev_of[gsg] = ("ms", inc(s, "ms"))

                    def sp_k(uc=uc, sg=sg, gsg=gsg, t0=t0):
                        @self.on("sync")
                        def _(s, uc=uc, sg=sg, gsg=gsg, t0=t0):
                            cc = uc - 8
                            wait(s, ev_of[gsg][0], ev_of[gsg][1])
                            dkp = "dk0" if cc % 2 == 0 else "dk1"
                            dinc(s.dma_start(out=self.u_sp[8 + cc, :, t0:t0 + 1024],
                                             in_=kst[:, t0:t0 + 1024]), dkp)
                            dinc(s.dma_start(
                                out=self.u_att[sg * 8:(sg + 1) * 8, :, 8 + cc, :]
                                .rearrange("tc p n -> p tc n"),
                                in_=kst[:, t0:t0 + 1024]
                                .rearrange("p (tc n) -> p tc n", n=128)), dkp)
                    spillq.append(sp_k)
                else:
                    @self.on("vector")
                    def _(v, uc=uc, sg=sg, gsg=gsg, t0=t0, b0=b0):
                        cc = uc - 16
                        wait(v, "mt", mt_of[gsg])
                        sname, scnt = k_sg[(cc, sg)]
                        wait(v, sname, scnt)
                        esl = gsg % 4
                        if esl in last_e:
                            wait(v, last_e[esl][0], last_e[esl][1])
                        for tl in range(2):
                            v.scalar_tensor_tensor(
                                out=self.estage[:, esl, tl * 512:(tl + 1) * 512],
                                in0=self.bank(b0 + tl), scalar=self.brkv_t[:, uc:uc + 1],
                                in1=kst[:, t0 + tl * 512:t0 + (tl + 1) * 512],
                                op0=ALU.add, op1=ALU.mult)
                        kv_last[cc] = inc(v, "mv_")
                        ev_of[gsg] = ("mv_", kv_last[cc])

                    def sp_v(uc=uc, sg=sg, gsg=gsg, t0=t0):
                        @self.on("sync")
                        def _(s, uc=uc, sg=sg, gsg=gsg, t0=t0):
                            cc = uc - 16
                            wait(s, "mv_", kv_last[cc])
                            esl = gsg % 4
                            des = DE_SEMS[esl]
                            dinc(s.dma_start(
                                out=self.u_sp[cc, :, t0:t0 + 1024],
                                in_=self.estage[:, esl, 0:1024]), des)
                            dinc(s.dma_start(
                                out=self.u_att[sg * 8:(sg + 1) * 8, :, cc, :]
                                .rearrange("tc p n -> p tc n"),
                                in_=self.estage[:, esl, 0:1024]
                                .rearrange("p (tc n) -> p tc n", n=128)), des)
                            last_e[esl] = (des, self.CNT[des])
                    spillq.append(sp_v)
                if len(spillq) >= 3:
                    spillq.pop(0)()

            want = ((it + 1) * len(dq_jobs)) // len(uc_order)
            while dq_next[0] < want:
                n = dq_next[0]
                srcw, icq, dest = dq_jobs[n]
                self.dq_front(srcw, icq, 1024, self.wf_ug,
                              self.wq_ug2[n % 3], n % 2)
                if n >= 2:
                    picq = dq_jobs[n - 2][1]
                    self.dq_back(self._dq["g"] - 2, picq, 1024, "hbm",
                                 dq_jobs[n - 2][2], self.wq_ug2[(n - 2) % 3])
                dq_next[0] += 1

        while dq_next[0] < len(dq_jobs):
            n = dq_next[0]
            srcw, icq, dest = dq_jobs[n]
            self.dq_front(srcw, icq, 1024, self.wf_ug, self.wq_ug2[n % 3], n % 2)
            if n >= 2:
                picq = dq_jobs[n - 2][1]
                self.dq_back(self._dq["g"] - 2, picq, 1024, "hbm",
                             dq_jobs[n - 2][2], self.wq_ug2[(n - 2) % 3])
            dq_next[0] += 1
        for n in (len(dq_jobs) - 2, len(dq_jobs) - 1):
            self.dq_back(self._dq["g"] - (len(dq_jobs) - 1 - n), dq_jobs[n][1], 1024,
                         "hbm", dq_jobs[n][2], self.wq_ug2[n % 3])
        while spillq:
            spillq.pop(0)()
        self._spq = None
        self.p2_mt_last = self.CNT["mt"]
        self.p2_ev_ms = self.CNT["ms"]
        self.p2_done_marks = (self.CNT["mv_"], tuple(self.CNT[d] for d in DE_SEMS),
                              self.CNT["dk0"], self.CNT["dk1"])

    def dq_init(self):
        self._dq = {"vdone": {}, "tdone": {}, "sdone": {}, "meta": {}, "g": -1}
        wkb = self.wkB[:, :]
        self.wf_ug = wkb[:, 2048:4096].rearrange("p (s n) -> p s n", s=2)
        self.wq_ug2 = [wkb[:, 4096:4608].bitcast(BF16),
                       wkb[:, 4608:5120].bitcast(BF16),
                       wkb[:, 5120:5632].bitcast(BF16)]

    def dq_chunk(self, srcw, icq, I, kind, dest, wf, wq, islot):
        self.dq_front(srcw, icq, I, wf, wq, islot, guard=1)
        self.dq_back(self._dq["g"], icq, I, kind, dest, wq)

    def dq_front(self, srcw, icq, I, wf, wq, islot, guard=3):
        inc, dinc, wait = self.inc, self.dinc, self.wait
        D = self._dq
        D["g"] += 1
        g = D["g"]
        D["meta"][g] = (icq, I, wq)
        vdone, tdone, sdone = D["vdone"], D["tdone"], D["sdone"]
        nblk = I // 32
        nb = I // 128
        slot = g % 2
        dbs = "db0" if slot == 0 else "db1"

        @self.on("sync")
        def _(s, srcw=srcw, icq=icq, slot=slot, islot=islot, dbs=dbs, I=I, g=g, wf=wf):
            if g >= 2:
                wait(s, "mv_", vdone[g - 2])
            dinc(s.dma_start(out=wf[:, islot, 0:I],
                             in_=srcw[icq * 128:(icq + 1) * 128, :]), dbs)

        @self.on("vector")
        def _(v, slot=slot, islot=islot, dbs=dbs, I=I, nblk=nblk, g=g, wf=wf):
            wait(v, dbs)
            v.tensor_reduce(out=self.bmax[:, 0, 0:nblk],
                            in_=wf[:, islot, 0:I].rearrange("p (b k) -> p b k", k=32),
                            axis=AX, op=ALU.max, apply_absolute_value=True)
            v.drain()
            v.tensor_scalar(out=self.bs[:, 0, 0:nblk],
                            in0=self.bmax[:, 0, 0:nblk],
                            scalar1=1.0 / 7.0, scalar2=1e-12,
                            op0=ALU.mult, op1=ALU.max)
            v.drain()
            v.reciprocal(out=self.binv[:, 0, 0:nblk],
                         in_=self.bs[:, 0, 0:nblk])
            v.drain()
            v.tensor_tensor(out=wf[:, islot, 0:I].rearrange("p (b k) -> p b k", k=32),
                            in0=wf[:, islot, 0:I].rearrange("p (b k) -> p b k", k=32),
                            in1=self.binv[:, 0, 0:nblk].broadcast_to((128, nblk, 32)),
                            op=ALU.mult)
            inc(v, "mv_")

        for q0 in range(0, I, 1024):
            @self.on("vector")
            def _(v, slot=slot, islot=islot, q0=q0, nblk=nblk, g=g, wf=wf, wq=wq,
                  guard=guard):
                if g >= guard and q0 == 0:
                    wait(v, "mt", tdone[g - guard])
                fq = self.fdq[:, 0:1024]
                v.tensor_scalar(out=self.iscr[:, 0:1024],
                                in0=wf[:, islot, q0:q0 + 1024],
                                scalar1=8.0, scalar2=None, op0=ALU.add)
                v.drain()
                v.scalar_tensor_tensor(out=fq, in0=self.iscr[:, 0:1024],
                                       scalar=7.5, in1=wf[:, islot, q0:q0 + 1024],
                                       op0=ALU.subtract, op1=ALU.is_le)
                v.drain()
                v.scalar_tensor_tensor(out=fq, in0=self.iscr[:, 0:1024],
                                       scalar=8.0, in1=fq,
                                       op0=ALU.subtract, op1=ALU.add)
                v.drain()
                b0 = q0 // 32
                v.tensor_tensor(out=wq[:, q0:q0 + 1024]
                                .rearrange("p (b k) -> p b k", k=32),
                                in0=fq.rearrange("p (b k) -> p b k", k=32),
                                in1=self.bs[:, 0, b0:b0 + 32]
                                .broadcast_to((128, 32, 32)),
                                op=ALU.mult)
                inc(v, "mv_")
        vdone[g] = self.CNT["mv_"]

    def dq_back(self, g, icq, I, kind, dest, wq):
        inc, dinc, wait = self.inc, self.dinc, self.wait
        D = self._dq
        vdone, tdone, sdone = D["vdone"], D["tdone"], D["sdone"]
        nb = I // 128
        for gq in range(nb // 8):
            @self.on("tensor")
            def _(t, nb=nb, g=g, gq=gq, wq=wq):
                if gq == 0:
                    wait(t, "mv_", vdone[g])
                    if g >= 1:
                        wait(t, "ms", sdone[g - 1])
                else:
                    wait(t, "ms")
                for bq in range(gq * 8, gq * 8 + 8):
                    li = t.transpose(self.pt[:, (bq % 8) * 128:(bq % 8) * 128 + 128],
                                     wq[:, bq * 128:(bq + 1) * 128],
                                     self.ident[:, :])
                inc(li, "mt")

            if kind != "sbuf" and getattr(self, "_spq", None) is not None:
                while self._spq:
                    self._spq.pop(0)()

            @self.on("scalar")
            def _(s, icq=icq, nb=nb, g=g, gq=gq, kind=kind, dest=dest):
                wait(s, "mt")
                if kind == "sbuf":
                    s.activation(
                        out=self.wdT.rearrange("p (b h) -> p b h", h=1024)
                        [:, gq * 8:(gq + 1) * 8, icq * 128:(icq + 1) * 128],
                        in_=self.pt[:, :].rearrange("p (b q) -> p b q", q=128),
                        func=AF.Identity)
                    inc(s, "ms")
                else:
                    wait(s, "dq0", self.CNT["dq0"])
                    wait(s, "dq1", self.CNT["dq1"])
                    for half in range(2):
                        s.activation(out=self.pdqst[:, half, 0:512],
                                     in_=self.pt[:, half * 512:(half + 1) * 512],
                                     func=AF.Identity)
                    inc(s, "ms")

            if kind != "sbuf":
                dqm = self.CNT["ms"]

                def sp_dq(icq=icq, g=g, gq=gq, dest=dest, dqm=dqm):
                    @self.on("sync")
                    def _(s, icq=icq, g=g, gq=gq, dest=dest, dqm=dqm):
                        wait(s, "ms", dqm)
                        for half in range(2):
                            dws = "dq0" if half == 0 else "dq1"
                            dinc(s.dma_start(
                                out=dest[icq, gq * 8 + half * 4:gq * 8 + half * 4 + 4, :, :]
                                .rearrange("h p q -> p h q"),
                                in_=self.pdqst[:, half, 0:512]
                                .rearrange("p (h q) -> p h q", q=128)), dws)
                if getattr(self, "_spq", None) is not None:
                    self._spq.append(sp_dq)
                else:
                    sp_dq()
        tdone[g] = self.CNT["mt"]
        sdone[g] = self.CNT["ms"]

    def pdq_down(self):
        p2_ev_last, p2_de, p2_dk0, p2_dk1 = self.p2_done_marks
        wait = self.wait

        @self.on("sync")
        def _(s):
            wait(s, "mv_", p2_ev_last)
            wait(s, "ms", self.p2_ev_ms)
            wait(s, "dk0", p2_dk0)
            wait(s, "dk1", p2_dk1)
            for dd, vv in zip(DE_SEMS, p2_de):
                wait(s, dd, vv)
            wait(s, "mt", self._dq["tdone"][self._dq["g"]])
        wf_dn = self.wkB[:, 0:6144].rearrange("p (s n) -> p s n", s=2)
        wq_dn = self.wkB[:, 6144:7680].bitcast(BF16)
        for icq in range(HC):
            self.dq_chunk(self.wd_d, icq, 3072, "sbuf", None, wf_dn, wq_dn, icq % 2)
        self.pdq_done = self.CNT["mv_"]
        self.dq_spill = (self.CNT["dq0"], self.CNT["dq1"])

    def p3(self):
        inc, dinc, wait = self.inc, self.dinc, self.wait
        scan_done = {}

        for c in range(16):
            slot = c % 2
            das = "da0" if slot == 0 else "da1"

            @self.on("sync")
            def _(s, c=c, slot=slot, das=das):
                if c == 0:
                    wait(s, "mv_", self.pdq_done)
                    for dd, vv in zip(DE_SEMS, self.p2_done_marks[1]):
                        wait(s, dd, vv)
                    wait(s, "dk0", self.p2_done_marks[2])
                    wait(s, "dk1", self.p2_done_marks[3])
                if c >= 2:
                    wait(s, "mv_", scan_done[c - 2])
                dinc(s.dma_start(out=self.ubuf[:, slot, :], in_=self.u_sp[c, :, :]), das)

            @self.on("vector")
            def _(v, c=c, slot=slot, das=das):
                wait(v, das)
                v.tensor_tensor_scan(out=self.ydisc[:, :],
                                     data0=self.lam_t[:, c % 8:c % 8 + 1]
                                     .broadcast_to((128, T)),
                                     data1=self.ubuf[:, slot, :],
                                     initial=0.0, op0=ALU.mult, op1=ALU.add)
                v.drain()
                v.tensor_copy(out=self.S_loc[:, c:c + 1], in_=self.ydisc[:, T - 1:T])
                v.drain()
                scan_done[c] = inc(v, "mv_")

        @self.on("tensor")
        def _(t):
            wait(t, "mv_")
            inc(t.transpose(self.px[0:16, 0:128], self.S_loc[:, :], self.identf[:, :]),
                "mt")

        @self.on("vector")
        def _(v):
            wait(v, "mt")
            v.tensor_copy(out=self.ag_sb[0:16, :], in_=self.px[0:16, 0:128])
            v.drain()
            inc(v, "mv_")

        @self.on("scalar")
        def _(s):
            wait(s, "mv_")
            dinc(s.dma_start(out=self.ag_in[:, :], in_=self.ag_sb[0:16, :]), "dci")

        @self.on("gpsimd")
        def _(g):
            wait(g, "dci")
            inc(g.collective_compute("AllGather", ALU.bypass,
                                     replica_groups=[list(range(8))],
                                     ins=[self.ag_in.opt()], outs=[self.ag_out.opt()]),
                "mc")
            wait(g, "mc")
            dinc(g.dma_start(out=self.ag_sb[:, :], in_=self.ag_out[:, :]), "dcc")

        @self.on("tensor")
        def _(t):
            wait(t, "dcc")
            inc(t.matmul(self.px[0:128, 0:16], self.ag_sb[:, :], self.sel_sb[:, :],
                         start=True, stop=True), "mt")

        @self.on("vector")
        def _(v):
            wait(v, "mt")
            v.tensor_copy(out=self.S_prev[:, :], in_=self.px[0:128, 0:16])
            v.drain()
            self.sprev_ready = inc(v, "mv_")

    def p4(self):
        inc, wait = self.inc, self.wait
        prev = {}
        for st in range(32):
            @self.on("scalar")
            def _(s, st=st):
                if st == 0:
                    wait(s, "mv_", self.r1_scales)
                    wait(s, "mt", self.p2_mt_last)
                s.activation(out=self.iscr[:, (st % 2) * 1024:(st % 2) * 1024 + 1024],
                             in_=self.arenaA[:, st * 1024:(st + 1) * 1024],
                             func=AF.Identity, scale=self.scales[:, 1:2],
                             bias=self.consts[:, 0:1])
                s.drain()
                s.activation(out=self.arenaA[:, st * 1024:(st + 1) * 1024],
                             in_=self.iscr[:, (st % 2) * 1024:(st % 2) * 1024 + 1024],
                             func=AF.Identity, scale=self.scales[:, 0:1],
                             bias=self.scales[:, 17:18])
                s.drain()
                prev[st] = inc(s, "ms")
        self.p4_done = ("ms", self.CNT["ms"])

    def p5(self):
        inc, dinc, wait = self.inc, self.dinc, self.wait
        mt_of = {}
        mvred = {}
        ev_mm = {}
        spillq = []
        self.p65_init()
        self.ring_init()
        att_next = [0]
        git = -1
        gsg = -1

        for wi, (wT, spill, mcols) in enumerate(
                [(self.wuT_sp, self.up_sp, self.upmaxc),
                 (self.wgT_sp, self.gate_sp, self.gatemaxc)]):
            for ic in range(IC):
                git += 1
                slot = git % 2
                dws = "dw0" if slot == 0 else "dw1"

                @self.on("sync")
                def _(s, ic=ic, slot=slot, dws=dws, git=git, wT=wT):
                    if git < 2:
                        wait(s, "dq0", self.dq_spill[0])
                        wait(s, "dq1", self.dq_spill[1])
                    if git >= 2:
                        wait(s, "mt", mt_of[(git - 2, 1)])
                    dinc(s.dma_start(out=self.wstream[:, slot, :, :],
                                     in_=wT[ic, :, :, :].rearrange("h p q -> p h q")),
                         dws)

                for g2 in range(2):
                    banks = self.ralloc(4)
                    bmarks = [self.rmark[b] for b in banks]
                    t0 = g2 * 2048

                    @self.on("tensor")
                    def _(t, ic=ic, slot=slot, dws=dws, git=git, g2=g2,
                          banks=banks, bmarks=bmarks, t0=t0):
                        if g2 == 0:
                            wait(t, dws)
                            if git == 0:
                                wait(t, "ms")
                                wait(t, "mv_")
                                wait(t, "mt", self.p2_mt_last)
                        for m in bmarks:
                            if m is not None:
                                wait(t, m[0], m[1])
                        for hc in range(HC):
                            for j in range(4):
                                li = t.matmul(self.rbank[banks[j]],
                                              self.wstream[:, slot, hc, :],
                                              self.xnA(hc, t0 + j * 512, 512),
                                              start=(hc == 0), stop=(hc == HC - 1))
                        mt_of[(git, g2)] = inc(li, "mt")

                    for pj in range(2):
                        gsg += 1
                        esl = gsg % 4
                        des = DE_SEMS[esl]
                        b0_, b1_ = banks[2 * pj], banks[2 * pj + 1]
                        tt0 = t0 + pj * 1024

                        @self.on("scalar")
                        def _(s, ic=ic, git=git, g2=g2, gsg=gsg, esl=esl, des=des,
                              b0_=b0_, b1_=b1_, tt0=tt0, spill=spill):
                            wait(s, "mt", mt_of[(git, g2)])
                            le = self.last_e.get(esl)
                            if le is not None:
                                wait(s, le[0], le[1])
                            if gsg >= 4:
                                wait(s, "mv_", mvred[gsg - 4])
                            s.activation(out=self.estage[:, esl, 0:512],
                                         in_=self.rbank[b0_], func=AF.Identity)
                            s.activation(out=self.estage[:, esl, 512:1024],
                                         in_=self.rbank[b1_], func=AF.Identity)
                            m = ("ms", inc(s, "ms"))
                            self.rmark[b0_] = m
                            self.rmark[b1_] = m
                            ev_mm[gsg] = m[1]
                        spillq.append((ic, gsg, esl, des, tt0, spill))
                        if len(spillq) >= 3:
                            sic, sgsg, sesl, sdes, stt0, sspill = spillq.pop(0)

                            @self.on("sync")
                            def _(s, sic=sic, sgsg=sgsg, sesl=sesl, sdes=sdes,
                                  stt0=stt0, sspill=sspill):
                                wait(s, "ms", ev_mm[sgsg])
                                self.last_e[sesl] = (sdes, dinc(s.dma_start(
                                    out=sspill[sic, :, stt0:stt0 + 1024],
                                    in_=self.estage[:, sesl, 0:1024]), sdes))

                        @self.on("vector")
                        def _(v, gsg=gsg, esl=esl, mcols=mcols, git=git, g2=g2,
                              pj=pj, wi=wi):
                            m = self.rmark[0]
                            wait(v, "ms", self.CNT["ms"])
                            lit = (git - wi * IC) * 4 + g2 * 2 + pj
                            v.tensor_reduce(out=mcols[:, lit:lit + 1],
                                            in_=self.estage[:, esl, 0:1024],
                                            axis=AX, op=ALU.max,
                                            apply_absolute_value=True)
                            mvred[gsg] = inc(v, "mv_")

                want = ((git + 1) * NTT) // (2 * IC)
                while att_next[0] < want:
                    self.p65_front(att_next[0])
                    if att_next[0] >= 1:
                        self.p65_back(att_next[0] - 1)
                    att_next[0] += 1

        while spillq:
            sic, sgsg, sesl, sdes, stt0, sspill = spillq.pop(0)

            @self.on("sync")
            def _(s, sic=sic, sgsg=sgsg, sesl=sesl, sdes=sdes, stt0=stt0,
                  sspill=sspill):
                wait(s, "ms", ev_mm[sgsg])
                self.last_e[sesl] = (sdes, dinc(s.dma_start(
                    out=sspill[sic, :, stt0:stt0 + 1024],
                    in_=self.estage[:, sesl, 0:1024]), sdes))

        while att_next[0] < NTT:
            self.p65_front(att_next[0])
            if att_next[0] >= 1:
                self.p65_back(att_next[0] - 1)
            att_next[0] += 1
        self.p65_back(NTT - 1)
        self.p65_finish()
        self.p5_done_ms = self.CNT["ms"]
        self.p5_done_mv = self.CNT["mv_"]
        self.p5_done_de = tuple(self.CNT[d] for d in DE_SEMS)

    def r2(self):
        inc, wait = self.inc, self.wait
        self.xpart_max(self.upmaxc[:, 0:96], 1)
        self.xpart_max(self.gatemaxc[:, 0:96], 2)
        self.all_reduce_max()
        self.bcast_scale(1, 2, 3)
        self.bcast_scale(2, 4, 5)

        @self.on("vector")
        def _(v):
            wait(v, "mv_")
            v.tensor_scalar(out=self.scales[:, 10:11], in0=self.scales[:, 4:5],
                            scalar1=-CQ, scalar2=None, op0=ALU.mult)
            v.tensor_scalar(out=self.scales[:, 11:12], in0=self.scales[:, 4:5],
                            scalar1=1.0 / 65536.0, scalar2=None, op0=ALU.mult)
            v.tensor_scalar(out=self.scales[:, 16:17], in0=self.scales[:, 2:3],
                            scalar1=-CQ, scalar2=None, op0=ALU.mult)
            v.drain()
            self.r2_scales = inc(v, "mv_")

    def p6(self):
        inc, dinc, wait = self.inc, self.dinc, self.wait
        gi_mark = {}
        sig_mark = {}
        g2_mark = {}
        done = {}
        W = 2048
        NQ = IC * 2
        arf = self.arenaA[:, :].bitcast(F32)
        sigb = [arf[:, 0:W], arf[:, W:2 * W]]
        svb = [arf[:, 2 * W:3 * W], arf[:, 3 * W:4 * W]]
        esf = self.estage[:, :, :].rearrange("p a n -> p (a n)")
        est = [esf[:, 0:W], esf[:, W:2 * W]]
        le6 = {0: [self.last_e[0], self.last_e[1]],
               1: [self.last_e[2], self.last_e[3]]}

        def load(ic):
            slot = ic % 2
            das = "da0" if slot == 0 else "da1"

            @self.on("sync")
            def _(s, ic=ic, slot=slot, das=das):
                if ic == 0:
                    wait(s, "mv_", self.p65_done_mv)
                    for dd, vv in zip(DE_SEMS, self.p5_done_de):
                        wait(s, dd, vv)
                if ic >= 2:
                    wait(s, "mv_", gi_mark[2 * ic - 3])
                dinc(s.dma_start(out=self.gbuf[:, slot, :],
                                 in_=self.gate_sp[ic, :, :]), das)

        def gi_stage(Q):
            ic, q = Q // 2, Q % 2
            slot = ic % 2
            das = "da0" if slot == 0 else "da1"
            Qp = Q % 2
            o0 = q * W

            @self.on("vector")
            def _(v, ic=ic, slot=slot, das=das, o0=o0, Q=Q, Qp=Qp):
                if o0 == 0:
                    wait(v, das)
                if Q >= 2:
                    wait(v, "ms", sig_mark[Q - 2])
                v.tensor_scalar(out=self.iscr[:, Qp * W:Qp * W + W],
                                in0=self.gbuf[:, slot, o0:o0 + W],
                                scalar1=self.scales[:, 5:6],
                                scalar2=self.consts[:, 0:1],
                                op0=ALU.mult, op1=ALU.add)
                gi_mark[Q] = inc(v, "mv_")

        def sig_stage(Q):
            Qp = Q % 2

            @self.on("scalar")
            def _(s, Q=Q, Qp=Qp):
                wait(s, "mv_", gi_mark[Q])
                if Q >= 2:
                    wait(s, "mv_", g2_mark[Q - 2])
                s.activation(out=sigb[Qp], in_=self.iscr[:, Qp * W:Qp * W + W],
                             func=AF.Sigmoid, scale=self.scales[:, 4:5],
                             bias=self.scales[:, 10:11])
                s.drain()
                s.activation(out=svb[Qp].bitcast(I32), in_=sigb[Qp],
                             func=AF.Identity, scale=65536.0,
                             bias=self.consts[:, 1:2])
                sig_mark[Q] = inc(s, "ms")

        def tail_stage(Q):
            Qp = Q % 2
            ic = Q // 2

            @self.on("vector")
            def _(v, ic=ic, Q=Q, Qp=Qp):
                wait(v, "ms", sig_mark[Q])
                for le in le6[Qp]:
                    wait(v, le[0], le[1])
                v.tensor_scalar(out=sigb[Qp], in0=svb[Qp].bitcast(I32),
                                scalar1=65535.0, scalar2=self.scales[:, 11:12],
                                op0=ALU.min, op1=ALU.mult)
                v.drain()
                v.scalar_tensor_tensor(out=est[Qp],
                                       in0=self.iscr[:, Qp * W:Qp * W + W],
                                       scalar=CQ, in1=sigb[Qp],
                                       op0=ALU.subtract, op1=ALU.mult)
                v.drain()
                v.tensor_reduce(out=self.g2maxc[:, Q:Q + 1],
                                in_=est[Qp], axis=AX,
                                op=ALU.max, apply_absolute_value=True)
                g2_mark[Q] = done[ic] = inc(v, "mv_")

        def dma_stage(Q):
            ic, q = Q // 2, Q % 2
            Qp = Q % 2
            o0 = q * W

            @self.on("sync")
            def _(s, ic=ic, o0=o0, Q=Q, Qp=Qp):
                wait(s, "mv_", g2_mark[Q])
                des = DE_SEMS[Qp]
                m = (des, dinc(s.dma_start(
                    out=self.g2_sp[ic, :, o0:o0 + W],
                    in_=est[Qp]), des))
                le6[Qp] = [m]
                self.last_e[Qp] = m

        for Q in range(NQ):
            if Q % 2 == 0:
                load(Q // 2)
            gi_stage(Q)
            sig_stage(Q)
            if Q >= 1:
                tail_stage(Q - 1)
            if Q >= 2:
                dma_stage(Q - 2)
        tail_stage(NQ - 1)
        dma_stage(NQ - 2)
        dma_stage(NQ - 1)

        self.p6_done = self.CNT["mv_"]
        self.p6_de = tuple(self.CNT[d] for d in DE_SEMS)

    def r3(self):
        self.xpart_max(self.g2maxc[:, 0:48], 3)
        self.all_reduce_max()
        self.bcast_scale(3, 6, 7)
        self.r3_scales = self.CNT["mv_"]

    def p65_init(self):
        self._a65 = {"blk_done": {}, "ev_prev": {}, "mt65": {}, "dma": {}}

    def p65_front(self, tc):
        inc, dinc, wait = self.inc, self.dinc, self.wait
        blk_done = self._a65["blk_done"]
        mt65 = self._a65["mt65"]
        slot = tc % 2
        das = "da0" if slot == 0 else "da1"
        dbs = "db0" if slot == 0 else "db1"
        t0 = tc * 128

        @self.on("sync")
        def _(s, tc=tc, slot=slot, das=das, dbs=dbs, t0=t0):
            if tc == 0:
                wait(s, "mv_", self.sprev_ready)
                for dd, vv in zip(DE_SEMS, self.p2_done_marks[1]):
                    wait(s, dd, vv)
                wait(s, "dk0", self.p2_done_marks[2])
                wait(s, "dk1", self.p2_done_marks[3])
            if tc >= 2:
                wait(s, "mv_", blk_done[tc - 2])
            dinc(s.dma_start(out=self.att_u[:, slot, :],
                             in_=self.u_att[tc, :, :, :].rearrange("p c n -> p (c n)")),
                 das)
            dinc(s.dma_start(out=self.att_r[:, slot, :],
                             in_=self.r_att[tc, :, :, :].rearrange("p c n -> p (c n)")),
                 dbs)

        @self.on("vector")
        def _(v, tc=tc, slot=slot, das=das, dbs=dbs):
            wait(v, das)
            wait(v, dbs)
            for c in range(16):
                init = self.S_prev[:, c:c + 1] if tc == 0 else self.carries[:, c:c + 1]
                v.tensor_tensor_scan(
                    out=self.att_y[:, c * 128:(c + 1) * 128],
                    data0=self.lam_t[:, c % 8:c % 8 + 1].broadcast_to((128, 128)),
                    data1=self.att_u[:, slot, c * 128:(c + 1) * 128],
                    initial=init, op0=ALU.mult, op1=ALU.add)
            v.drain()
            v.tensor_copy(out=self.carries[:, :],
                          in_=self.att_y[:, :].rearrange("p (c n) -> p c n", n=128)
                          [:, :, 127:128].rearrange("p c n -> p (c n)"))
            v.drain()
            for c in range(8):
                v.scalar_tensor_tensor(
                    out=self.att_n[:, c * 128:(c + 1) * 128],
                    in0=self.att_u[:, slot, c * 128:(c + 1) * 128],
                    scalar=self.gamma_t[:, c:c + 1],
                    in1=self.att_y[:, c * 128:(c + 1) * 128],
                    op0=ALU.mult, op1=ALU.add)
            for c in range(8):
                v.scalar_tensor_tensor(
                    out=self.att_d[:, c * 128:(c + 1) * 128],
                    in0=self.att_u[:, slot, (8 + c) * 128:(9 + c) * 128],
                    scalar=self.gamma_t[:, c:c + 1],
                    in1=self.att_y[:, (8 + c) * 128:(9 + c) * 128],
                    op0=ALU.mult, op1=ALU.add)
            v.drain()
            v.tensor_scalar(out=self.att_d[:, :], in0=self.att_d[:, :], scalar1=1e-6,
                            scalar2=None, op0=ALU.add)
            v.drain()
            v.reciprocal(out=self.att_y[:, 0:1024], in_=self.att_d[:, :])
            v.drain()
            v.tensor_tensor(out=self.att_n[:, :], in0=self.att_n[:, :],
                            in1=self.att_y[:, 0:1024], op=ALU.mult)
            v.drain()
            if tc >= 2:
                wait(v, "mt", mt65[tc - 2])
            v.tensor_tensor(out=self.ratio[:, slot, :, :].rearrange("p c n -> p (c n)"),
                            in0=self.att_n[:, :], in1=self.att_r[:, slot, :],
                            op=ALU.mult)
            blk_done[tc] = inc(v, "mv_")

    def p65_back(self, tc):
        inc, dinc, wait = self.inc, self.dinc, self.wait
        blk_done = self._a65["blk_done"]
        ev_prev = self._a65["ev_prev"]
        mt65 = self._a65["mt65"]
        adma = self._a65["dma"]
        slot = tc % 2

        banks = self.ralloc(2)
        bmarks = [self.rmark[b] for b in banks]

        @self.on("tensor")
        def _(t, tc=tc, banks=banks, bmarks=bmarks):
            wait(t, "mv_", blk_done[tc])
            if tc == 0:
                wait(t, "dg", self.dg_wo)
            for m in bmarks:
                if m is not None:
                    wait(t, m[0], m[1])
            for c in range(8):
                t.matmul(self.rbank[banks[0]], self.ratio[:, slot, c, :],
                         self.wo_sb[:, c * 1024:c * 1024 + 512],
                         start=(c == 0), stop=False)
                t.matmul(self.rbank[banks[1]], self.ratio[:, slot, c, :],
                         self.wo_sb[:, c * 1024 + 512:(c + 1) * 1024],
                         start=(c == 0), stop=False)
            t.matmul(self.rbank[banks[0]], self.ones_b[0:1, 0:128],
                     self.bo_row[0:1, 0:512], start=False, stop=True)
            li = t.matmul(self.rbank[banks[1]], self.ones_b[0:1, 0:128],
                          self.bo_row[0:1, 512:1024], start=False, stop=True)
            mt65[tc] = inc(li, "mt")

        @self.on("scalar")
        def _(s, tc=tc, slot=slot, banks=banks):
            wait(s, "mt", mt65[tc])
            if tc >= 2:
                wait(s, adma[tc - 2][0], adma[tc - 2][1])
            s.activation(out=self.xnstage[:, slot, 0:512],
                         in_=self.rbank[banks[0]], func=AF.Identity)
            s.activation(out=self.xnstage[:, slot, 512:1024],
                         in_=self.rbank[banks[1]], func=AF.Identity)
            m = ("ms", inc(s, "ms"))
            self.rmark[banks[0]] = m
            self.rmark[banks[1]] = m
            ev_prev[tc] = m[1]

        @self.on("sync")
        def _(s, tc=tc, slot=slot):
            wait(s, "ms", ev_prev[tc])
            dys = "dy0" if slot == 0 else "dy1"
            adma[tc] = (dys, dinc(s.dma_start(
                out=self.att_sp[tc, :, :], in_=self.xnstage[:, slot, :]), dys))

    def p65_finish(self):
        self.p65_done_ms = self.CNT["ms"]
        self.p65_done_mv = self.CNT["mv_"]
        self.p65_att_dy = (self.CNT["dy0"], self.CNT["dy1"])

    def p675(self):
        inc, dinc, wait = self.inc, self.dinc, self.wait
        done = {}
        ci_mark = {}
        pv_mark = {}
        W = 2048
        ari = self.arenaA[:, :].bitcast(I32)
        g2i = [ari[:, 0:W], ari[:, 2 * W:3 * W]]
        upi = [ari[:, W:2 * W], ari[:, 3 * W:4 * W]]
        arb = self.arenaA[:, :]
        upq = [arb[:, 16384:16384 + W], arb[:, 16384 + W:16384 + 2 * W]]
        esf = self.estage[:, :, :].rearrange("p a n -> p (a n)")
        est = [esf[:, 0:W], esf[:, W:2 * W]]
        le6 = {0: [self.last_e[0]], 1: [self.last_e[1]]}

        def spill(Q):
            ic, q = Q // 2, Q % 2

            @self.on("sync")
            def _(s, ic=ic, q=q, Q=Q):
                wait(s, "mv_", pv_mark[Q])
                Qp = Q % 2
                des = DE_SEMS[Qp]
                m = (des, dinc(s.dma_start(
                    out=self.o_att[q * 16:(q + 1) * 16, :, ic, :]
                    .rearrange("tc p n -> p tc n"),
                    in_=est[Qp].rearrange("p (tc n) -> p tc n", n=128)), des))
                le6[Qp] = [m]
                self.last_e[Qp] = m

        for ic in range(IC):
            slot = ic % 2
            das = "da0" if slot == 0 else "da1"
            dbs = "db0" if slot == 0 else "db1"

            @self.on("sync")
            def _(s, ic=ic, slot=slot, das=das, dbs=dbs):
                if ic == 0:
                    for dd, vv in zip(DE_SEMS, self.p6_de):
                        wait(s, dd, vv)
                    wait(s, "dy0", self.p65_att_dy[0])
                    wait(s, "dy1", self.p65_att_dy[1])
                if ic >= 2:
                    wait(s, "ms", ci_mark[2 * ic - 3])
                dinc(s.dma_start(out=self.g2in[:, slot, :], in_=self.g2_sp[ic, :, :]), das)
                dinc(s.dma_start(out=self.upin[:, slot, :], in_=self.up_sp[ic, :, :]), dbs)

            for q in range(2):
                Q = ic * 2 + q
                Qp = Q % 2
                o0 = q * W
                if Q >= 2:
                    spill(Q - 2)

                @self.on("scalar")
                def _(s, ic=ic, slot=slot, das=das, dbs=dbs, o0=o0, Q=Q, Qp=Qp):
                    if Q == 0:
                        wait(s, "mv_", self.r3_scales)
                        wait(s, "mv_", self.p6_done)
                    if o0 == 0:
                        wait(s, das)
                        wait(s, dbs)
                    if Q >= 2:
                        wait(s, "mv_", pv_mark[Q - 2])
                    s.activation(out=upi[Qp],
                                 in_=self.upin[:, slot, o0:o0 + W],
                                 func=AF.Identity, scale=self.scales[:, 3:4],
                                 bias=self.consts[:, 0:1])
                    s.activation(out=g2i[Qp],
                                 in_=self.g2in[:, slot, o0:o0 + W],
                                 func=AF.Identity, scale=self.scales[:, 7:8],
                                 bias=self.consts[:, 0:1])
                    if Q % 2 == 0:
                        s.drain()
                        s.activation(out=upq[Qp], in_=upi[Qp],
                                     func=AF.Identity, scale=self.scales[:, 2:3],
                                     bias=self.scales[:, 16:17])
                    ci_mark[Q] = inc(s, "ms")

                @self.on("vector")
                def _(v, ic=ic, o0=o0, Q=Q, Qp=Qp):
                    wait(v, "ms", ci_mark[Q])
                    for le in le6[Qp]:
                        wait(v, le[0], le[1])
                    if Q % 2 == 1:
                        v.tensor_scalar(out=upq[Qp], in0=upi[Qp],
                                        scalar1=CQ, scalar2=self.scales[:, 2:3],
                                        op0=ALU.subtract, op1=ALU.mult)
                        v.drain()
                    v.scalar_tensor_tensor(out=est[Qp], in0=g2i[Qp],
                                           scalar=CQ, in1=upq[Qp],
                                           op0=ALU.subtract, op1=ALU.mult)
                    v.drain()
                    v.tensor_reduce(out=self.pmaxc[:, Q:Q + 1],
                                    in_=est[Qp], axis=AX,
                                    op=ALU.max, apply_absolute_value=True)
                    pv_mark[Q] = done[ic] = inc(v, "mv_")
        spill(IC * 2 - 2)
        spill(IC * 2 - 1)

        self.p675_done = self.CNT["mv_"]
        self.p675_de = tuple(self.CNT[d] for d in DE_SEMS)

    def r4(self):
        inc, wait = self.inc, self.wait
        self.xpart_max(self.pmaxc[:, 0:48], 4)
        self.all_reduce_max()
        self.bcast_scale(4, 13, 14, raw_col=12)

        @self.on("vector")
        def _(v):
            wait(v, "mv_")
            v.tensor_tensor(out=self.scales[:, 15:16], in0=self.scales[:, 6:7],
                            in1=self.scales[:, 12:13], op=ALU.mult)
            v.drain()
            v.tensor_scalar(out=self.scales[:, 8:9], in0=self.scales[:, 15:16],
                            scalar1=1.0 / QMAX, scalar2=1e-12, op0=ALU.mult, op1=ALU.max)
            v.drain()
            v.reciprocal(out=self.scales[:, 15:16], in_=self.scales[:, 8:9])
            v.drain()
            v.tensor_tensor(out=self.scales[:, 9:10], in0=self.scales[:, 6:7],
                            in1=self.scales[:, 15:16], op=ALU.mult)
            v.drain()
            self.r4_scales = inc(v, "mv_")

    def p7(self):
        inc, dinc, wait = self.inc, self.dinc, self.wait
        oq_mark = {}
        mm_mark = {}
        ev_mark = {}
        dma_mark = {}
        att_mark = {}
        pa2 = self.bigps[:, 0:1024]

        def loads(tc):
            slot = tc % 2
            das = "da0" if slot == 0 else "da1"
            dbs = "db0" if slot == 0 else "db1"
            dxs = "dx0" if slot == 0 else "dx1"
            t0 = tc * 128

            @self.on("sync")
            def _(s, tc=tc, slot=slot, das=das, dbs=dbs, dxs=dxs, t0=t0):
                if tc == 0:
                    wait(s, "mv_", self.p675_done)
                    for dd, vv in zip(DE_SEMS, self.p675_de):
                        wait(s, dd, vv)
                    wait(s, "dy0", self.p65_att_dy[0])
                    wait(s, "dy1", self.p65_att_dy[1])
                if tc >= 2:
                    wait(s, "mv_", ev_mark[tc - 2])
                dinc(s.dma_start(out=self.ostage[:, slot, :],
                                 in_=self.o_att[tc, :, :, :]
                                 .rearrange("p c n -> p (c n)")), das)
                dinc(s.dma_start(out=self.xld[:, slot, :], in_=self.x_d[t0:t0 + 128, :]),
                     dxs)
                dinc(s.dma_start(out=self.attnA(tc), in_=self.att_sp[tc, :, :]), dbs)

        def oquant(tc):
            slot = tc % 2
            das = "da0" if slot == 0 else "da1"
            for hf in range(2):
                h0 = hf * 1536

                @self.on("scalar")
                def _(s, tc=tc, slot=slot, das=das, h0=h0, hf=hf):
                    if tc == 0 and hf == 0:
                        wait(s, "mv_", self.r4_scales)
                    if hf == 0:
                        wait(s, das)
                        if tc >= 2:
                            wait(s, "mt", mm_mark[tc - 2])
                    if tc >= 1 or hf >= 1:
                        wait(s, "mv_")
                    s.activation(out=self.iscr[:, 0:1536],
                                 in_=self.ostage[:, slot, h0:h0 + 1536],
                                 func=AF.Identity, scale=self.scales[:, 9:10],
                                 bias=self.consts[:, 0:1])
                    s.drain()
                    inc(s, "ms")

                @self.on("vector")
                def _(v, tc=tc, h0=h0):
                    wait(v, "ms")
                    oq = self.oq if tc % 2 == 0 else self.oq2
                    v.tensor_scalar(out=oq[:, h0:h0 + 1536], in0=self.iscr[:, 0:1536],
                                    scalar1=CQ, scalar2=self.scales[:, 8:9],
                                    op0=ALU.subtract, op1=ALU.mult)
                    oq_mark[tc] = inc(v, "mv_")

        def mms(tc):
            @self.on("tensor")
            def _(t, tc=tc):
                wait(t, "mv_", oq_mark[tc])
                if tc == 0:
                    wait(t, "ms", self.p65_done_ms)
                if tc >= 2:
                    wait(t, "mv_", ev_mark[tc - 2])
                oq = self.oq if tc % 2 == 0 else self.oq2
                ps = self.pa if tc % 2 == 0 else pa2
                for ic in range(IC):
                    t.matmul(ps[:, 0:512], oq[:, ic * 128:(ic + 1) * 128],
                             self.wdT[:, ic * 1024:ic * 1024 + 512],
                             start=(ic == 0), stop=(ic == IC - 1))
                    li = t.matmul(ps[:, 512:1024], oq[:, ic * 128:(ic + 1) * 128],
                                  self.wdT[:, ic * 1024 + 512:(ic + 1) * 1024],
                                  start=(ic == 0), stop=(ic == IC - 1))
                mm_mark[tc] = inc(li, "mt")

        def evac(tc):
            slot = tc % 2
            dxs = "dx0" if slot == 0 else "dx1"
            dbs = "db0" if slot == 0 else "db1"

            @self.on("vector")
            def _(v, tc=tc, slot=slot, dxs=dxs, dbs=dbs):
                wait(v, "mt", mm_mark[tc])
                wait(v, dxs)
                wait(v, dbs)
                if tc >= 2:
                    wait(v, dma_mark[tc - 2][0], dma_mark[tc - 2][1])
                ps = self.pa if tc % 2 == 0 else pa2
                v.scalar_tensor_tensor(out=self.outst[:, slot, :], in0=ps[:, :],
                                       scalar=1.0, in1=self.attnA(tc),
                                       op0=ALU.mult, op1=ALU.add)
                v.drain()
                v.tensor_tensor(out=self.outst[:, slot, :], in0=self.outst[:, slot, :],
                                in1=self.xld[:, slot, :], op=ALU.add)
                ev_mark[tc] = inc(v, "mv_")

        def store(tc):
            slot = tc % 2
            t0 = tc * 128

            @self.on("sync")
            def _(s, tc=tc, slot=slot, t0=t0):
                wait(s, "mv_", ev_mark[tc])
                dys = "dy0" if slot == 0 else "dy1"
                dma_mark[tc] = (dys, dinc(s.dma_start(
                    out=self.y_d[t0:t0 + 128, :], in_=self.outst[:, slot, :]), dys))

        for tc in range(NTT):
            loads(tc)
            oquant(tc)
            mms(tc)
            if tc >= 1:
                evac(tc - 1)
                store(tc - 1)
        evac(NTT - 1)
        store(NTT - 1)

    def fin(self):
        @self.on("sync")
        def _(s):
            self.wait(s, "dy0")
            self.wait(s, "dy1")


def build_nc():
    b = Builder()
    nc = b.build()
    b.ctx.close()
    return nc


_NC_CACHE = None


def kernel(L0_attn_inputs, L0_attn_W_rkv, L0_attn_b_rkv, L0_attn_W_o, L0_attn_b_o,
           L0_attn_nu_log, L0_attn_gamma_log, L0_ffn_w_gate, L0_ffn_w_up, L0_ffn_w_down):
    global _NC_CACHE
    x = np.ascontiguousarray(np.asarray(L0_attn_inputs, dtype=np.float32))
    wrkv = np.ascontiguousarray(np.asarray(L0_attn_W_rkv, dtype=np.float32))
    brkv = np.ascontiguousarray(np.asarray(L0_attn_b_rkv, dtype=np.float32))
    wo = np.ascontiguousarray(np.asarray(L0_attn_W_o, dtype=np.float32))
    bo = np.ascontiguousarray(np.asarray(L0_attn_b_o, dtype=np.float32))
    nulog = np.ascontiguousarray(np.asarray(L0_attn_nu_log, dtype=np.float32))
    glog = np.ascontiguousarray(np.asarray(L0_attn_gamma_log, dtype=np.float32))
    wg = np.ascontiguousarray(np.asarray(L0_ffn_w_gate, dtype=np.float32))
    wu = np.ascontiguousarray(np.asarray(L0_ffn_w_up, dtype=np.float32))
    wd = np.ascontiguousarray(np.asarray(L0_ffn_w_down, dtype=np.float32))

    if _NC_CACHE is None:
        _NC_CACHE = build_nc()
    nc = _NC_CACHE

    in_maps = []
    for c in range(8):
        bb, hh = c // 2, c % 2
        sel = np.zeros((128, 16), np.float32)
        if hh == 1:
            for g in range(16):
                sel[16 * (c - 1) + g, g] = 1.0
        in_maps.append({
            "x": np.ascontiguousarray(x[bb, hh * T:(hh + 1) * T, :]),
            "wrkv": wrkv, "brkv": brkv, "wo": wo, "bo": bo,
            "nulog": nulog, "glog": glog, "wg": wg, "wu": wu, "wd": wd,
            "sel": sel,
        })
    res = run_bass_kernel_spmd(nc, in_maps, core_ids=list(range(8)))
    out = np.empty((B, L, H), dtype=np.float32)
    for c in range(8):
        bb, hh = c // 2, c % 2
        out[bb, hh * T:(hh + 1) * T, :] = res.results[c]["y"]
    return out
